# version 07
# baseline (speedup 1.0000x reference)
import numpy as np
import ml_dtypes
import concourse.bass as bass
import concourse.mybir as mybir
from concourse.bass_utils import run_bass_kernel_spmd
from contextlib import ExitStack

F32 = mybir.dt.float32
BF16 = mybir.dt.bfloat16
FP16 = mybir.dt.float16
AF = mybir.ActivationFunctionType
ALU = mybir.AluOpType

NCORES = 8
B = 4
T = 4096
TL = 2048
D = 1024
DC = D // 128
PAIRS = [[0, 1], [2, 3], [4, 5], [6, 7]]
NEG = -1e38
F0_EXPERTS = list(range(9))
F0_DEBUG = False
F1_EXPERTS = list(range(16)) + [99]

ENGS = ("tensor", "vector", "scalar", "gpsimd", "sync")


class Sem:
    __slots__ = ("h", "n", "name")

    def __init__(self, h, name):
        self.h = h
        self.n = 0
        self.name = name

    def at(self):
        return (self, self.n)


class Prog:
    def __init__(self, nc, stack):
        self.nc = nc
        self.stack = stack
        self.ops = {e: [] for e in ENGS}
        self._waited = {}

    def sem(self, name):
        h = self.stack.enter_context(self.nc.semaphore(name))
        return Sem(h, name)

    def add(self, eng, fn, waits=(), inc=None, k=1):
        w = []
        for s, v in waits:
            if v <= 0:
                continue
            key = (eng, s.name)
            if self._waited.get(key, 0) >= v:
                continue
            self._waited[key] = v
            w.append((s.h, v))
        self.ops[eng].append((fn, w, inc.h if inc is not None else None, k))
        if inc is not None:
            inc.n += k

    def emit(self, block):
        for engname in ENGS:
            ops = self.ops[engname]

            def mk(myops):
                def f(e):
                    for fn, waits, inch, k in myops:
                        for h, v in waits:
                            e.wait_ge(h, v)
                        if fn is None:
                            if inch is not None:
                                e.nop().then_inc(inch, k)
                            continue
                        inst = fn(e)
                        if inch is not None:
                            inst.then_inc(inch, k)
                return f

            getattr(block, engname)(mk(ops))


class PsumPool:

    def __init__(self, banks):
        self.banks = banks
        self.free_at = [None] * len(banks)
        self.held = [False] * len(banks)
        self.idx = 0

    def acquire(self):
        for _ in range(len(self.banks)):
            i = self.idx % len(self.banks)
            self.idx += 1
            if not self.held[i]:
                self.held[i] = True
                w = self.free_at[i]
                self.free_at[i] = None
                return self.banks[i], ([w] if w else []), i
        raise RuntimeError("all psum banks held")

    def release(self, i, sem):
        self.held[i] = False
        self.free_at[i] = (sem, sem.n)


def ceil_div(a, b):
    return (a + b - 1) // b


class Ctx:
    def __init__(self, nc, prog, stack):
        self.nc = nc
        self.p = prog
        self.stack = stack
        self.t = {}
        self.dram = {}
        self.mm = prog.sem("mm")
        self.ev = prog.sem("ev")
        self.es = prog.sem("es")
        self.dw = prog.sem("dw")
        self.dx = prog.sem("dx")
        self.gp = prog.sem("gp")
        self.cc = prog.sem("cc")
        self.pool = None
        self.tag = ""
        self.xsrc = "x"
        self.xdst = "xo"

    def sb(self, name, shape, dtype):
        t = self.stack.enter_context(
            self.nc.sbuf_tensor(self.tag + name, shape, dtype))
        self.t[name] = t
        return t

    def ndt(self, name, shape, dtype):
        return self.nc.dram_tensor(self.tag + name, list(shape), dtype).ap()

    def nsb(self, name, shape, dtype, side=None):
        return self.nc.sbuf_tensor(self.tag + name, shape, dtype, side=side)

    def mkpool(self, n=6):
        banks = [self.stack.enter_context(
            self.nc.psum_tensor(f"pb{i}", [128, 512], F32)) for i in range(n)]
        self.pool = PsumPool(banks)
        return banks


def emit_weight_load(ctx, dst, src_ap, eng="gpsimd", waits=()):
    s = ctx.dw if eng == "gpsimd" else ctx.dx
    ctx.p.add(eng, lambda e, d=dst, a=src_ap: e.dma_start(out=d, in_=a),
              waits=waits, inc=s, k=16)
    return s.at()


def chunked(src_ap, p=128):
    rows, O = src_ap.shape
    C = rows // p
    return bass.AP(src_ap.tensor, 0, [[O, p], [p * O, C], [1, O]])


def emit_matmul_chain(ctx, psum_ap, pairs, waits=()):
    p = ctx.p
    n = len(pairs)
    for i, (lh, rh) in enumerate(pairs):
        w = waits if i == 0 else ()
        p.add("tensor",
              lambda e, o=psum_ap, a=lh, b=rh, st=(i == 0), sp=(i == n - 1):
              e.matmul(o, a, b, start=st, stop=sp),
              waits=w, inc=(ctx.mm if i == n - 1 else None))
    return ctx.mm.at()



def emit_ln(ctx, x, xln, TT, consts_w, tag, gain_col=None, bias_col=None,
            x_wait=None):
    p = ctx.p
    ones = ctx.t["onesbf"]
    sq = ctx.t["lnsq"]
    mu = ctx.t["lnmu"]
    rs = ctx.t["lnrs"]
    epsc = ctx.t["epscol"]
    waits = ([x_wait] if x_wait else []) + [consts_w]
    for k in range(DC):
        p.add("vector", lambda e, k=k: e.tensor_copy(
            out=xln[:, k * TT:(k + 1) * TT], in_=x[:, k * TT:(k + 1) * TT]),
            waits=waits if k == 0 else (), inc=ctx.ev)
    cast_done = ctx.ev.at()
    prev_mm = None
    for it in range(ceil_div(TT, 512)):
        t0 = it * 512
        tw = min(512, TT - t0)
        b1, w1, i1 = ctx.pool.acquire()
        b2, w2, i2 = ctx.pool.acquire()
        for k in range(DC):
            sqw = [cast_done]
            if k == 0 and prev_mm:
                sqw.append(prev_mm)
            p.add("scalar", lambda e, k=k, t0=t0, tw=tw: e.activation(
                out=sq[:, k * 512:k * 512 + tw],
                in_=xln[:, k * TT + t0:k * TT + t0 + tw], func=AF.Square),
                waits=sqw, inc=ctx.es)
            sq_ok = ctx.es.at()
            p.add("tensor", lambda e, k=k, t0=t0, tw=tw, b=b1, st=(k == 0), sp=(k == DC - 1):
                  e.matmul(b[:, 0:tw], ones[:], xln[:, k * TT + t0:k * TT + t0 + tw],
                           start=st, stop=sp),
                  waits=([cast_done] + w1 if k == 0 else []),
                  inc=(ctx.mm if k == DC - 1 else None))
            p.add("tensor", lambda e, k=k, tw=tw, b=b2, st=(k == 0), sp=(k == DC - 1):
                  e.matmul(b[:, 0:tw], ones[:], sq[:, k * 512:k * 512 + tw],
                           start=st, stop=sp),
                  waits=([sq_ok] + w2 if k == 0 else [sq_ok]),
                  inc=(ctx.mm if k == DC - 1 else None))
        mm_done = ctx.mm.at()
        prev_mm = mm_done
        p.add("vector", lambda e, t0=t0, tw=tw, b=b1: e.tensor_scalar(
            out=mu[:, t0:t0 + tw], in0=b[:, 0:tw], scalar1=1.0 / D, scalar2=None,
            op0=ALU.mult), waits=[mm_done], inc=ctx.ev)
        ctx.pool.release(i1, ctx.ev)
        p.add("scalar", lambda e, t0=t0, tw=tw: e.activation(
            out=rs[:, t0:t0 + tw], in_=mu[:, t0:t0 + tw], func=AF.Square),
            waits=[ctx.ev.at()], inc=ctx.es)
        p.add("vector", lambda e, t0=t0, tw=tw, b=b2: e.scalar_tensor_tensor(
            out=rs[:, t0:t0 + tw], in0=b[:, 0:tw], scalar=1.0 / D,
            in1=rs[:, t0:t0 + tw], op0=ALU.mult, op1=ALU.subtract),
            waits=[ctx.es.at()], inc=ctx.ev)
        ctx.pool.release(i2, ctx.ev)
        p.add("scalar", lambda e, t0=t0, tw=tw: e.activation(
            out=rs[:, t0:t0 + tw], in_=rs[:, t0:t0 + tw], func=AF.Ln,
            bias=epsc[:, 0:1]),
            waits=[ctx.ev.at()], inc=ctx.es)
        p.add("scalar", lambda e, t0=t0, tw=tw: e.activation(
            out=rs[:, t0:t0 + tw], in_=rs[:, t0:t0 + tw], func=AF.Exp,
            scale=-0.5), inc=ctx.es)
    stats_done = ctx.es.at()
    for k in range(DC):
        p.add("vector", lambda e, k=k: e.tensor_sub(
            out=xln[:, k * TT:(k + 1) * TT], in0=x[:, k * TT:(k + 1) * TT],
            in1=mu[:, 0:TT]),
            waits=[stats_done] if k == 0 else (), inc=ctx.ev)
        p.add("vector", lambda e, k=k: e.tensor_mul(
            out=xln[:, k * TT:(k + 1) * TT], in0=xln[:, k * TT:(k + 1) * TT],
            in1=rs[:, 0:TT]), inc=ctx.ev)
        if gain_col is not None:
            p.add("vector", lambda e, k=k: e.tensor_scalar(
                out=xln[:, k * TT:(k + 1) * TT], in0=xln[:, k * TT:(k + 1) * TT],
                scalar1=gain_col[:, k:k + 1], scalar2=bias_col[:, k:k + 1],
                op0=ALU.mult, op1=ALU.add), inc=ctx.ev)
    return ctx.ev.at()



def emit_proj_fm(ctx, w_sb, w_wait, xln, xln_wait, TT, outdim, evac,
                 outw_cols=None, nmax=512, kchunks=DC):
    oc_n = outdim // 128
    outw = outw_cols if outw_cols is not None else outdim
    for oc in range(oc_n):
        for t0 in range(0, TT, nmax):
            tw = min(nmax, TT - t0)
            bank, bw, bi = ctx.pool.acquire()
            pairs = []
            for k in range(kchunks):
                lh = w_sb[:, k * outw + oc * 128: k * outw + oc * 128 + 128]
                rh = xln[:, k * TT + t0: k * TT + t0 + tw]
                pairs.append((lh, rh))
            mmw = emit_matmul_chain(ctx, bank[:, 0:tw], pairs,
                                    waits=[w_wait, xln_wait] + bw)
            sem = evac(bank, oc, t0, tw, [mmw])
            ctx.pool.release(bi, sem)


def emit_proj_tm(ctx, w_sb, w_wait, xln, xln_wait, TT, outdim, evac,
                 outw_cols=None, nmax=512):
    outw = outw_cols if outw_cols is not None else outdim
    for tc in range(TT // 128):
        for o0 in range(0, outdim, nmax):
            ow = min(nmax, outdim - o0)
            bank, bw, bi = ctx.pool.acquire()
            pairs = []
            for k in range(DC):
                lh = xln[:, k * TT + tc * 128: k * TT + tc * 128 + 128]
                rh = w_sb[:, k * outw + o0: k * outw + o0 + ow]
                pairs.append((lh, rh))
            mmw = emit_matmul_chain(ctx, bank[:, 0:ow], pairs,
                                    waits=[w_wait, xln_wait] + bw)
            sem = evac(bank, tc, o0, ow, [mmw])
            ctx.pool.release(bi, sem)



def _T(a):
    return np.ascontiguousarray(np.asarray(a).T)


def _colmaj(v):
    v = np.asarray(v).reshape(-1)
    return np.ascontiguousarray(v.reshape(-1, 128).T)


def _rep(v, n=128):
    v = np.asarray(v).reshape(-1)
    return np.ascontiguousarray(np.broadcast_to(v[None, :], (n, v.shape[0])))


def _fm(x2d):
    tt = x2d.shape[0]
    return np.ascontiguousarray(
        x2d.T.reshape(DC, 128, tt).transpose(1, 0, 2).reshape(128, DC * tt))


def _unfm(dev):
    tt = dev.shape[1] // DC
    return np.ascontiguousarray(
        dev.reshape(128, DC, tt).transpose(1, 0, 2).reshape(D, tt).T)


def prep_weights(inp):
    w = {}
    wq = np.asarray(inp["L0_attn_a_Wq"])
    wk = np.asarray(inp["L0_attn_a_Wk"])
    wqp = np.zeros((D, 384), np.float32)
    wkp = np.zeros((D, 384), np.float32)
    bqp = np.zeros((384,), np.float32)
    bkp = np.zeros((384,), np.float32)
    for h in range(8):
        off = 128 * (h // 3) + 32 * (h % 3)
        wqp[:, off:off + 16] = wq[16 * h:16 * h + 16, :].T
        wkp[:, off:off + 16] = wk[16 * h:16 * h + 16, :].T
        bqp[off:off + 16] = np.asarray(inp["L0_attn_a_bq"])[16 * h:16 * h + 16]
        bkp[off:off + 16] = np.asarray(inp["L0_attn_a_bk"])[16 * h:16 * h + 16]
    w["a0_wq"] = wqp
    w["a0_wk"] = wkp
    w["a0_bq"] = _colmaj(bqp)
    w["a0_bk"] = _colmaj(bkp)
    w["a0_wv"] = _T(inp["L0_attn_a_Wv"])
    w["a0_bv"] = _rep(inp["L0_attn_a_bv"])
    w["a0_gamma"] = np.ascontiguousarray(
        np.broadcast_to(np.asarray(inp["L0_attn_a_gamma"]).reshape(1, 1), (128, 1)))
    w["f0_gw"] = np.asarray(inp["L0_ffn_gate_w"])
    w["f0_wg"] = np.asarray(inp["L0_ffn_w_gate"])
    w["f0_wu"] = np.asarray(inp["L0_ffn_w_up"])
    w["f0_wd"] = np.asarray(inp["L0_ffn_w_down"])
    w["f0_sg"] = np.asarray(inp["L0_ffn_ws_gate"])
    w["f0_su"] = np.asarray(inp["L0_ffn_ws_up"])
    w["f0_sd"] = np.asarray(inp["L0_ffn_ws_down"])
    w["b0_wq"] = _T(inp["L0_attn_b_wq"])
    w["b0_wk"] = _T(inp["L0_attn_b_wk"])
    w["b0_wv"] = _T(inp["L0_attn_b_wv"])
    w["b0_wo"] = _T(inp["L0_attn_b_wo"])
    w["b0_bq"] = _colmaj(inp["L0_attn_b_bq"])
    w["b0_bk"] = _colmaj(inp["L0_attn_b_bk"])
    w["b0_bv"] = _rep(inp["L0_attn_b_bv"])
    w["b0_bo"] = _colmaj(inp["L0_attn_b_bo"])
    w["w1_decay"] = _colmaj(inp["L1_attn_a_time_decay"])
    w["w1_first"] = _colmaj(inp["L1_attn_a_time_first"])
    w["w1_mk"] = _colmaj(np.asarray(inp["L1_attn_a_time_mix_k"]).reshape(-1))
    w["w1_mv"] = _colmaj(np.asarray(inp["L1_attn_a_time_mix_v"]).reshape(-1))
    w["w1_mr"] = _colmaj(np.asarray(inp["L1_attn_a_time_mix_r"]).reshape(-1))
    w["w1_bmask"] = [np.full((128, 8), float(c % 2), np.float32)
                     for c in range(NCORES)]
    w["w1_wk"] = _T(inp["L1_attn_a_W_k"])
    w["w1_wv"] = _T(inp["L1_attn_a_W_v"])
    w["w1_wr"] = _T(inp["L1_attn_a_W_r"])
    w["w1_wo"] = _T(inp["L1_attn_a_W_o"])
    w["f1_gw"] = _T(inp["L1_ffn_gate_w"])
    w["f1_eb"] = _rep(inp["L1_ffn_expert_bias"])
    w["f1_w1"] = np.ascontiguousarray(np.asarray(inp["L1_ffn_w1"]).transpose(0, 2, 1))
    w["f1_w3"] = np.ascontiguousarray(np.asarray(inp["L1_ffn_w3"]).transpose(0, 2, 1))
    w["f1_w2"] = np.ascontiguousarray(np.asarray(inp["L1_ffn_w2"]).transpose(0, 2, 1))
    w["f1_b1"] = np.ascontiguousarray(
        np.asarray(inp["L1_ffn_b1"]).reshape(16, 4, 128).transpose(0, 2, 1))
    w["f1_b3"] = np.ascontiguousarray(
        np.asarray(inp["L1_ffn_b3"]).reshape(16, 4, 128).transpose(0, 2, 1))
    w["f1_b2"] = np.asarray(inp["L1_ffn_b2"])
    w["f1_sw1"] = _T(inp["L1_ffn_sw1"])
    w["f1_sw3"] = _T(inp["L1_ffn_sw3"])
    w["f1_sw2"] = _T(inp["L1_ffn_sw2"])
    w["f1_sb1"] = _colmaj(inp["L1_ffn_sb1"])
    w["f1_sb3"] = _colmaj(inp["L1_ffn_sb3"])
    w["f1_sb2"] = _colmaj(inp["L1_ffn_sb2"])
    w["b1_nw"] = _colmaj(inp["L1_attn_b_norm_w"])
    w["b1_nb"] = _colmaj(inp["L1_attn_b_norm_b"])
    w["b1_qkv"] = _T(inp["L1_attn_b_qkv_w"])
    qb = np.asarray(inp["L1_attn_b_qkv_b"])
    w["b1_bq"] = _rep(qb[0:D])
    w["b1_bk"] = _rep(qb[D:2 * D])
    w["b1_bv"] = _colmaj(qb[2 * D:3 * D])
    w["b1_proj"] = _T(inp["L1_attn_b_proj_w"])
    w["b1_pb"] = _colmaj(inp["L1_attn_b_proj_b"])
    return w


WEIGHT_SHAPES = {
    "a0_wq": (D, 384), "a0_bq": (128, 3), "a0_wk": (D, 384), "a0_bk": (128, 3),
    "a0_wv": (D, D), "a0_bv": (128, D), "a0_gamma": (128, 1),
    "f0_gw": (D, 8), "f0_wg": (8, D, 2048), "f0_wu": (8, D, 2048),
    "f0_wd": (8, 2048, D), "f0_sg": (D, 2048), "f0_su": (D, 2048),
    "f0_sd": (2048, D),
    "b0_wq": (D, D), "b0_wk": (D, D), "b0_wv": (D, D), "b0_wo": (D, D),
    "b0_bq": (128, 8), "b0_bk": (128, 8), "b0_bv": (128, D), "b0_bo": (128, 8),
    "w1_decay": (128, 8), "w1_first": (128, 8), "w1_mk": (128, 8),
    "w1_bmask": (128, 8),
    "w1_mv": (128, 8), "w1_mr": (128, 8),
    "w1_wk": (D, D), "w1_wv": (D, D), "w1_wr": (D, D), "w1_wo": (D, D),
    "f1_gw": (D, 16), "f1_eb": (128, 16),
    "f1_w1": (16, D, 512), "f1_w3": (16, D, 512), "f1_w2": (16, 512, D),
    "f1_b1": (16, 128, 4), "f1_b3": (16, 128, 4), "f1_b2": (16, D),
    "f1_sw1": (D, D), "f1_sw3": (D, D), "f1_sw2": (D, D),
    "f1_sb1": (128, 8), "f1_sb3": (128, 8), "f1_sb2": (128, 8),
    "b1_nw": (128, 8), "b1_nb": (128, 8), "b1_qkv": (D, 3 * D),
    "b1_bq": (128, D), "b1_bk": (128, D), "b1_bv": (128, 8),
    "b1_proj": (D, D), "b1_pb": (128, 8),
}


def setup_common(nc, stack, wnames):
    prog = Prog(nc, stack)
    ctx = Ctx(nc, prog, stack)
    ctx.dram["x"] = nc.dram_tensor("x", [128, DC * TL], F32, kind="ExternalInput").ap()
    ctx.dram["xo"] = nc.dram_tensor("xo", [128, DC * TL], F32, kind="ExternalOutput").ap()
    for name in wnames:
        ctx.dram[name] = nc.dram_tensor(
            name, list(WEIGHT_SHAPES[name]), F32, kind="ExternalInput").ap()
    if F0_DEBUG:
        ctx.dram["dbg"] = nc.dram_tensor("dbg", [128, 16 * 1024], BF16,
                                         kind="ExternalOutput").ap()
        ctx.dram["dbgy"] = nc.dram_tensor("dbgy", [128, DC * 1024], F32,
                                          kind="ExternalOutput").ap()
        ctx.dram["dbgg"] = nc.dram_tensor("dbgg", [128, (TL // 128) * 8], BF16,
                                          kind="ExternalOutput").ap()
        ctx.dram["dbgl"] = nc.dram_tensor("dbgl", [128, 32], F32,
                                          kind="ExternalOutput").ap()
    return ctx


def emit_load_x(ctx):
    XSRC = ctx.dram[ctx.xsrc]
    XDST = ctx.dram[ctx.xdst]
    x = ctx.sb("x_sb", [128, DC * TL], F32)
    ctx.p.add("sync", lambda e: e.dma_start(out=x[:], in_=XSRC),
              inc=ctx.dx, k=16)
    return x, ctx.dx.at()


def emit_store_x(ctx, x, waits):
    XSRC = ctx.dram[ctx.xsrc]
    XDST = ctx.dram[ctx.xdst]
    ctx.p.add("sync", lambda e: e.dma_start(out=XDST, in_=x[:]),
              waits=waits, inc=ctx.dx, k=16)
    ctx.p.add("sync", None, waits=[ctx.dx.at()])


def emit_consts(ctx, TT=TL, sdt=BF16):
    ones = ctx.sb("onesbf", [128, 128], sdt)
    ctx.p.add("vector", lambda e: e.memset(ones[:], 1.0), inc=ctx.ev)
    eps = ctx.sb("epscol", [128, 1], F32)
    ctx.p.add("vector", lambda e: e.memset(eps[:], 1e-5), inc=ctx.ev)
    ctx.sb("lnsq", [128, DC * 512], sdt)
    ctx.sb("lnmu", [128, TT], F32)
    ctx.sb("lnrs", [128, TT], F32)
    return ones, ctx.ev.at()



def emit_l0_attn_a(ctx, x, xw):
    XSRC = ctx.dram[ctx.xsrc]
    XDST = ctx.dram[ctx.xdst]
    p = ctx.p
    nc = ctx.nc
    ones, cw = emit_consts(ctx)
    xln = ctx.sb("xln", [128, DC * TL], BF16)
    ln_w = emit_ln(ctx, x, xln, TL, cw, "ln", x_wait=xw)

    k_in = ctx.ndt("k_in", [384, TL], BF16)
    v_in = ctx.ndt("v_in", [TL, D], BF16)
    k_out = ctx.ndt("k_out", [768, TL], BF16)
    v_out = ctx.ndt("v_out", [2 * TL, D], BF16)

    wq_sb = ctx.sb("wq_sb", [128, DC * 384], BF16)
    wk_sb = ctx.sb("wk_sb", [128, DC * 384], BF16)
    wv_sb = ctx.sb("wv_sb", [128, DC * D], BF16)
    bq_sb = ctx.sb("bq_sb", [128, 3], F32)
    bk_sb = ctx.sb("bk_sb", [128, 3], F32)
    bv_sb = ctx.sb("bv_sb", [128, D], BF16)
    gam_sb = ctx.sb("gam_sb", [128, 1], F32)

    w_q = emit_weight_load(ctx, wq_sb[:], chunked(ctx.dram["a0_wq"]))
    w_k = emit_weight_load(ctx, wk_sb[:], chunked(ctx.dram["a0_wk"]))
    w_v = emit_weight_load(ctx, wv_sb[:], chunked(ctx.dram["a0_wv"]))
    w_bq = emit_weight_load(ctx, bq_sb[:], ctx.dram["a0_bq"])
    w_bk = emit_weight_load(ctx, bk_sb[:], ctx.dram["a0_bk"])
    w_bv = emit_weight_load(ctx, bv_sb[:], ctx.dram["a0_bv"])
    w_g = emit_weight_load(ctx, gam_sb[:], ctx.dram["a0_gamma"])

    q_sb = ctx.sb("q_sb", [128, 3 * TL], BF16)
    k_sb = ctx.sb("k_sb", [128, 3 * TL], BF16)
    bq4 = ctx.sb("bq4", [128, 3], F32)
    p.add("vector", lambda e: e.tensor_scalar(
        out=bq4[:], in0=bq_sb[:], scalar1=0.25, scalar2=None, op0=ALU.mult),
        waits=[w_bq], inc=ctx.ev)
    bq4_w = ctx.ev.at()

    def evac_q(bank, oc, t0, tw, waits):
        p.add("scalar", lambda e, bk=bank, oc=oc, t0=t0, tw=tw: e.activation(
            out=q_sb[:, oc * TL + t0: oc * TL + t0 + tw], in_=bk[:, 0:tw],
            func=AF.Identity, bias=bq4[:, oc:oc + 1], scale=0.25),
            waits=waits + [bq4_w], inc=ctx.es)
        return ctx.es

    def evac_k(bank, oc, t0, tw, waits):
        p.add("scalar", lambda e, bk=bank, oc=oc, t0=t0, tw=tw: e.activation(
            out=k_sb[:, oc * TL + t0: oc * TL + t0 + tw], in_=bk[:, 0:tw],
            func=AF.Identity, bias=bk_sb[:, oc:oc + 1]),
            waits=waits + [w_bk], inc=ctx.es)
        return ctx.es

    emit_proj_fm(ctx, wq_sb, w_q, xln, ln_w, TL, 384, evac_q, outw_cols=384)
    emit_proj_fm(ctx, wk_sb, w_k, xln, ln_w, TL, 384, evac_k, outw_cols=384)

    vstage = ctx.sb("vstage", [128, 4 * 512], BF16)
    vslot_free = [None] * 4
    vslot_i = [0]

    def evac_v(bank, tc, o0, ow, waits):
        sl = vslot_i[0] % 4
        vslot_i[0] += 1
        w2 = waits + [w_bv] + ([vslot_free[sl]] if vslot_free[sl] else [])
        p.add("vector", lambda e, bk=bank, sl=sl, ow=ow: e.tensor_add(
            out=vstage[:, sl * 512: sl * 512 + ow], in0=bk[:, 0:ow],
            in1=bv_sb[:, o0:o0 + ow]),
            waits=w2, inc=ctx.ev)
        evw = ctx.ev.at()
        p.add("sync", lambda e, sl=sl, tc=tc, o0=o0, ow=ow: e.dma_start(
            out=v_in[tc * 128:(tc + 1) * 128, o0:o0 + ow],
            in_=vstage[:, sl * 512: sl * 512 + ow]),
            waits=[evw], inc=ctx.dx, k=16)
        vslot_free[sl] = ctx.dx.at()
        return ctx.ev

    emit_proj_tm(ctx, wv_sb, w_v, xln, ln_w, TL, D, evac_v)
    v_shipped = ctx.dx.at()
    kv_done_s = ctx.es.at()

    p.add("sync", lambda e: e.dma_start(
        out=chunked(k_in), in_=k_sb[:]),
        waits=[kv_done_s], inc=ctx.dx, k=16)
    p.add("gpsimd", lambda e: e.collective_compute(
        "AllGather", ALU.bypass, replica_groups=PAIRS,
        ins=[k_in.opt()], outs=[k_out.opt()]),
        waits=[ctx.dx.at()], inc=ctx.cc)
    p.add("gpsimd", lambda e: e.collective_compute(
        "AllGather", ALU.bypass, replica_groups=PAIRS,
        ins=[v_in.opt()], outs=[v_out.opt()]),
        waits=[v_shipped], inc=ctx.cc)
    cc_w = ctx.cc.at()

    for k in range(DC):
        p.add("vector", lambda e, k=k: e.tensor_add(
            out=x[:, k * TL:(k + 1) * TL], in0=x[:, k * TL:(k + 1) * TL],
            in1=xln[:, k * TL:(k + 1) * TL]),
            waits=[ln_w], inc=ctx.ev)
    xb_w = ctx.ev.at()

    kf2 = ctx.sb("kf2", [128, T], BF16)

    def kf_slice(kc, lo, hi):
        if kc < 2:
            return wv_sb[:, kc * T + lo: kc * T + hi]
        return kf2[:, lo:hi]

    for r in range(2):
        for c2 in range(3):
            dst = kf_slice(c2, r * TL, r * TL + TL)
            p.add("sync", lambda e, r=r, c2=c2, d=dst: e.dma_start(
                out=d, in_=k_out[r * 384 + c2 * 128: r * 384 + (c2 + 1) * 128, :]),
                waits=[cc_w], inc=ctx.dx, k=16)
    kv_full_w = ctx.dx.at()

    P_buf = xln
    sexp = ctx.sb("sexp", [128, 4 * 512], BF16)
    vh_sb = ctx.sb("vh_sb", [128, 32 * 128], BF16)
    dinv = ctx.t["lnrs"]
    tmp5 = ctx.t["lnmu"]
    slot_free = [None] * 4
    vh_free = [None]
    ones_w = cw

    for h in range(8):
        kc, kr = h // 3, 32 * (h % 3)
        hs = 0
        vh_ap = bass.AP(v_out.tensor, h * 128, [[D, 128], [128 * D, 32], [1, 128]])
        wv_h = [cc_w] + ([vh_free[hs]] if vh_free[hs] else [])
        p.add("sync", lambda e, hs=hs, a=vh_ap: e.dma_start(
            out=vh_sb[:, hs * 4096:(hs + 1) * 4096], in_=a),
            waits=wv_h, inc=ctx.dx, k=16)
        vh_w = ctx.dx.at()
        for tq0 in range(0, TL, 512):
            for c in range(32):
                bank, bw, bi = ctx.pool.acquire()
                lh = kf_slice(kc, c * 128, (c + 1) * 128)[kr:kr + 32, :]
                rh = q_sb[kr:kr + 32, kc * TL + tq0: kc * TL + tq0 + 512]
                p.add("tensor", lambda e, o=bank, a=lh, b=rh: e.matmul(
                    o[:, 0:512], a, b, start=True, stop=True),
                    waits=[kv_full_w, kv_done_s] + bw, inc=ctx.mm)
                mmw = ctx.mm.at()
                sl = c % 4
                p.add("scalar", lambda e, bk=bank, c=c: e.activation(
                    out=P_buf[:, c * 512:(c + 1) * 512], in_=bk[:, 0:512],
                    func=AF.Exp), waits=[mmw, xb_w], inc=ctx.es)
                expw = [ctx.es.at()] + ([slot_free[sl]] if slot_free[sl] else [])
                p.add("scalar", lambda e, bk=bank, sl=sl: e.activation(
                    out=sexp[:, sl * 512:(sl + 1) * 512], in_=bk[:, 0:512],
                    func=AF.Exp, scale=0.2), waits=expw, inc=ctx.es)
                esw = ctx.es.at()
                p.add("vector", lambda e, c=c, sl=sl: e.tensor_max(
                    out=P_buf[:, c * 512:(c + 1) * 512],
                    in0=P_buf[:, c * 512:(c + 1) * 512],
                    in1=sexp[:, sl * 512:(sl + 1) * 512]),
                    waits=[esw], inc=ctx.ev)
                slot_free[sl] = ctx.ev.at()
                ctx.pool.release(bi, ctx.ev)
            P_w = ctx.ev.at()
            den_bank, dbw, dbi = ctx.pool.acquire()
            dpairs = [(ones[:], P_buf[:, c * 512:(c + 1) * 512]) for c in range(32)]
            dmw = emit_matmul_chain(ctx, den_bank[:, 0:512], dpairs,
                                    waits=[P_w, ones_w] + dbw)
            p.add("vector", lambda e, b=den_bank: e.reciprocal(
                out=dinv[:, 0:512], in_=b[:, 0:512]),
                waits=[dmw], inc=ctx.ev)
            ctx.pool.release(dbi, ctx.ev)
            dinv_w = ctx.ev.at()
            bank, bw, bi = ctx.pool.acquire()
            avpairs = [(vh_sb[:, hs * 4096 + c * 128: hs * 4096 + (c + 1) * 128],
                        P_buf[:, c * 512:(c + 1) * 512]) for c in range(32)]
            avw = emit_matmul_chain(ctx, bank[:, 0:512], avpairs,
                                    waits=[P_w, vh_w] + bw)
            vh_free[hs] = avw
            p.add("vector", lambda e, bk=bank: e.tensor_mul(
                out=tmp5[:, 0:512], in0=bk[:, 0:512], in1=dinv[:, 0:512]),
                waits=[avw, dinv_w], inc=ctx.ev)
            ctx.pool.release(bi, ctx.ev)
            p.add("vector", lambda e, h=h, tq0=tq0: e.scalar_tensor_tensor(
                out=x[:, h * TL + tq0: h * TL + tq0 + 512],
                in0=tmp5[:, 0:512], scalar=gam_sb[:, 0:1],
                in1=x[:, h * TL + tq0: h * TL + tq0 + 512],
                op0=ALU.mult, op1=ALU.add),
                waits=[xb_w, w_g], inc=ctx.ev)
    return ctx.ev.at()



def ap3(t, off, dims):
    return bass.AP(t.tensor if hasattr(t, "tensor") else t, off, dims)


def emit_l0_ffn(ctx):
    XSRC = ctx.dram[ctx.xsrc]
    XDST = ctx.dram[ctx.xdst]
    p = ctx.p
    nc = ctx.nc
    import contextlib
    xg = ctx.nsb("x_sb", [128, DC * TL], F32, side="right")
    x = xg.__enter__()
    p.add("sync", lambda e: e.dma_start(out=x[:], in_=XSRC),
          inc=ctx.dx, k=16)
    xw = ctx.dx.at()
    ones, cw = emit_consts(ctx)
    xln = ctx.sb("xln", [128, DC * TL], BF16)
    ln_w = emit_ln(ctx, x, xln, TL, cw, "ln", x_wait=xw)
    xg.__exit__(None, None, None)

    E = 8
    gw_sb = ctx.sb("gw_sb", [128, DC * E], BF16)
    w_gw = emit_weight_load(ctx, gw_sb[:], chunked(ctx.dram["f0_gw"]))
    gtok = ctx.sb("gtok", [128, (TL // 128) * E], BF16)
    gA = ctx.sb("gA", [128, E], F32)
    gL = ctx.sb("gL", [128, E], F32)
    if F0_DEBUG:
        ctx.sb("dbgl_sb", [128, 4 * E], F32)
    gB = ctx.sb("gB", [128, E], F32)
    gcol = ctx.sb("gcol", [128, 4], F32)
    idbf = ctx.sb("idbf", [128, 128], BF16)
    lnmu_t = ctx.t["lnmu"]
    p.add("gpsimd", lambda e: e.iota(
        lnmu_t[:, 0:128], [[1, 128]], base=0, channel_multiplier=-1,
        allow_small_or_imprecise_dtypes=True), waits=[ln_w], inc=ctx.gp)
    p.add("vector", lambda e: e.tensor_scalar(
        out=idbf[:], in0=lnmu_t[:, 0:128], scalar1=0.0, scalar2=None,
        op0=ALU.is_equal), waits=[ctx.gp.at()], inc=ctx.ev)
    id_w = ctx.ev.at()

    for tc in range(TL // 128):
        bank, bw, bi = ctx.pool.acquire()
        pairs = []
        for k in range(DC):
            pairs.append((xln[:, k * TL + tc * 128: k * TL + (tc + 1) * 128],
                          gw_sb[:, k * E:(k + 1) * E]))
        mmw = emit_matmul_chain(ctx, bank[:, 0:E], pairs, waits=[w_gw, ln_w] + bw)
        def gser(eng, fn):
            p.add(eng, fn, waits=[ctx.ev.at(), ctx.es.at(), ctx.mm.at()],
                  inc=(ctx.ev if eng == "vector" else ctx.es))
        gser("vector", lambda e, b=bank: e.tensor_copy(out=gL[:], in_=b[:, 0:E]))
        if F0_DEBUG and tc == 0:
            gser("vector", lambda e: e.tensor_copy(out=ctx.t["dbgl_sb"][:, 0:E], in_=gL[:]))
        gser("vector", lambda e: e.reduce_max(out=gcol[:, 0:1], in_=gL[:], axis=mybir.AxisListType.X))
        gser("vector", lambda e: e.tensor_scalar(out=gA[:], in0=gL[:], scalar1=gcol[:, 0:1], scalar2=None, op0=ALU.is_ge))
        if F0_DEBUG and tc == 0:
            gser("vector", lambda e: e.tensor_copy(out=ctx.t["dbgl_sb"][:, E:2 * E], in_=gA[:]))
        gser("vector", lambda e: e.scalar_tensor_tensor(out=gB[:], in0=gA[:], scalar=-1e30, in1=gL[:], op0=ALU.mult, op1=ALU.add))
        gser("vector", lambda e: e.reduce_max(out=gcol[:, 1:2], in_=gB[:], axis=mybir.AxisListType.X))
        gser("vector", lambda e: e.tensor_scalar(out=gB[:], in0=gB[:], scalar1=gcol[:, 1:2], scalar2=None, op0=ALU.is_ge))
        gser("vector", lambda e: e.tensor_sub(out=gcol[:, 2:3], in0=gcol[:, 1:2], in1=gcol[:, 0:1]))
        gser("scalar", lambda e: e.activation(out=gcol[:, 2:3], in_=gcol[:, 2:3], func=AF.Exp))
        gser("vector", lambda e: e.tensor_scalar(out=gcol[:, 3:4], in0=gcol[:, 2:3], scalar1=1.0, scalar2=None, op0=ALU.add))
        gser("vector", lambda e: e.reciprocal(out=gcol[:, 3:4], in_=gcol[:, 3:4]))
        gser("vector", lambda e: e.tensor_scalar(out=gB[:], in0=gB[:], scalar1=gcol[:, 2:3], scalar2=None, op0=ALU.mult))
        gser("vector", lambda e: e.tensor_add(out=gB[:], in0=gB[:], in1=gA[:]))
        gser("vector", lambda e, tc=tc: e.tensor_scalar(out=gtok[:, tc * E:(tc + 1) * E], in0=gB[:], scalar1=gcol[:, 3:4], scalar2=None, op0=ALU.mult))
        if F0_DEBUG and tc == 0:
            gser("vector", lambda e: e.tensor_copy(out=ctx.t["dbgl_sb"][:, 2 * E:3 * E], in_=gB[:]))
            gser("vector", lambda e: e.tensor_copy(out=ctx.t["dbgl_sb"][:, 3 * E:3 * E + 4], in_=gcol[:, 0:4]))
            p.add("sync", lambda e: e.dma_start(out=ctx.dram["dbgl"], in_=ctx.t["dbgl_sb"][:]),
                  waits=[ctx.ev.at()], inc=ctx.dx, k=16)
        ctx.pool.release(bi, ctx.ev)
    gtok_w = ctx.ev.at()
    if F0_DEBUG:
        p.add("sync", lambda e: e.dma_start(out=ctx.dram["dbgg"], in_=gtok[:]),
              waits=[gtok_w], inc=ctx.dx, k=16)

    ptr_bf = ctx.stack.enter_context(nc.psum_tensor(ctx.tag + "ptr_bf", [128, 1024], BF16))
    gfm = ctx.sb("gfm", [128, TL], BF16)
    grow = ctx.sb("grow", [1, 1024], BF16)
    grow_user = [None]
    for tc in range(TL // 128):
        p.add("tensor", lambda e, tc=tc: e.transpose(
            ptr_bf[0:E, 0:128], gtok[:, tc * E:(tc + 1) * E], idbf[:]),
            waits=[gtok_w, id_w] if tc == 0 else [ctx.ev.at()], inc=ctx.mm)
        p.add("vector", lambda e, tc=tc: e.tensor_copy(
            out=gfm[0:E, tc * 128:(tc + 1) * 128], in_=ptr_bf[0:E, 0:128]),
            waits=[ctx.mm.at()], inc=ctx.ev)
    gfm_w = ctx.ev.at()

    hid = ctx.sb("hid", [128, 16 * 1024], BF16)
    w13 = ctx.sb("w13", [128, 2 * DC * 1024], BF16)
    w2b_ = ctx.sb("w2b_", [128, 2 * DC * 1024], BF16)
    grep = ctx.sb("grep", [128, 1024], F32)
    stile = ctx.sb("stile", [128, 2 * 512], BF16)
    utile = ctx.sb("utile", [128, 2 * 512], BF16)
    stage = ctx.sb("stage", [128, 8 * 512], F32)

    p.add("gpsimd", lambda e, XS=XSRC, XD=XDST: e.dma_start(out=XD, in_=XS),
          inc=ctx.dw, k=16)
    xo_init = ctx.dw.at()

    HD = 1024 * 2048
    qslot_user = [None, None]
    w2_user = [None]
    hid_user = [None]
    qctr = [0]
    oslot = [0]
    ofree = [None] * 8

    def run_expert(e_idx, th, shared, first=False):
        sfree = [None, None]
        if not shared:
            gw2 = [gfm_w] + ([grow_user[0]] if grow_user[0] else [])
            p.add("sync", lambda e: e.dma_start(
                out=grow[0:1, 0:1024],
                in_=gfm[e_idx:e_idx + 1, th * 1024:(th + 1) * 1024]),
                waits=gw2, inc=ctx.dx, k=16)
            growd = ctx.dx.at()
            for i in range(2):
                bank, bw, bi = ctx.pool.acquire()
                p.add("tensor", lambda e, b=bank, i=i: e.matmul(
                    b[:, 0:512], ones[0:1, 0:128],
                    grow[0:1, i * 512:(i + 1) * 512],
                    start=True, stop=True),
                    waits=[growd, cw] + bw, inc=ctx.mm)
                p.add("vector", lambda e, b=bank, i=i: e.tensor_copy(
                    out=grep[:, i * 512:(i + 1) * 512], in_=b[:, 0:512]),
                    waits=[ctx.mm.at()], inc=ctx.ev)
                ctx.pool.release(bi, ctx.ev)
            grow_user[0] = ctx.mm.at()
        grep_w = ctx.ev.at()
        if shared:
            aw2a = ap3(ctx.dram["f0_sd"], 0, [[1024, 128], [128 * 1024, DC], [1, 1024]])
            aw2b = ap3(ctx.dram["f0_sd"], 1024 * 1024, [[1024, 128], [128 * 1024, DC], [1, 1024]])
        else:
            off = e_idx * 2048 * 1024
            aw2a = ap3(ctx.dram["f0_wd"], off, [[1024, 128], [128 * 1024, DC], [1, 1024]])
            aw2b = ap3(ctx.dram["f0_wd"], off + 1024 * 1024, [[1024, 128], [128 * 1024, DC], [1, 1024]])
        lw = [w2_user[0]] if w2_user[0] else []
        wl2a = emit_weight_load(ctx, w2b_[:, 0:DC * 1024], aw2a, waits=lw)
        wl2b = emit_weight_load(ctx, w2b_[:, DC * 1024:], aw2b, waits=lw)
        for dq in range(4):
            sl = qctr[0] % 2
            qctr[0] += 1
            base = sl * DC * 1024
            w1q = w13[:, base: base + DC * 512]
            w3q = w13[:, base + DC * 512: base + DC * 1024]
            if shared:
                a1 = ap3(ctx.dram["f0_sg"], dq * 512, [[2048, 128], [128 * 2048, DC], [1, 512]])
                a3 = ap3(ctx.dram["f0_su"], dq * 512, [[2048, 128], [128 * 2048, DC], [1, 512]])
            else:
                a1 = ap3(ctx.dram["f0_wg"], e_idx * HD + dq * 512,
                         [[2048, 128], [128 * 2048, DC], [1, 512]])
                a3 = ap3(ctx.dram["f0_wu"], e_idx * HD + dq * 512,
                         [[2048, 128], [128 * 2048, DC], [1, 512]])
            lw = [qslot_user[sl]] if qslot_user[sl] else []
            wl1 = emit_weight_load(ctx, w1q, a1, waits=lw)
            wl3 = emit_weight_load(ctx, w3q, a3, waits=lw)
            for dc in range(4):
                for t5 in range(2):
                    tog = (dc * 2 + t5) % 2
                    hcol = (dq * 4 + dc) * 1024 + t5 * 512
                    xcol = th * 1024 + t5 * 512
                    b1, bw1, bi1 = ctx.pool.acquire()
                    pairs = [(w1q[:, k * 512 + dc * 128: k * 512 + (dc + 1) * 128],
                              xln[:, k * TL + xcol: k * TL + xcol + 512])
                             for k in range(DC)]
                    m1w = emit_matmul_chain(ctx, b1[:, 0:512], pairs,
                                            waits=[wl1, ln_w] + bw1)
                    sw = [m1w] + ([sfree[tog]] if sfree[tog] else [])
                    p.add("scalar", lambda e, b=b1, tog=tog: e.activation(
                        out=stile[:, tog * 512:(tog + 1) * 512], in_=b[:, 0:512],
                        func=AF.Silu), waits=sw, inc=ctx.es)
                    ctx.pool.release(bi1, ctx.es)
                    b3, bw3, bi3 = ctx.pool.acquire()
                    pairs = [(w3q[:, k * 512 + dc * 128: k * 512 + (dc + 1) * 128],
                              xln[:, k * TL + xcol: k * TL + xcol + 512])
                             for k in range(DC)]
                    m3w = emit_matmul_chain(ctx, b3[:, 0:512], pairs,
                                            waits=[wl3, ln_w] + bw3)
                    if shared:
                        p.add("vector", lambda e, b=b3, tog=tog: e.tensor_copy(
                            out=utile[:, tog * 512:(tog + 1) * 512], in_=b[:, 0:512]),
                            waits=[m3w], inc=ctx.ev)
                    else:
                        p.add("vector", lambda e, b=b3, tog=tog, t5=t5: e.tensor_mul(
                            out=utile[:, tog * 512:(tog + 1) * 512], in0=b[:, 0:512],
                            in1=grep[:, t5 * 512:(t5 + 1) * 512]),
                            waits=[m3w, grep_w], inc=ctx.ev)
                    ctx.pool.release(bi3, ctx.ev)
                    hw = [ctx.es.at(), ctx.ev.at()]
                    if hid_user[0] is not None:
                        hw.append(hid_user[0])
                    p.add("vector", lambda e, hcol=hcol, tog=tog: e.tensor_mul(
                        out=hid[:, hcol:hcol + 512],
                        in0=stile[:, tog * 512:(tog + 1) * 512],
                        in1=utile[:, tog * 512:(tog + 1) * 512]),
                        waits=hw, inc=ctx.ev)
                    sfree[tog] = ctx.ev.at()
            qslot_user[sl] = ctx.mm.at()
        hid_w = ctx.ev.at()
        for oc in range(DC):
            for t5 in range(2):
                bank, bw, bi = ctx.pool.acquire()
                pairs = []
                for half in range(2):
                    for kc in range(DC):
                        pairs.append((w2b_[:, half * DC * 1024 + kc * 1024 + oc * 128:
                                           half * DC * 1024 + kc * 1024 + (oc + 1) * 128],
                                      hid[:, (half * 8 + kc) * 1024 + t5 * 512:
                                          (half * 8 + kc) * 1024 + t5 * 512 + 512]))
                mmw = emit_matmul_chain(ctx, bank[:, 0:512], pairs,
                                        waits=[wl2a, wl2b, hid_w] + bw)
                sl2 = oslot[0] % 8
                oslot[0] += 1
                sw = [mmw] + ([ofree[sl2]] if ofree[sl2] else [])
                p.add("vector", lambda e, b=bank, sl2=sl2: e.tensor_copy(
                    out=stage[:, sl2 * 512:(sl2 + 1) * 512], in_=b[:, 0:512]),
                    waits=sw, inc=ctx.ev)
                ctx.pool.release(bi, ctx.ev)
                tcol = th * 1024 + t5 * 512
                p.add("gpsimd", lambda e, sl2=sl2, oc=oc, tcol=tcol, XD=XDST: e.dma_start(
                    out=XD[:, oc * TL + tcol: oc * TL + tcol + 512],
                    in_=stage[:, sl2 * 512:(sl2 + 1) * 512], accum_op=ALU.add),
                    waits=[ctx.ev.at(), xo_init], inc=ctx.dw, k=16)
                ofree[sl2] = ctx.dw.at()
        hid_user[0] = ctx.mm.at()
        w2_user[0] = ctx.mm.at()

    for th in range(2):
        for pos, e_idx in enumerate(F0_EXPERTS):
            run_expert(e_idx if e_idx < 8 else 0, th, shared=(e_idx == 8))
    p.add("sync", None, waits=[ctx.dw.at(), ctx.dx.at()])


def _emit_f0(ctx):
    emit_l0_ffn(ctx)



def emit_l0_attn_b(ctx):
    XSRC = ctx.dram[ctx.xsrc]
    XDST = ctx.dram[ctx.xdst]
    p = ctx.p
    nc = ctx.nc
    p.add("gpsimd", lambda e: e.dma_start(out=XDST, in_=XSRC),
          inc=ctx.dw, k=16)
    xo_init = ctx.dw.at()

    xg = ctx.nsb("x_sb", [128, DC * TL], F32, side="right")
    x = xg.__enter__()
    p.add("sync", lambda e: e.dma_start(out=x[:], in_=XSRC),
          inc=ctx.dx, k=16)
    xw = ctx.dx.at()
    ones, cw = emit_consts(ctx, sdt=FP16)
    xln = ctx.sb("xln", [128, DC * TL], FP16)
    ln_w = emit_ln(ctx, x, xln, TL, cw, "ln", x_wait=xw)
    xg.__exit__(None, None, None)

    k_in = ctx.ndt("k_in", [D, TL], FP16)
    v_in = ctx.ndt("v_in", [TL, D], BF16)
    k_out = ctx.ndt("k_out", [2 * D, TL], FP16)
    v_out = ctx.ndt("v_out", [2 * TL, D], BF16)
    att_d = ctx.ndt("att_d", [D, TL], BF16)

    q_sb = ctx.sb("q_sb", [128, DC * TL], FP16)
    wbuf = ctx.sb("wbuf", [128, DC * D], FP16)
    bqc = ctx.sb("bqc", [128, 8], F32)
    bkc = ctx.sb("bkc", [128, 8], F32)
    boc = ctx.sb("boc", [128, 8], F32)
    bvr = ctx.sb("bvr", [128, D], BF16)
    stage = ctx.sb("stage", [128, 8 * 512], F32)
    stbf = ctx.sb("stbf", [128, 4 * 512], BF16)
    w_bq = emit_weight_load(ctx, bqc[:], ctx.dram["b0_bq"])
    w_bk = emit_weight_load(ctx, bkc[:], ctx.dram["b0_bk"])
    w_bo = emit_weight_load(ctx, boc[:], ctx.dram["b0_bo"])
    w_bv = emit_weight_load(ctx, bvr[:], ctx.dram["b0_bv"])

    kg = ctx.nsb("k_sb", [128, DC * TL], FP16, side="right")
    k_sb = kg.__enter__()
    w_k = emit_weight_load(ctx, wbuf[:], chunked(ctx.dram["b0_wk"]))

    def evac_k(bank, oc, t0, tw, waits):
        p.add("vector", lambda e, bk=bank, oc=oc, t0=t0, tw=tw: e.tensor_scalar(
            out=k_sb[:, oc * TL + t0: oc * TL + t0 + tw], in0=bk[:, 0:tw],
            scalar1=bkc[:, oc:oc + 1], scalar2=-2.0, op0=ALU.add, op1=ALU.mult),
            waits=waits + [w_bk], inc=ctx.ev)
        return ctx.ev

    emit_proj_fm(ctx, wbuf, w_k, xln, ln_w, TL, D, evac_k)
    p.add("sync", lambda e: e.dma_start(out=chunked(k_in), in_=k_sb[:]),
          waits=[ctx.ev.at()], inc=ctx.dx, k=16)
    p.add("gpsimd", lambda e: e.collective_compute(
        "AllGather", ALU.bypass, replica_groups=PAIRS,
        ins=[k_in.opt()], outs=[k_out.opt()]),
        waits=[ctx.dx.at()], inc=ctx.cc)
    kg.__exit__(None, None, None)

    w_q = emit_weight_load(ctx, wbuf[:], chunked(ctx.dram["b0_wq"]),
                           waits=[ctx.mm.at()])

    def evac_q(bank, oc, t0, tw, waits):
        p.add("scalar", lambda e, bk=bank, oc=oc, t0=t0, tw=tw: e.activation(
            out=q_sb[:, oc * TL + t0: oc * TL + t0 + tw], in_=bk[:, 0:tw],
            func=AF.Identity, bias=bqc[:, oc:oc + 1]),
            waits=waits + [w_bq], inc=ctx.es)
        return ctx.es

    emit_proj_fm(ctx, wbuf, w_q, xln, ln_w, TL, D, evac_q)

    w_v = emit_weight_load(ctx, wbuf[:], chunked(ctx.dram["b0_wv"]),
                           waits=[ctx.mm.at()])
    vslot_free = [None] * 4
    vslot_i = [0]

    def evac_v(bank, tc, o0, ow, waits):
        sl = vslot_i[0] % 4
        vslot_i[0] += 1
        w2 = waits + [w_bv] + ([vslot_free[sl]] if vslot_free[sl] else [])
        p.add("vector", lambda e, bk=bank, sl=sl, o0=o0, ow=ow: e.tensor_add(
            out=stbf[:, sl * 512: sl * 512 + ow],
            in0=bk[:, 0:ow], in1=bvr[:, o0:o0 + ow]),
            waits=w2, inc=ctx.ev)
        evw = ctx.ev.at()
        p.add("sync", lambda e, sl=sl, tc=tc, o0=o0, ow=ow: e.dma_start(
            out=v_in[tc * 128:(tc + 1) * 128, o0:o0 + ow],
            in_=stbf[:, sl * 512: sl * 512 + ow]),
            waits=[evw], inc=ctx.dx, k=16)
        vslot_free[sl] = ctx.dx.at()
        return ctx.ev

    emit_proj_tm(ctx, wbuf, w_v, xln, ln_w, TL, D, evac_v)
    p.add("gpsimd", lambda e: e.collective_compute(
        "AllGather", ALU.bypass, replica_groups=PAIRS,
        ins=[v_in.opt()], outs=[v_out.opt()]),
        waits=[ctx.dx.at()], inc=ctx.cc)
    cc_w = ctx.cc.at()

    P_buf = xln
    augq = ctx.sb("augq", [128, TL], FP16)
    augk = ctx.sb("augk", [128, T], FP16)
    vha = ctx.sb("vha", [128, 32 * 65], FP16)
    rowq = ctx.sb("rowq", [1, TL], FP16)
    rowk = ctx.sb("rowk", [1, T], FP16)
    drow = ctx.sb("drow", [1, 512], F32)
    drbf = ctx.sb("drbf", [1, 512], FP16)
    epsln = ctx.sb("epsln", [128, 1], F32)
    p.add("vector", lambda e: e.memset(epsln[:], 1e-6), inc=ctx.ev)
    sq = ctx.t["lnsq"]
    slots = ctx.t["lnmu"]
    dinvrep = ctx.t["lnrs"]
    augq_user = [None]
    augk_user = [None]
    vha_user = [None]
    prev_P = [None]

    for h in range(16):
        qc, qr = h // 2, 64 * (h % 2)
        aw = [ctx.es.at()] + ([augq_user[0]] if augq_user[0] else [])
        p.add("vector", lambda e: e.memset(augq[64:66, :], 1.0),
              waits=([augq_user[0]] if augq_user[0] else []), inc=ctx.ev)
        p.add("vector", lambda e, qc=qc, qr=qr: e.tensor_copy(
            out=augq[0:64, :], in_=q_sb[qr:qr + 64, qc * TL:(qc + 1) * TL]),
            waits=aw, inc=ctx.ev)
        p.add("scalar", lambda e: e.activation(
            out=sq[0:64, 0:TL], in_=augq[0:64, :], func=AF.Square),
            waits=[ctx.ev.at()], inc=ctx.es)
        sq_w = ctx.es.at()
        for i in range(TL // 512):
            bank, bw, bi = ctx.pool.acquire()
            p.add("tensor", lambda e, b=bank, i=i: e.matmul(
                b[0:1, 0:512], ones[0:64, 0:1], sq[0:64, i * 512:(i + 1) * 512],
                start=True, stop=True),
                waits=[sq_w, cw] + bw, inc=ctx.mm)
            p.add("scalar", lambda e, b=bank, i=i: e.activation(
                out=rowq[0:1, i * 512:(i + 1) * 512], in_=b[0:1, 0:512],
                func=AF.Identity), waits=[ctx.mm.at()], inc=ctx.es)
            ctx.pool.release(bi, ctx.es)
        p.add("sync", lambda e: e.dma_start(out=augq[65:66, :], in_=rowq[0:1, :]),
              waits=[ctx.es.at(), ctx.ev.at()], inc=ctx.dx, k=16)
        augq_w = ctx.dx.at()
        p.add("vector", lambda e: e.memset(augk[64:66, :], 1.0),
              waits=([augk_user[0]] if augk_user[0] else []), inc=ctx.ev)
        mko = ctx.ev.at()
        for r in range(2):
            p.add("sync", lambda e, r=r, h=h: e.dma_start(
                out=augk[0:64, r * TL:(r + 1) * TL],
                in_=k_out[r * D + 64 * h: r * D + 64 * h + 64, :]),
                waits=[cc_w] + ([augk_user[0]] if augk_user[0] else []),
                inc=ctx.dx, k=16)
        p.add("scalar", lambda e: e.activation(
            out=sq[0:64, 0:T // 2], in_=augk[0:64, 0:T // 2], func=AF.Square,
            scale=0.5), waits=[ctx.dx.at()], inc=ctx.es)
        p.add("scalar", lambda e: e.activation(
            out=sq[0:64, T // 2:T], in_=augk[0:64, T // 2:T], func=AF.Square,
            scale=0.5), inc=ctx.es)
        sqk_w = ctx.es.at()
        for i in range(T // 512):
            bank, bw, bi = ctx.pool.acquire()
            p.add("tensor", lambda e, b=bank, i=i: e.matmul(
                b[0:1, 0:512], ones[0:64, 0:1], sq[0:64, i * 512:(i + 1) * 512],
                start=True, stop=True),
                waits=[sqk_w, cw] + bw, inc=ctx.mm)
            p.add("scalar", lambda e, b=bank, i=i: e.activation(
                out=rowk[0:1, i * 512:(i + 1) * 512], in_=b[0:1, 0:512],
                func=AF.Identity), waits=[ctx.mm.at()], inc=ctx.es)
            ctx.pool.release(bi, ctx.es)
        p.add("sync", lambda e: e.dma_start(out=augk[64:65, :], in_=rowk[0:1, :]),
              waits=[ctx.es.at(), mko], inc=ctx.dx, k=16)
        augk_w = ctx.dx.at()
        vw = [cc_w] + ([vha_user[0]] if vha_user[0] else [])
        p.add("vector", lambda e: e.memset(vha[:], 1.0),
              waits=([vha_user[0]] if vha_user[0] else []), inc=ctx.ev)
        mvo = ctx.ev.at()
        vsrc = bass.AP(v_out.tensor, 64 * h, [[D, 128], [128 * D, 32], [1, 64]])
        vdst = bass.AP(vha, 0, [[32 * 65, 128], [65, 32], [1, 64]])
        p.add("gpsimd", lambda e, a=vsrc, d=vdst: e.dma_start(out=d, in_=a),
              waits=vw + [mvo], inc=ctx.dw, k=16)
        vha_w = ctx.dw.at()

        for tq0 in range(0, TL, 512):
            for c in range(32):
                bank, bw, bi = ctx.pool.acquire()
                p.add("tensor", lambda e, b=bank, c=c, tq0=tq0: e.matmul(
                    b[:, 0:512], augk[0:66, c * 128:(c + 1) * 128],
                    augq[0:66, tq0:tq0 + 512], start=True, stop=True),
                    waits=[augq_w, augk_w] + bw +
                          ([prev_P[0]] if (prev_P[0] and c == 0 and tq0 == 0) else []),
                    inc=ctx.mm)
                mmw = ctx.mm.at()
                sl = (c % 2) * 512
                p.add("scalar", lambda e, b=bank, sl=sl: e.activation(
                    out=slots[:, sl:sl + 512], in_=b[:, 0:512], func=AF.Ln,
                    bias=epsln[:, 0:1]), waits=[mmw], inc=ctx.es)
                p.add("scalar", lambda e, sl=sl: e.activation(
                    out=slots[:, sl:sl + 512], in_=slots[:, sl:sl + 512],
                    func=AF.Exp, scale=0.5), inc=ctx.es)
                p.add("scalar", lambda e, c=c, sl=sl: e.activation(
                    out=P_buf[:, c * 512:(c + 1) * 512], in_=slots[:, sl:sl + 512],
                    func=AF.Exp, scale=-1.0), inc=ctx.es)
                ctx.pool.release(bi, ctx.es)
            P_w = ctx.es.at()
            bank, bw, bi = ctx.pool.acquire()
            avpairs = [(vha[:, c * 65:(c + 1) * 65],
                        P_buf[:, c * 512:(c + 1) * 512]) for c in range(32)]
            avw = emit_matmul_chain(ctx, bank[0:65, 0:512], avpairs,
                                    waits=[P_w, vha_w] + bw)
            vha_user[0] = avw
            prev_P[0] = avw
            p.add("vector", lambda e, b=bank: e.reciprocal(
                out=drow[0:1, 0:512], in_=b[64:65, 0:512]),
                waits=[avw], inc=ctx.ev)
            p.add("vector", lambda e: e.tensor_copy(
                out=drbf[0:1, 0:512], in_=drow[0:1, 0:512]), inc=ctx.ev)
            b2, bw2, bi2 = ctx.pool.acquire()
            p.add("tensor", lambda e, b=b2: e.matmul(
                b[0:64, 0:512], ones[0:1, 0:64], drbf[0:1, 0:512],
                start=True, stop=True),
                waits=[ctx.ev.at(), cw] + bw2, inc=ctx.mm)
            p.add("vector", lambda e, b=b2: e.tensor_copy(
                out=dinvrep[0:64, 0:512], in_=b[0:64, 0:512]),
                waits=[ctx.mm.at()], inc=ctx.ev)
            ctx.pool.release(bi2, ctx.ev)
            sl = (tq0 // 512) % 4
            p.add("vector", lambda e, b=bank, sl=sl: e.tensor_mul(
                out=stbf[0:64, sl * 512: sl * 512 + 512],
                in0=b[0:64, 0:512], in1=dinvrep[0:64, 0:512]),
                waits=[ctx.ev.at()], inc=ctx.ev)
            ctx.pool.release(bi, ctx.ev)
            p.add("sync", lambda e, h=h, tq0=tq0, sl=sl: e.dma_start(
                out=att_d[64 * h:64 * h + 64, tq0:tq0 + 512],
                in_=stbf[0:64, sl * 512: sl * 512 + 512]),
                waits=[ctx.ev.at()], inc=ctx.dx, k=16)
        augq_user[0] = ctx.mm.at()
        augk_user[0] = ctx.mm.at()
    att_ship = ctx.dx.at()

    w_o = emit_weight_load(ctx, wbuf[:], chunked(ctx.dram["b0_wo"]),
                           waits=[ctx.mm.at()])
    attk = ctx.sb("attk", [128, DC * 512], FP16)
    oslot_free = [None] * 4
    osl = [0]
    attk_user = [None]
    for t4 in range(4):
        akw = [att_ship] + ([attk_user[0]] if attk_user[0] else [])
        asrc = ap3(att_d, t4 * 512, [[TL, 128], [128 * TL, DC], [1, 512]])
        p.add("gpsimd", lambda e, a=asrc: e.dma_start(out=attk[:], in_=a),
              waits=akw, inc=ctx.dw, k=16)
        ak_w = ctx.dw.at()
        for oc in range(DC):
            bank, bw, bi = ctx.pool.acquire()
            pairs = [(wbuf[:, k * D + oc * 128: k * D + (oc + 1) * 128],
                      attk[:, k * 512:(k + 1) * 512]) for k in range(DC)]
            mmw = emit_matmul_chain(ctx, bank[:, 0:512], pairs,
                                    waits=[w_o, ak_w] + bw)
            sl = osl[0] % 4
            osl[0] += 1
            sw = [mmw, w_bo] + ([oslot_free[sl]] if oslot_free[sl] else [])
            p.add("vector", lambda e, b=bank, sl=sl, oc=oc: e.tensor_scalar(
                out=stage[:, sl * 512:(sl + 1) * 512], in0=b[:, 0:512],
                scalar1=boc[:, oc:oc + 1], scalar2=None, op0=ALU.add),
                waits=sw, inc=ctx.ev)
            ctx.pool.release(bi, ctx.ev)
            evw = ctx.ev.at()
            p.add("gpsimd", lambda e, sl=sl, oc=oc, t4=t4: e.dma_start(
                out=XDST[:, oc * TL + t4 * 512: oc * TL + (t4 + 1) * 512],
                in_=stage[:, sl * 512:(sl + 1) * 512], accum_op=ALU.add),
                waits=[evw, xo_init], inc=ctx.dw, k=16)
            oslot_free[sl] = ctx.dw.at()
        attk_user[0] = ctx.mm.at()
    p.add("sync", None, waits=[ctx.dw.at()])


def _emit_b0(ctx):
    emit_l0_attn_b(ctx)



def emit_l1_wkv(ctx):
    XSRC = ctx.dram[ctx.xsrc]
    XDST = ctx.dram[ctx.xdst]
    p = ctx.p
    nc = ctx.nc
    p.add("gpsimd", lambda e: e.dma_start(out=XDST, in_=XSRC),
          inc=ctx.dw, k=16)
    xo_init = ctx.dw.at()

    xg = ctx.nsb("x_sb", [128, DC * TL], F32, side="right")
    x = xg.__enter__()
    p.add("sync", lambda e: e.dma_start(out=x[:], in_=XSRC),
          inc=ctx.dx, k=16)
    xw = ctx.dx.at()
    ones, cw = emit_consts(ctx)

    wbuf = ctx.sb("wbuf", [128, DC * D], BF16)
    stage = ctx.sb("stage", [128, 8 * 512], F32)
    stbf = ctx.sb("stbf", [128, 4 * 512], BF16)
    dcol = ctx.sb("dcol", [128, 8], F32)
    ucol = ctx.sb("ucol", [128, 8], F32)
    mkc = ctx.sb("mkc", [128, 8], F32)
    mvc = ctx.sb("mvc", [128, 8], F32)
    mrc = ctx.sb("mrc", [128, 8], F32)
    bmask = ctx.sb("bmask", [128, 8], F32)
    bprev = ctx.sb("bprev", [128, 8], F32)
    carry = ctx.sb("carry", [128, 4], F32)
    w_d = emit_weight_load(ctx, dcol[:], ctx.dram["w1_decay"])
    w_u = emit_weight_load(ctx, ucol[:], ctx.dram["w1_first"])
    w_mk = emit_weight_load(ctx, mkc[:], ctx.dram["w1_mk"])
    w_mv = emit_weight_load(ctx, mvc[:], ctx.dram["w1_mv"])
    w_mr = emit_weight_load(ctx, mrc[:], ctx.dram["w1_mr"])
    w_bm = emit_weight_load(ctx, bmask[:], ctx.dram["w1_bmask"])

    guards = []

    def salloc(name, shape, dtype):
        g = ctx.nsb(name, shape, dtype)
        t = g.__enter__()
        guards.append(g)
        return t

    xln = salloc("xln", [128, DC * TL], BF16)
    ctx.t["xln"] = xln

    ln_w = emit_ln(ctx, x, xln, TL, cw, "ln", x_wait=xw)
    xg.__exit__(None, None, None)

    bcs = salloc("bcs", [128, 8], F32)
    mixk = salloc("mixk", [128, DC * 512], BF16)
    mixv = salloc("mixv", [128, DC * 512], BF16)
    mixr = salloc("mixr", [128, DC * 512], BF16)
    dbuf = salloc("dbuf", [128, DC * 512], BF16)
    wv_sb = salloc("wv2_sb", [128, DC * D], BF16)
    wr_sb = salloc("wr_sb", [128, DC * D], BF16)

    bc_in = ctx.ndt("bc_in", [128, 8], F32)
    bc_out = ctx.ndt("bc_out", [256, 8], F32)
    kv_in = ctx.ndt("kv_in", [2 * D, TL], BF16)
    kv_out = ctx.ndt("kv_out", [4 * D, TL], BF16)
    r_d = ctx.ndt("r_d", [D, TL], BF16)
    wkv_d = ctx.ndt("wkv_d", [D, T], BF16)

    for k in range(DC):
        p.add("vector", lambda e, k=k: e.tensor_copy(
            out=bcs[:, k:k + 1], in_=xln[:, k * TL + TL - 1: k * TL + TL]),
            waits=[ln_w] if k == 0 else (), inc=ctx.ev)
    p.add("sync", lambda e: e.dma_start(out=bc_in, in_=bcs[:]),
          waits=[ctx.ev.at()], inc=ctx.dx, k=16)
    p.add("gpsimd", lambda e: e.collective_compute(
        "AllGather", ALU.bypass, replica_groups=PAIRS,
        ins=[bc_in.opt()], outs=[bc_out.opt()]),
        waits=[ctx.dx.at()], inc=ctx.cc)
    p.add("sync", lambda e: e.dma_start(out=bprev[:], in_=bc_out[0:128, :]),
          waits=[ctx.cc.at()], inc=ctx.dx, k=16)
    p.add("vector", lambda e: e.tensor_mul(out=bprev[:], in0=bprev[:], in1=bmask[:]),
          waits=[ctx.dx.at(), w_bm], inc=ctx.ev)
    bprev_w = ctx.ev.at()

    w_wk = emit_weight_load(ctx, wbuf[:], chunked(ctx.dram["w1_wk"]))
    w_wv = emit_weight_load(ctx, wv_sb[:], chunked(ctx.dram["w1_wv"]))
    w_wr = emit_weight_load(ctx, wr_sb[:], chunked(ctx.dram["w1_wr"]))

    slot_i = [0]
    slot_free = [None] * 4
    mix_user = [None]

    def ship(tgt, r0, t0, bank, waits, act=None):
        sl = slot_i[0] % 4
        slot_i[0] += 1
        w2 = list(waits) + ([slot_free[sl]] if slot_free[sl] else [])
        if act is None:
            p.add("vector", lambda e, b=bank, sl=sl: e.tensor_copy(
                out=stbf[:, sl * 512: sl * 512 + 512], in_=b[:, 0:512]),
                waits=w2, inc=ctx.ev)
            dwn = ctx.ev.at()
            sem = ctx.ev
        else:
            p.add("scalar", lambda e, b=bank, sl=sl: e.activation(
                out=stbf[:, sl * 512: sl * 512 + 512], in_=b[:, 0:512], func=act),
                waits=w2, inc=ctx.es)
            dwn = ctx.es.at()
            sem = ctx.es
        p.add("sync", lambda e, sl=sl, tgt=tgt, r0=r0, t0=t0: e.dma_start(
            out=tgt[r0:r0 + 128, t0:t0 + 512],
            in_=stbf[:, sl * 512: sl * 512 + 512]),
            waits=[dwn], inc=ctx.dx, k=16)
        slot_free[sl] = ctx.dx.at()
        return sem

    for ti in [1, 2, 3, 0]:
        t0 = ti * 512
        mu = ([mix_user[0]] if mix_user[0] else [])
        for k in range(DC):
            lo = t0 - 1 if ti > 0 else 0
            span = 512 if ti > 0 else 511
            od = 0 if ti > 0 else 1
            p.add("vector", lambda e, k=k, t0=t0, lo=lo, span=span, od=od: e.tensor_sub(
                out=dbuf[:, k * 512 + od: k * 512 + od + span],
                in0=xln[:, k * TL + t0 + od: k * TL + t0 + 512],
                in1=xln[:, k * TL + lo: k * TL + lo + span]),
                waits=(mu + [ln_w]) if k == 0 else (), inc=ctx.ev)
            for mx, mc in ((mixk, mkc), (mixv, mvc), (mixr, mrc)):
                p.add("vector", lambda e, k=k, t0=t0, lo=lo, span=span, od=od, mx=mx, mc=mc:
                      e.scalar_tensor_tensor(
                          out=mx[:, k * 512 + od: k * 512 + od + span],
                          in0=dbuf[:, k * 512 + od: k * 512 + od + span],
                          scalar=mc[:, k:k + 1],
                          in1=xln[:, k * TL + lo: k * TL + lo + span],
                          op0=ALU.mult, op1=ALU.add),
                      waits=[w_mk, w_mv, w_mr] if k == 0 else (), inc=ctx.ev)
            if ti == 0:
                p.add("vector", lambda e, k=k: e.tensor_sub(
                    out=dbuf[:, k * 512: k * 512 + 1],
                    in0=xln[:, k * TL: k * TL + 1], in1=bprev[:, k:k + 1]),
                    waits=[bprev_w], inc=ctx.ev)
                for mx, mc in ((mixk, mkc), (mixv, mvc), (mixr, mrc)):
                    p.add("vector", lambda e, k=k, mx=mx, mc=mc: e.scalar_tensor_tensor(
                        out=mx[:, k * 512: k * 512 + 1],
                        in0=dbuf[:, k * 512: k * 512 + 1],
                        scalar=mc[:, k:k + 1], in1=bprev[:, k:k + 1],
                        op0=ALU.mult, op1=ALU.add), inc=ctx.ev)
        mix_w = ctx.ev.at()
        for (wsb, wld, mx, tgt, r00, act) in (
                (wbuf, w_wk, mixk, kv_in, 0, None),
                (wv_sb, w_wv, mixv, kv_in, D, None),
                (wr_sb, w_wr, mixr, r_d, 0, AF.Sigmoid)):
            for oc in range(DC):
                bank, bw, bi = ctx.pool.acquire()
                pairs = [(wsb[:, k * D + oc * 128: k * D + (oc + 1) * 128],
                          mx[:, k * 512:(k + 1) * 512]) for k in range(DC)]
                mmw = emit_matmul_chain(ctx, bank[:, 0:512], pairs,
                                       waits=[wld, mix_w] + bw)
                sem = ship(tgt, r00 + oc * 128, t0, bank, [mmw], act)
                ctx.pool.release(bi, sem)
        mix_user[0] = ctx.mm.at()
    p.add("gpsimd", lambda e: e.collective_compute(
        "AllGather", ALU.bypass, replica_groups=PAIRS,
        ins=[kv_in.opt()], outs=[kv_out.opt()]),
        waits=[ctx.dx.at()], inc=ctx.cc)
    kv_cc = ctx.cc.at()

    for g in reversed(guards):
        g.__exit__(None, None, None)
    TH = TL
    guards2 = []

    def salloc2(name, shape, dtype):
        g = ctx.nsb(name, shape, dtype)
        t = g.__enter__()
        guards2.append(g)
        return t

    S = [salloc2(f"scan{i}", [128, TH + 2], F32) for i in range(8)]
    ktile = salloc2("ktile", [128, TH], BF16)
    vtile = salloc2("vtile", [128, TH], BF16)
    wkvbf = salloc2("wkvbf", [128, TH], BF16)

    for cc2 in range(8):
        for th2 in range(2):
            tb = th2 * TH
            bwv = [ctx.ev.at(), ctx.es.at(), ctx.gp.at(), ctx.dx.at(), ctx.mm.at()]
            if th2 == 0:
                p.add("vector", lambda e: e.memset(carry[:, 0:1], NEG), waits=bwv, inc=ctx.ev)
                p.add("vector", lambda e: e.memset(carry[:, 1:3], 0.0), inc=ctx.ev)
            p.add("gpsimd", lambda e, tb=tb: e.iota(
                S[0][:, 0:TH], [[1, TH]], base=tb, channel_multiplier=0,
                allow_small_or_imprecise_dtypes=True), waits=bwv, inc=ctx.gp)
            p.add("vector", lambda e, cc2=cc2: e.tensor_scalar(
                out=S[0][:, 0:TH], in0=S[0][:, 0:TH], scalar1=dcol[:, cc2:cc2 + 1],
                scalar2=None, op0=ALU.mult), waits=bwv + [ctx.gp.at(), w_d], inc=ctx.ev)
            p.add("sync", lambda e, th2=th2, cc2=cc2: e.dma_start(
                out=ktile[:],
                in_=kv_out[th2 * 2 * D + cc2 * 128: th2 * 2 * D + (cc2 + 1) * 128, :]),
                waits=bwv + [kv_cc], inc=ctx.dx, k=16)
            p.add("sync", lambda e, th2=th2, cc2=cc2: e.dma_start(
                out=vtile[:],
                in_=kv_out[th2 * 2 * D + D + cc2 * 128: th2 * 2 * D + D + (cc2 + 1) * 128, :]),
                waits=bwv + [kv_cc], inc=ctx.dx, k=16)
            kvw = ctx.dx.at()
            p.add("vector", lambda e, cc2=cc2: e.scalar_tensor_tensor(
                out=S[1][:, 0:TH], in0=ktile[:], scalar=dcol[:, cc2:cc2 + 1],
                in1=S[0][:, 0:TH], op0=ALU.subtract, op1=ALU.subtract),
                waits=[kvw], inc=ctx.ev)
            p.add("vector", lambda e, cc2=cc2: e.tensor_scalar(
                out=S[4][:, 0:TH], in0=ktile[:], scalar1=ucol[:, cc2:cc2 + 1],
                scalar2=None, op0=ALU.add), waits=[kvw, w_u], inc=ctx.ev)
            p.add("vector", lambda e: e.memset(S[3][:, 0:TH], 0.0), inc=ctx.ev)
            p.add("vector", lambda e: e.tensor_copy(
                out=S[2][:, 0:1], in_=carry[:, 0:1]), inc=ctx.ev)
            p.add("vector", lambda e: e.tensor_tensor_scan(
                out=S[2][:, 1:TH + 1], data0=S[1][:, 0:TH], data1=S[3][:, 0:TH],
                initial=carry[:, 0:1], op0=ALU.max, op1=ALU.add),
                waits=[ctx.ev.at()], inc=ctx.ev)
            p.add("vector", lambda e: e.tensor_add(
                out=S[0][:, 0:TH], in0=S[0][:, 0:TH], in1=S[2][:, 0:TH]), inc=ctx.ev)
            p.add("vector", lambda e: e.tensor_sub(
                out=S[3][:, 0:TH], in0=S[2][:, 0:TH], in1=S[2][:, 1:TH + 1]), inc=ctx.ev)
            p.add("scalar", lambda e: e.activation(
                out=S[3][:, 0:TH], in_=S[3][:, 0:TH], func=AF.Exp),
                waits=[ctx.ev.at()], inc=ctx.es)
            p.add("vector", lambda e: e.tensor_sub(
                out=S[1][:, 0:TH], in0=S[1][:, 0:TH], in1=S[2][:, 1:TH + 1]), inc=ctx.ev)
            p.add("scalar", lambda e: e.activation(
                out=S[1][:, 0:TH], in_=S[1][:, 0:TH], func=AF.Exp),
                waits=[ctx.ev.at()], inc=ctx.es)
            ed_w = ctx.es.at()
            p.add("vector", lambda e: e.tensor_mul(
                out=S[5][:, 0:TH], in0=vtile[:], in1=S[1][:, 0:TH]),
                waits=[ed_w], inc=ctx.ev)
            p.add("vector", lambda e: e.tensor_copy(
                out=S[6][:, 0:1], in_=carry[:, 1:2]), inc=ctx.ev)
            p.add("vector", lambda e: e.tensor_tensor_scan(
                out=S[6][:, 1:TH + 1], data0=S[3][:, 0:TH], data1=S[5][:, 0:TH],
                initial=carry[:, 1:2], op0=ALU.mult, op1=ALU.add),
                waits=[ctx.ev.at()], inc=ctx.ev)
            p.add("vector", lambda e: e.tensor_copy(
                out=S[7][:, 0:1], in_=carry[:, 2:3]), inc=ctx.ev)
            p.add("vector", lambda e: e.tensor_tensor_scan(
                out=S[7][:, 1:TH + 1], data0=S[3][:, 0:TH], data1=S[1][:, 0:TH],
                initial=carry[:, 2:3], op0=ALU.mult, op1=ALU.add),
                waits=[ctx.ev.at()], inc=ctx.ev)
            if th2 == 0:
                p.add("vector", lambda e: e.tensor_copy(
                    out=carry[:, 0:1], in_=S[2][:, TH:TH + 1]), inc=ctx.ev)
                p.add("vector", lambda e: e.tensor_copy(
                    out=carry[:, 1:2], in_=S[6][:, TH:TH + 1]), inc=ctx.ev)
                p.add("vector", lambda e: e.tensor_copy(
                    out=carry[:, 2:3], in_=S[7][:, TH:TH + 1]), inc=ctx.ev)
            p.add("vector", lambda e: e.tensor_sub(
                out=S[2][:, 0:TH], in0=S[0][:, 0:TH], in1=S[4][:, 0:TH]), inc=ctx.ev)
            p.add("vector", lambda e: e.tensor_scalar(
                out=S[0][:, 0:TH], in0=S[2][:, 0:TH], scalar1=0.0, scalar2=None,
                op0=ALU.min), inc=ctx.ev)
            p.add("vector", lambda e: e.tensor_scalar(
                out=S[4][:, 0:TH], in0=S[2][:, 0:TH], scalar1=-1.0, scalar2=0.0,
                op0=ALU.mult, op1=ALU.min), inc=ctx.ev)
            p.add("scalar", lambda e: e.activation(
                out=S[0][:, 0:TH], in_=S[0][:, 0:TH], func=AF.Exp),
                waits=[ctx.ev.at()], inc=ctx.es)
            p.add("scalar", lambda e: e.activation(
                out=S[4][:, 0:TH], in_=S[4][:, 0:TH], func=AF.Exp), inc=ctx.es)
            e12_w = ctx.es.at()
            p.add("vector", lambda e: e.tensor_mul(
                out=S[2][:, 0:TH], in0=S[6][:, 0:TH], in1=S[0][:, 0:TH]),
                waits=[e12_w], inc=ctx.ev)
            p.add("vector", lambda e: e.tensor_mul(
                out=S[3][:, 0:TH], in0=vtile[:], in1=S[4][:, 0:TH]), inc=ctx.ev)
            p.add("vector", lambda e: e.tensor_add(
                out=S[2][:, 0:TH], in0=S[2][:, 0:TH], in1=S[3][:, 0:TH]), inc=ctx.ev)
            p.add("vector", lambda e: e.tensor_mul(
                out=S[3][:, 0:TH], in0=S[7][:, 0:TH], in1=S[0][:, 0:TH]), inc=ctx.ev)
            p.add("vector", lambda e: e.scalar_tensor_tensor(
                out=S[3][:, 0:TH], in0=S[3][:, 0:TH], scalar=1e-9, in1=S[4][:, 0:TH],
                op0=ALU.add, op1=ALU.add), inc=ctx.ev)
            p.add("vector", lambda e: e.reciprocal(
                out=S[3][:, 0:TH], in_=S[3][:, 0:TH]), inc=ctx.ev)
            p.add("vector", lambda e: e.tensor_mul(
                out=wkvbf[:], in0=S[2][:, 0:TH], in1=S[3][:, 0:TH]), inc=ctx.ev)
            p.add("sync", lambda e, cc2=cc2, tb=tb: e.dma_start(
                out=wkv_d[cc2 * 128:(cc2 + 1) * 128, tb:tb + TH], in_=wkvbf[:]),
                waits=[ctx.ev.at()], inc=ctx.dx, k=16)
    wkv_ship = ctx.dx.at()

    for g in reversed(guards2):
        g.__exit__(None, None, None)
    wkv_loc = ctx.stack.enter_context(
        ctx.nsb("wkv_loc", [128, DC * TL], BF16))
    h0 = ctx.stack.enter_context(
        ctx.nsb("wl0", [128, DC * TL], BF16))
    r_loc = ctx.stack.enter_context(
        ctx.nsb("r_loc", [128, DC * TL], BF16))
    a0_ = ap3(wkv_d, 0, [[T, 128], [128 * T, DC], [1, TL]])
    a1_ = ap3(wkv_d, TL, [[T, 128], [128 * T, DC], [1, TL]])
    p.add("sync", lambda e, a=a0_: e.dma_start(out=h0[:], in_=a),
          waits=[wkv_ship], inc=ctx.dx, k=16)
    p.add("sync", lambda e, a=a1_: e.dma_start(out=wkv_loc[:], in_=a),
          waits=[wkv_ship], inc=ctx.dx, k=16)
    p.add("sync", lambda e: e.dma_start(out=r_loc[:], in_=chunked(r_d)),
          inc=ctx.dx, k=16)
    wl_w = ctx.dx.at()
    for k in range(DC):
        p.add("vector", lambda e, k=k: e.tensor_sub(
            out=wkv_loc[:, k * TL:(k + 1) * TL], in0=wkv_loc[:, k * TL:(k + 1) * TL],
            in1=h0[:, k * TL:(k + 1) * TL]),
            waits=[wl_w] if k == 0 else (), inc=ctx.ev)
        p.add("vector", lambda e, k=k: e.scalar_tensor_tensor(
            out=wkv_loc[:, k * TL:(k + 1) * TL],
            in0=wkv_loc[:, k * TL:(k + 1) * TL], scalar=bmask[:, 0:1],
            in1=h0[:, k * TL:(k + 1) * TL], op0=ALU.mult, op1=ALU.add), inc=ctx.ev)
        p.add("vector", lambda e, k=k: e.tensor_mul(
            out=wkv_loc[:, k * TL:(k + 1) * TL],
            in0=wkv_loc[:, k * TL:(k + 1) * TL],
            in1=r_loc[:, k * TL:(k + 1) * TL]), inc=ctx.ev)
    rwkv_w = ctx.ev.at()
    w_wo = emit_weight_load(ctx, wbuf[:], chunked(ctx.dram["w1_wo"]),
                            waits=[ctx.mm.at()])
    oslot = [0]
    ofree = [None] * 8
    for oc in range(DC):
        for t0 in range(0, TL, 512):
            bank, bw, bi = ctx.pool.acquire()
            pairs = [(wbuf[:, k * D + oc * 128: k * D + (oc + 1) * 128],
                      wkv_loc[:, k * TL + t0: k * TL + t0 + 512]) for k in range(DC)]
            mmw = emit_matmul_chain(ctx, bank[:, 0:512], pairs,
                                    waits=[w_wo, rwkv_w] + bw)
            sl = oslot[0] % 8
            oslot[0] += 1
            sw = [mmw] + ([ofree[sl]] if ofree[sl] else [])
            p.add("vector", lambda e, b=bank, sl=sl: e.tensor_copy(
                out=stage[:, sl * 512:(sl + 1) * 512], in_=b[:, 0:512]),
                waits=sw, inc=ctx.ev)
            ctx.pool.release(bi, ctx.ev)
            evw = ctx.ev.at()
            p.add("gpsimd", lambda e, sl=sl, oc=oc, t0=t0: e.dma_start(
                out=XDST[:, oc * TL + t0: oc * TL + t0 + 512],
                in_=stage[:, sl * 512:(sl + 1) * 512], accum_op=ALU.add),
                waits=[evw, xo_init], inc=ctx.dw, k=16)
            ofree[sl] = ctx.dw.at()
    p.add("sync", None, waits=[ctx.dw.at()])


def _emit_w1(ctx):
    emit_l1_wkv(ctx)



def emit_l1_ffn(ctx):
    XSRC = ctx.dram[ctx.xsrc]
    XDST = ctx.dram[ctx.xdst]
    p = ctx.p
    nc = ctx.nc
    E = 16
    p.add("gpsimd", lambda e: e.dma_start(out=XDST, in_=XSRC),
          inc=ctx.dw, k=16)
    xo_init = ctx.dw.at()

    xg = ctx.nsb("x_sb", [128, DC * TL], F32, side="right")
    x = xg.__enter__()
    p.add("sync", lambda e: e.dma_start(out=x[:], in_=XSRC),
          inc=ctx.dx, k=16)
    xw = ctx.dx.at()
    ones, cw = emit_consts(ctx)
    xln = ctx.sb("xln", [128, DC * TL], BF16)
    ln_w = emit_ln(ctx, x, xln, TL, cw, "ln", x_wait=xw)
    xg.__exit__(None, None, None)

    gw_sb = ctx.sb("gw_sb", [128, DC * E], BF16)
    w_gw = emit_weight_load(ctx, gw_sb[:], chunked(ctx.dram["f1_gw"]))
    ebr = ctx.sb("ebr", [128, E], F32)
    w_eb = emit_weight_load(ctx, ebr[:], ctx.dram["f1_eb"])
    gtok = ctx.sb("gtok", [128, (TL // 128) * E], BF16)
    gA = ctx.sb("gA", [128, E], F32)
    gB = ctx.sb("gB", [128, E], F32)
    gS = ctx.sb("gS", [128, E], F32)
    gcol = ctx.sb("gcol", [128, 4], F32)
    idbf = ctx.sb("idbf", [128, 128], BF16)
    lnmu_t = ctx.t["lnmu"]
    p.add("gpsimd", lambda e: e.iota(
        lnmu_t[:, 0:128], [[1, 128]], base=0, channel_multiplier=-1,
        allow_small_or_imprecise_dtypes=True), waits=[ln_w], inc=ctx.gp)
    p.add("vector", lambda e: e.tensor_scalar(
        out=idbf[:], in0=lnmu_t[:, 0:128], scalar1=0.0, scalar2=None,
        op0=ALU.is_equal), waits=[ctx.gp.at()], inc=ctx.ev)

    for tc in range(TL // 128):
        bank, bw, bi = ctx.pool.acquire()
        pairs = [(xln[:, k * TL + tc * 128: k * TL + (tc + 1) * 128],
                  gw_sb[:, k * E:(k + 1) * E]) for k in range(DC)]
        mmw = emit_matmul_chain(ctx, bank[:, 0:E], pairs, waits=[w_gw, ln_w] + bw)

        def gser(eng, fn):
            p.add(eng, fn, waits=[ctx.ev.at(), ctx.es.at(), ctx.mm.at(), w_eb],
                  inc=(ctx.ev if eng == "vector" else ctx.es))
        gser("scalar", lambda e, b=bank: e.activation(
            out=gS[:], in_=b[:, 0:E], func=AF.Sigmoid))
        gser("vector", lambda e, b=bank: e.tensor_add(
            out=gB[:], in0=b[:, 0:E], in1=ebr[:]))
        gser("vector", lambda e: e.reduce_max(
            out=gcol[:, 0:1], in_=gB[:], axis=mybir.AxisListType.X))
        gser("vector", lambda e: e.tensor_scalar(
            out=gA[:], in0=gB[:], scalar1=gcol[:, 0:1], scalar2=None, op0=ALU.is_ge))
        gser("vector", lambda e: e.scalar_tensor_tensor(
            out=gB[:], in0=gA[:], scalar=-1e30, in1=gB[:], op0=ALU.mult, op1=ALU.add))
        gser("vector", lambda e: e.reduce_max(
            out=gcol[:, 1:2], in_=gB[:], axis=mybir.AxisListType.X))
        gser("vector", lambda e: e.tensor_scalar(
            out=gB[:], in0=gB[:], scalar1=gcol[:, 1:2], scalar2=None, op0=ALU.is_ge))
        gser("vector", lambda e: e.tensor_add(out=gA[:], in0=gA[:], in1=gB[:]))
        gser("vector", lambda e: e.tensor_mul(out=gA[:], in0=gA[:], in1=gS[:]))
        gser("vector", lambda e: e.reduce_sum(
            out=gcol[:, 2:3], in_=gA[:], axis=mybir.AxisListType.X))
        gser("vector", lambda e: e.tensor_scalar(
            out=gcol[:, 2:3], in0=gcol[:, 2:3], scalar1=1e-10, scalar2=None,
            op0=ALU.add))
        gser("vector", lambda e: e.reciprocal(out=gcol[:, 2:3], in_=gcol[:, 2:3]))
        gser("vector", lambda e, tc=tc: e.tensor_scalar(
            out=gtok[:, tc * E:(tc + 1) * E], in0=gA[:], scalar1=gcol[:, 2:3],
            scalar2=None, op0=ALU.mult))
        ctx.pool.release(bi, ctx.ev)
    gtok_w = ctx.ev.at()

    ptr_bf = ctx.stack.enter_context(nc.psum_tensor(ctx.tag + "ptr_bf", [128, 1024], BF16))
    gfm = ctx.sb("gfm", [128, TL], BF16)
    grow = ctx.sb("grow", [1, 1024], BF16)
    grow_user = [None]
    for tc in range(TL // 128):
        p.add("tensor", lambda e, tc=tc: e.transpose(
            ptr_bf[0:E, 0:128], gtok[:, tc * E:(tc + 1) * E], idbf[:]),
            waits=[gtok_w] if tc == 0 else [ctx.ev.at()], inc=ctx.mm)
        p.add("vector", lambda e, tc=tc: e.tensor_copy(
            out=gfm[0:E, tc * 128:(tc + 1) * 128], in_=ptr_bf[0:E, 0:128]),
            waits=[ctx.mm.at()], inc=ctx.ev)
    gfm_w = ctx.ev.at()

    b1c = ctx.sb("b1c", [128, E * 4], F32)
    b3c = ctx.sb("b3c", [128, E * 4], F32)
    b2sb = ctx.sb("b2sb", [128, D], BF16)
    sb1c = ctx.sb("sb1c", [128, 8], F32)
    sb3c = ctx.sb("sb3c", [128, 8], F32)
    sb2c = ctx.sb("sb2c", [128, 8], F32)
    w_b1 = emit_weight_load(ctx, b1c[:], ap3(ctx.dram["f1_b1"], 0,
                            [[4, 128], [512, E], [1, 4]]))
    w_b3 = emit_weight_load(ctx, b3c[:], ap3(ctx.dram["f1_b3"], 0,
                            [[4, 128], [512, E], [1, 4]]))
    w_b2 = emit_weight_load(ctx, b2sb[0:16, :], ctx.dram["f1_b2"])
    w_s1 = emit_weight_load(ctx, sb1c[:], ctx.dram["f1_sb1"])
    w_s3 = emit_weight_load(ctx, sb3c[:], ctx.dram["f1_sb3"])
    w_s2 = emit_weight_load(ctx, sb2c[:], ctx.dram["f1_sb2"])

    hid = ctx.sb("hid", [128, 8 * 1024], BF16)
    w1b = ctx.sb("w1b", [128, 2 * DC * 512], BF16)
    w3b = ctx.sb("w3b", [128, 2 * DC * 512], BF16)
    w2bu = ctx.sb("w2bu", [128, 2 * 4 * D], BF16)
    wS = ctx.sb("wS", [128, DC * D], BF16)
    grep = ctx.sb("grep", [128, 1024], F32)
    stile = ctx.sb("stile", [128, 2 * 512], BF16)
    stage = ctx.sb("stage", [128, 8 * 512], F32)
    oslot = [0]
    ofree = [None] * 8
    slot_user = [None, None]
    wS_user = [None]
    hid_user = [None]

    def run_f1_expert(e_idx, th, shared, add_b2=False):
        NDF = 8 if shared else 4
        grep_w = ctx.ev.at()
        if not shared:
            gw2 = [gfm_w] + ([grow_user[0]] if grow_user[0] else [])
            p.add("sync", lambda e, e_idx=e_idx, th=th: e.dma_start(
                out=grow[0:1, 0:1024],
                in_=gfm[e_idx:e_idx + 1, th * 1024:(th + 1) * 1024]),
                waits=gw2, inc=ctx.dx, k=16)
            growd = ctx.dx.at()
            for i in range(2):
                bank, bw, bi = ctx.pool.acquire()
                p.add("tensor", lambda e, b=bank, i=i: e.matmul(
                    b[:, 0:512], ones[0:1, 0:128], grow[0:1, i * 512:(i + 1) * 512],
                    start=True, stop=True), waits=[growd, cw] + bw, inc=ctx.mm)
                p.add("vector", lambda e, b=bank, i=i: e.tensor_copy(
                    out=grep[:, i * 512:(i + 1) * 512], in_=b[:, 0:512]),
                    waits=[ctx.mm.at()], inc=ctx.ev)
                ctx.pool.release(bi, ctx.ev)
            grow_user[0] = ctx.mm.at()
            grep_w = ctx.ev.at()
        if shared:
            lw = [wS_user[0]] if wS_user[0] else []
            wl1 = emit_weight_load(
                ctx, wS[:], ap3(ctx.dram["f1_sw1"], 0,
                                [[D, 128], [128 * D, DC], [1, D]]), waits=lw)
            w1q = wS[:]
            w3q = None
            w2q = None
        else:
            sl = e_idx % 2
            lw = [slot_user[sl]] if slot_user[sl] else []
            wl1 = emit_weight_load(
                ctx, w1b[:, sl * DC * 512:(sl + 1) * DC * 512],
                ap3(ctx.dram["f1_w1"], e_idx * D * 512,
                    [[512, 128], [128 * 512, DC], [1, 512]]), waits=lw)
            wl3 = emit_weight_load(
                ctx, w3b[:, sl * DC * 512:(sl + 1) * DC * 512],
                ap3(ctx.dram["f1_w3"], e_idx * D * 512,
                    [[512, 128], [128 * 512, DC], [1, 512]]), waits=lw)
            wl2 = emit_weight_load(
                ctx, w2bu[:, sl * 4 * D:(sl + 1) * 4 * D],
                ap3(ctx.dram["f1_w2"], e_idx * 512 * D,
                    [[D, 128], [128 * D, 4], [1, D]]), waits=lw)
            w1q = w1b[:, sl * DC * 512:(sl + 1) * DC * 512]
            w3q = w3b[:, sl * DC * 512:(sl + 1) * DC * 512]
            w2q = w2bu[:, sl * 4 * D:(sl + 1) * 4 * D]
        sfree = [None, None]
        for dc in range(NDF):
            for t5 in range(2):
                tog = t5
                xcol = th * 1024 + t5 * 512
                bank, bw, bi = ctx.pool.acquire()
                pairs = [(w1q[:, k * NDF * 128 + dc * 128: k * NDF * 128 + (dc + 1) * 128],
                          xln[:, k * TL + xcol: k * TL + xcol + 512])
                         for k in range(DC)]
                mmw = emit_matmul_chain(ctx, bank[:, 0:512], pairs,
                                        waits=[wl1, ln_w] + bw)
                bcol = (sb1c[:, dc:dc + 1] if shared
                        else b1c[:, e_idx * 4 + dc: e_idx * 4 + dc + 1])
                sw = [mmw, w_b1, w_s1] + ([sfree[tog]] if sfree[tog] else [])
                p.add("scalar", lambda e, b=bank, tog=tog, bc=bcol: e.activation(
                    out=stile[:, tog * 512:(tog + 1) * 512], in_=b[:, 0:512],
                    func=AF.Silu, bias=bc), waits=sw, inc=ctx.es)
                ctx.pool.release(bi, ctx.es)
                hcol = dc * 1024 + t5 * 512
                hw = [ctx.es.at()] + ([hid_user[0]] if hid_user[0] else [])
                if shared:
                    p.add("vector", lambda e, hcol=hcol, tog=tog: e.tensor_copy(
                        out=hid[:, hcol:hcol + 512],
                        in_=stile[:, tog * 512:(tog + 1) * 512]),
                        waits=hw, inc=ctx.ev)
                else:
                    p.add("vector", lambda e, hcol=hcol, tog=tog, t5=t5: e.tensor_mul(
                        out=hid[:, hcol:hcol + 512],
                        in0=stile[:, tog * 512:(tog + 1) * 512],
                        in1=grep[:, t5 * 512:(t5 + 1) * 512]),
                        waits=hw + [grep_w], inc=ctx.ev)
                sfree[tog] = ctx.ev.at()
        if shared:
            ph1 = ctx.mm.at()
            wl3 = emit_weight_load(
                ctx, wS[:], ap3(ctx.dram["f1_sw3"], 0,
                                [[D, 128], [128 * D, DC], [1, D]]), waits=[ph1])
            w3q = wS[:]
        for dc in range(NDF):
            for t5 in range(2):
                xcol = th * 1024 + t5 * 512
                hcol = dc * 1024 + t5 * 512
                bank, bw, bi = ctx.pool.acquire()
                pairs = [(w3q[:, k * NDF * 128 + dc * 128: k * NDF * 128 + (dc + 1) * 128],
                          xln[:, k * TL + xcol: k * TL + xcol + 512])
                         for k in range(DC)]
                mmw = emit_matmul_chain(ctx, bank[:, 0:512], pairs,
                                        waits=[wl3, ln_w] + bw)
                bcol = (sb3c[:, dc:dc + 1] if shared
                        else b3c[:, e_idx * 4 + dc: e_idx * 4 + dc + 1])
                p.add("vector", lambda e, b=bank, hcol=hcol, bc=bcol: e.scalar_tensor_tensor(
                    out=hid[:, hcol:hcol + 512], in0=b[:, 0:512], scalar=bc,
                    in1=hid[:, hcol:hcol + 512], op0=ALU.add, op1=ALU.mult),
                    waits=[mmw, w_b3, w_s3], inc=ctx.ev)
                ctx.pool.release(bi, ctx.ev)
        hid_w = ctx.ev.at()
        if shared:
            ph2 = ctx.mm.at()
            wl2 = emit_weight_load(
                ctx, wS[:], ap3(ctx.dram["f1_sw2"], 0,
                                [[D, 128], [128 * D, DC], [1, D]]), waits=[ph2])
            w2q = wS[:]
        for oc in range(DC):
            for t5 in range(2):
                tcol = th * 1024 + t5 * 512
                bank, bw, bi = ctx.pool.acquire()
                nk = 8 if shared else 4
                pairs = [(w2q[:, kc * D + oc * 128: kc * D + (oc + 1) * 128],
                          hid[:, kc * 1024 + t5 * 512: kc * 1024 + t5 * 512 + 512])
                         for kc in range(nk)]
                if not add_b2:
                    mmw = emit_matmul_chain(ctx, bank[:, 0:512], pairs,
                                            waits=[wl2, hid_w] + bw)
                else:
                    for i, (lh, rh) in enumerate(pairs):
                        w0 = [wl2, hid_w] + bw if i == 0 else ()
                        p.add("tensor",
                              lambda e, o=bank[:, 0:512], a=lh, b=rh, st=(i == 0): e.matmul(
                                  o, a, b, start=st, stop=False),
                              waits=w0)
                    p.add("tensor", lambda e, b=bank, oc=oc, tcol=tcol: e.matmul(
                        b[:, 0:512], b2sb[0:16, oc * 128:(oc + 1) * 128],
                        gfm[0:16, tcol:tcol + 512], start=False, stop=True),
                        waits=[w_b2, gfm_w], inc=ctx.mm)
                    mmw = ctx.mm.at()
                sl2 = oslot[0] % 8
                oslot[0] += 1
                sw = [mmw] + ([ofree[sl2]] if ofree[sl2] else [])
                if shared:
                    p.add("vector", lambda e, b=bank, sl2=sl2, oc=oc: e.tensor_scalar(
                        out=stage[:, sl2 * 512:(sl2 + 1) * 512], in0=b[:, 0:512],
                        scalar1=sb2c[:, oc:oc + 1], scalar2=None, op0=ALU.add),
                        waits=sw + [w_s2], inc=ctx.ev)
                else:
                    p.add("vector", lambda e, b=bank, sl2=sl2: e.tensor_copy(
                        out=stage[:, sl2 * 512:(sl2 + 1) * 512], in_=b[:, 0:512]),
                        waits=sw, inc=ctx.ev)
                ctx.pool.release(bi, ctx.ev)
                p.add("gpsimd", lambda e, sl2=sl2, oc=oc, tcol=tcol, XD=XDST: e.dma_start(
                    out=XD[:, oc * TL + tcol: oc * TL + tcol + 512],
                    in_=stage[:, sl2 * 512:(sl2 + 1) * 512], accum_op=ALU.add),
                    waits=[ctx.ev.at(), xo_init], inc=ctx.dw, k=16)
                ofree[sl2] = ctx.dw.at()
        if shared:
            wS_user[0] = ctx.mm.at()
        else:
            slot_user[e_idx % 2] = ctx.mm.at()
        hid_user[0] = ctx.mm.at()

    for th in range(2):
        nsh = [e for e in F1_EXPERTS if e != 99]
        for e_idx in F1_EXPERTS:
            if e_idx == 99:
                run_f1_expert(0, th, True)
            else:
                run_f1_expert(e_idx, th, False,
                              add_b2=(nsh and e_idx == nsh[-1]))
    p.add("sync", None, waits=[ctx.dw.at(), ctx.dx.at()])


def _emit_f1(ctx):
    emit_l1_ffn(ctx)



def emit_l1_attn_b(ctx):
    XSRC = ctx.dram[ctx.xsrc]
    XDST = ctx.dram[ctx.xdst]
    p = ctx.p
    nc = ctx.nc
    p.add("gpsimd", lambda e: e.dma_start(out=XDST, in_=XSRC),
          inc=ctx.dw, k=16)
    xo_init = ctx.dw.at()

    xg = ctx.nsb("x_sb", [128, DC * TL], F32, side="right")
    x = xg.__enter__()
    p.add("sync", lambda e: e.dma_start(out=x[:], in_=XSRC),
          inc=ctx.dx, k=16)
    xw = ctx.dx.at()
    ones, cw = emit_consts(ctx)
    xln = ctx.sb("xln", [128, DC * TL], BF16)
    ln_w = emit_ln(ctx, x, xln, TL, cw, "ln", x_wait=xw)
    xg.__exit__(None, None, None)

    nwc = ctx.sb("nwc", [128, 8], F32)
    nbc = ctx.sb("nbc", [128, 8], F32)
    bqr = ctx.sb("bqr", [128, D], BF16)
    bkr = ctx.sb("bkr", [128, D], F32)
    bvc = ctx.sb("bvc", [128, 8], F32)
    pbc = ctx.sb("pbc", [128, 8], F32)
    w_nw = emit_weight_load(ctx, nwc[:], ctx.dram["b1_nw"])
    w_nb = emit_weight_load(ctx, nbc[:], ctx.dram["b1_nb"])
    w_bq = emit_weight_load(ctx, bqr[:], ctx.dram["b1_bq"])
    w_bk = emit_weight_load(ctx, bkr[:], ctx.dram["b1_bk"])
    w_bv = emit_weight_load(ctx, bvc[:], ctx.dram["b1_bv"])
    w_pb = emit_weight_load(ctx, pbc[:], ctx.dram["b1_pb"])
    bkr16 = ctx.sb("bkr16", [128, D], BF16)
    p.add("vector", lambda e: e.tensor_scalar(
        out=bkr16[:], in0=bkr[:], scalar1=1.0 / 64.0, scalar2=None, op0=ALU.mult),
        waits=[w_bk], inc=ctx.ev)
    bk_w = ctx.ev.at()

    wbuf = ctx.sb("wbuf", [128, DC * D], BF16)
    stage = ctx.sb("stage", [128, 8 * 512], F32)
    stbf = ctx.sb("stbf", [128, 4 * 512], BF16)

    cc_in = ctx.ndt("cc_in", [D, D], F32)
    cc_out = ctx.ndt("cc_out", [D, D], F32)
    v_d = ctx.ndt("v_d", [D, TL], BF16)

    guards = []

    def salloc(name, shape, dtype):
        g = ctx.nsb(name, shape, dtype)
        t = g.__enter__()
        guards.append(g)
        return t

    h = ctx.sb("h_sb", [128, DC * TL], BF16)
    h_w = emit_ln(ctx, xln, h, TL, cw, "ln2", gain_col=nwc, bias_col=nbc,
                  x_wait=ln_w)

    q_tm = salloc("q_tm", [128, (TL // 128) * D], BF16)
    k_tm = salloc("k_tm", [128, (TL // 128) * D], BF16)

    def evac_q(bank, tc, o0, ow, waits):
        p.add("vector", lambda e, bk=bank, tc=tc, o0=o0, ow=ow: e.tensor_add(
            out=q_tm[:, tc * D + o0: tc * D + o0 + ow], in0=bk[:, 0:ow],
            in1=bqr[:, o0:o0 + ow]), waits=waits + [w_bq], inc=ctx.ev)
        return ctx.ev

    def evac_k(bank, tc, o0, ow, waits):
        p.add("vector", lambda e, bk=bank, tc=tc, o0=o0, ow=ow: e.scalar_tensor_tensor(
            out=k_tm[:, tc * D + o0: tc * D + o0 + ow], in0=bk[:, 0:ow],
            scalar=1.0 / 64.0, in1=bkr16[:, o0:o0 + ow],
            op0=ALU.mult, op1=ALU.add), waits=waits + [bk_w], inc=ctx.ev)
        return ctx.ev

    w_q = emit_weight_load(ctx, wbuf[:], ap3(ctx.dram["b1_qkv"], 0,
                           [[3 * D, 128], [128 * 3 * D, DC], [1, D]]))
    emit_proj_tm(ctx, wbuf, w_q, h, h_w, TL, D, evac_q)
    w_k = emit_weight_load(ctx, wbuf[:], ap3(ctx.dram["b1_qkv"], D,
                           [[3 * D, 128], [128 * 3 * D, DC], [1, D]]),
                           waits=[ctx.mm.at()])
    emit_proj_tm(ctx, wbuf, w_k, h, h_w, TL, D, evac_k)
    w_v = emit_weight_load(ctx, wbuf[:], ap3(ctx.dram["b1_qkv"], 2 * D,
                           [[3 * D, 128], [128 * 3 * D, DC], [1, D]]),
                           waits=[ctx.mm.at()])
    vslot = [0]
    vfree = [None] * 4

    def evac_v(bank, oc, t0, tw, waits):
        sl = vslot[0] % 4
        vslot[0] += 1
        sw = waits + [w_bv] + ([vfree[sl]] if vfree[sl] else [])
        p.add("vector", lambda e, bk=bank, sl=sl, oc=oc, tw=tw: e.tensor_scalar(
            out=stbf[:, sl * 512: sl * 512 + tw], in0=bk[:, 0:tw],
            scalar1=bvc[:, oc:oc + 1], scalar2=None, op0=ALU.add),
            waits=sw, inc=ctx.ev)
        evw = ctx.ev.at()
        p.add("sync", lambda e, sl=sl, oc=oc, t0=t0, tw=tw: e.dma_start(
            out=v_d[oc * 128:(oc + 1) * 128, t0:t0 + tw],
            in_=stbf[:, sl * 512: sl * 512 + tw]), waits=[evw], inc=ctx.dx, k=16)
        vfree[sl] = ctx.dx.at()
        return ctx.ev

    emit_proj_fm(ctx, wbuf, w_v, h, h_w, TL, D, evac_v)
    v_ship = ctx.dx.at()


    qk_w = ctx.ev.at()
    sslot = [0]
    sfree = [None] * 4
    for kdc in range(DC):
        for qt in range(2):
            bank, bw, bi = ctx.pool.acquire()
            pairs = [(k_tm[:, c * D + kdc * 128: c * D + (kdc + 1) * 128],
                      q_tm[:, c * D + qt * 512: c * D + qt * 512 + 512])
                     for c in range(TL // 128)]
            mmw = emit_matmul_chain(ctx, bank[:, 0:512], pairs, waits=[qk_w] + bw)
            sl = sslot[0] % 4
            sslot[0] += 1
            sw = [mmw] + ([sfree[sl]] if sfree[sl] else [])
            p.add("vector", lambda e, b=bank, sl=sl: e.tensor_copy(
                out=stage[:, sl * 512:(sl + 1) * 512], in_=b[:, 0:512]),
                waits=sw, inc=ctx.ev)
            ctx.pool.release(bi, ctx.ev)
            evw = ctx.ev.at()
            p.add("sync", lambda e, sl=sl, kdc=kdc, qt=qt: e.dma_start(
                out=cc_in[kdc * 128:(kdc + 1) * 128, qt * 512:(qt + 1) * 512],
                in_=stage[:, sl * 512:(sl + 1) * 512]), waits=[evw], inc=ctx.dx, k=16)
            sfree[sl] = ctx.dx.at()
    p.add("gpsimd", lambda e: e.collective_compute(
        "AllReduce", ALU.add, replica_groups=PAIRS,
        ins=[cc_in.opt()], outs=[cc_out.opt()]),
        waits=[ctx.dx.at()], inc=ctx.cc)
    s_cc = ctx.cc.at()

    guards[1].__exit__(None, None, None)
    guards[0].__exit__(None, None, None)

    PT = ctx.stack.enter_context(ctx.nsb("PT", [128, DC * 1024], BF16))
    a_fm = ctx.stack.enter_context(ctx.nsb("a_fm", [128, DC * TL], BF16))
    Dc = ctx.stack.enter_context(ctx.nsb("Dc", [128, 8], F32))
    vt = ctx.stack.enter_context(ctx.nsb("vt", [128, DC * 512], BF16))
    slots = ctx.t["lnmu"]
    for kdc in range(DC):
        sl = (kdc % 2) * 1024
        p.add("sync", lambda e, kdc=kdc, sl=sl: e.dma_start(
            out=slots[:, sl:sl + 1024], in_=cc_out[kdc * 128:(kdc + 1) * 128, :]),
            waits=[s_cc] + ([ctx.es.at()] if kdc >= 2 else []), inc=ctx.dx, k=16)
        p.add("scalar", lambda e, kdc=kdc, sl=sl: e.activation(
            out=PT[:, kdc * 1024:(kdc + 1) * 1024], in_=slots[:, sl:sl + 1024],
            func=AF.Exp), waits=[ctx.dx.at()], inc=ctx.es)
    PT_w = ctx.es.at()
    for qdc in range(DC):
        bank, bw, bi = ctx.pool.acquire()
        pairs = [(PT[:, kdc * 1024 + qdc * 128: kdc * 1024 + (qdc + 1) * 128],
                  ones[:, 0:1]) for kdc in range(DC)]
        mmw = emit_matmul_chain(ctx, bank[:, 0:1], pairs, waits=[PT_w, cw] + bw)
        p.add("vector", lambda e, b=bank, qdc=qdc: e.tensor_copy(
            out=Dc[:, qdc:qdc + 1], in_=b[:, 0:1]), waits=[mmw], inc=ctx.ev)
        ctx.pool.release(bi, ctx.ev)
    p.add("vector", lambda e: e.reciprocal(out=Dc[:], in_=Dc[:]),
          waits=[ctx.ev.at()], inc=ctx.ev)
    Dinv_w = ctx.ev.at()
    vt_user = [None]
    for ct in range(4):
        vw = [v_ship] + ([vt_user[0]] if vt_user[0] else [])
        p.add("sync", lambda e, ct=ct: e.dma_start(
            out=vt[:], in_=ap3(v_d, ct * 512, [[TL, 128], [128 * TL, DC], [1, 512]])),
            waits=vw, inc=ctx.dx, k=16)
        vt_w = ctx.dx.at()
        for qic in range(DC):
            bank, bw, bi = ctx.pool.acquire()
            pairs = [(PT[:, kdc * 1024 + qic * 128: kdc * 1024 + (qic + 1) * 128],
                      vt[:, kdc * 512:(kdc + 1) * 512]) for kdc in range(DC)]
            mmw = emit_matmul_chain(ctx, bank[:, 0:512], pairs,
                                    waits=[PT_w, vt_w] + bw)
            p.add("vector", lambda e, b=bank, qic=qic, ct=ct: e.tensor_scalar(
                out=a_fm[:, qic * TL + ct * 512: qic * TL + (ct + 1) * 512],
                in0=b[:, 0:512], scalar1=Dc[:, qic:qic + 1], scalar2=None,
                op0=ALU.mult), waits=[mmw, Dinv_w], inc=ctx.ev)
            ctx.pool.release(bi, ctx.ev)
        vt_user[0] = ctx.mm.at()
    a_w = ctx.ev.at()
    w_p = emit_weight_load(ctx, wbuf[:], chunked(ctx.dram["b1_proj"]),
                           waits=[ctx.mm.at()])
    oslot = [0]
    ofree = [None] * 8
    for oc in range(DC):
        for t0 in range(0, TL, 512):
            bank, bw, bi = ctx.pool.acquire()
            pairs = [(wbuf[:, k * D + oc * 128: k * D + (oc + 1) * 128],
                      a_fm[:, k * TL + t0: k * TL + t0 + 512]) for k in range(DC)]
            mmw = emit_matmul_chain(ctx, bank[:, 0:512], pairs,
                                    waits=[w_p, a_w] + bw)
            sl = oslot[0] % 8
            oslot[0] += 1
            sw = [mmw, w_pb, ln_w] + ([ofree[sl]] if ofree[sl] else [])
            p.add("vector", lambda e, b=bank, sl=sl, oc=oc, t0=t0: e.scalar_tensor_tensor(
                out=stage[:, sl * 512:(sl + 1) * 512], in0=b[:, 0:512],
                scalar=pbc[:, oc:oc + 1], in1=xln[:, oc * TL + t0: oc * TL + t0 + 512],
                op0=ALU.add, op1=ALU.add), waits=sw, inc=ctx.ev)
            ctx.pool.release(bi, ctx.ev)
            evw = ctx.ev.at()
            p.add("gpsimd", lambda e, sl=sl, oc=oc, t0=t0: e.dma_start(
                out=XDST[:, oc * TL + t0: oc * TL + t0 + 512],
                in_=stage[:, sl * 512:(sl + 1) * 512], accum_op=ALU.add),
                waits=[evw, xo_init], inc=ctx.dw, k=16)
            ofree[sl] = ctx.dw.at()
    p.add("sync", None, waits=[ctx.dw.at()])


def _emit_b1(ctx):
    emit_l1_attn_b(ctx)


SUBLAYER_WEIGHTS = {
    "a0": ["a0_wq", "a0_bq", "a0_wk", "a0_bk", "a0_wv", "a0_bv", "a0_gamma"],
    "f0": ["f0_gw", "f0_wg", "f0_wu", "f0_wd", "f0_sg", "f0_su", "f0_sd"],
    "b0": ["b0_wq", "b0_wk", "b0_wv", "b0_wo", "b0_bq", "b0_bk", "b0_bv", "b0_bo"],
    "w1": ["w1_decay", "w1_first", "w1_mk", "w1_mv", "w1_mr", "w1_bmask",
           "w1_wk", "w1_wv", "w1_wr", "w1_wo"],
    "f1": ["f1_gw", "f1_eb", "f1_w1", "f1_w3", "f1_w2", "f1_b1", "f1_b3", "f1_b2",
           "f1_sw1", "f1_sw3", "f1_sw2", "f1_sb1", "f1_sb3", "f1_sb2"],
    "b1": ["b1_nw", "b1_nb", "b1_qkv", "b1_bq", "b1_bk", "b1_bv", "b1_proj", "b1_pb"],
}

SUBLAYER_EMIT = {}


def _emit_a0(ctx):
    x, xw = emit_load_x(ctx)
    done = emit_l0_attn_a(ctx, x, xw)
    emit_store_x(ctx, x, [done])


SUBLAYER_EMIT["a0"] = _emit_a0
SUBLAYER_EMIT["f0"] = _emit_f0
SUBLAYER_EMIT["b0"] = _emit_b0
SUBLAYER_EMIT["w1"] = _emit_w1
SUBLAYER_EMIT["f1"] = _emit_f1
SUBLAYER_EMIT["b1"] = _emit_b1


def build_sublayer(name):
    nc = bass.Bass("TRN2", target_bir_lowering=False)
    stack = ExitStack()
    with stack:
        ctx = setup_common(nc, stack, SUBLAYER_WEIGHTS[name])
        ctx.mkpool(7)
        SUBLAYER_EMIT[name](ctx)
        with nc.Block() as block:
            ctx.p.emit(block)
    return nc


def run_sublayer(name, X, w):
    nc = build_sublayer(name)
    in_maps = []
    for c in range(NCORES):
        m = {"x": X[c]}
        for wn in SUBLAYER_WEIGHTS[name]:
            val = w[wn]
            m[wn] = val[c] if isinstance(val, list) else val
        in_maps.append(m)
    res = run_bass_kernel_spmd(nc, in_maps, core_ids=list(range(NCORES)))
    if F0_DEBUG:
        global DBG_OUT
        DBG_OUT = [np.asarray(res.results[c].get("dbg")) for c in range(NCORES)]
        global DBG_Y
        DBG_Y = [np.asarray(res.results[c].get("dbgy")) for c in range(NCORES)]
        global DBG_G
        DBG_G = [np.asarray(res.results[c].get("dbgg")) for c in range(NCORES)]
        global DBG_L
        DBG_L = [np.asarray(res.results[c].get("dbgl")) for c in range(NCORES)]
    return [np.asarray(res.results[c]["xo"]) for c in range(NCORES)]


SUBLAYER_ORDER = ["a0", "f0", "b0", "f0", "w1", "f1", "b1", "f1"]
FUSED = True
FUSED_K = len(SUBLAYER_ORDER)
FUSED_TAP = None
ALL_WEIGHTS = sorted({w for ws in SUBLAYER_WEIGHTS.values() for w in ws})


def emit_sub_barrier(ctx, i):
    prog = ctx.p
    s = prog.sem(f"bar{i}")
    for e in ENGS:
        prog.add(e, lambda en: en.nop(), inc=s)
    tgt = s.n
    for e in ENGS:
        prog.add(e, None, waits=[(s, tgt)])


def build_fused():
    nc = bass.Bass("TRN2", target_bir_lowering=False)
    stack = ExitStack()
    with stack:
        prog = Prog(nc, stack)
        ctx = Ctx(nc, prog, stack)
        n = FUSED_K
        ctx.dram["x0"] = nc.dram_tensor("x", [128, DC * TL], F32,
                                        kind="ExternalInput").ap()
        ctx.dram[f"x{n}"] = nc.dram_tensor("xo", [128, DC * TL], F32,
                                           kind="ExternalOutput").ap()
        for i in range(1, n):
            ctx.dram[f"x{i}"] = nc.dram_tensor(f"xb{i}", [128, DC * TL], F32).ap()
        fw = sorted({w for nm in SUBLAYER_ORDER[:n] for w in SUBLAYER_WEIGHTS[nm]})
        for name in fw:
            ctx.dram[name] = nc.dram_tensor(
                name, list(WEIGHT_SHAPES[name]), F32, kind="ExternalInput").ap()
        ctx.mkpool(7)
        if FUSED_TAP is not None:
            ctx.dram["tap"] = nc.dram_tensor("tap", [128, DC * TL], F32,
                                             kind="ExternalOutput").ap()
        outer = stack
        for i, nm in enumerate(SUBLAYER_ORDER[:n]):
            ctx.tag = f"s{i}_"
            ctx.xsrc = f"x{i}"
            ctx.xdst = f"x{i + 1}"
            ctx.t = {}
            sub = ExitStack()
            ctx.stack = sub
            with sub:
                SUBLAYER_EMIT[nm](ctx)
            ctx.stack = outer
            emit_sub_barrier(ctx, i)
            if FUSED_TAP is not None and i == FUSED_TAP:
                prog.add("gpsimd", lambda e, i=i: e.dma_start(
                    out=ctx.dram["tap"], in_=ctx.dram[f"x{i + 1}"]),
                    inc=ctx.dw, k=16)
                emit_sub_barrier(ctx, 100 + i)
        with nc.Block() as block:
            prog.emit(block)
    return nc


def kernel_fused(X, w):
    nc = build_fused()
    fw = sorted({wn for nm in SUBLAYER_ORDER[:FUSED_K] for wn in SUBLAYER_WEIGHTS[nm]})
    in_maps = []
    for c in range(NCORES):
        m = {"x": X[c]}
        for wn in fw:
            val = w[wn]
            m[wn] = val[c] if isinstance(val, list) else val
        in_maps.append(m)
    res = run_bass_kernel_spmd(nc, in_maps, core_ids=list(range(NCORES)))
    return [np.asarray(res.results[c]["xo"]) for c in range(NCORES)]


def kernel(**inputs):
    w = prep_weights(inputs)
    x0 = np.asarray(inputs["L0_attn_a_x"])
    X = []
    for c in range(NCORES):
        b, s = c // 2, c % 2
        X.append(_fm(x0[b, s * TL:(s + 1) * TL, :]))
    if FUSED:
        X = kernel_fused(X, w)
        for name in SUBLAYER_ORDER[FUSED_K:]:
            X = run_sublayer(name, X, w)
    else:
        for name in SUBLAYER_ORDER:
            X = run_sublayer(name, X, w)
    out = np.zeros((B, T, D), np.float32)
    for c in range(NCORES):
        b, s = c // 2, c % 2
        out[b, s * TL:(s + 1) * TL, :] = _unfm(X[c])
    return out


# version 08
# speedup vs baseline: 1.0475x; 1.0475x over previous
import numpy as np
import ml_dtypes
import concourse.bass as bass
import concourse.mybir as mybir
from concourse.bass_utils import run_bass_kernel_spmd
from contextlib import ExitStack

F32 = mybir.dt.float32
BF16 = mybir.dt.bfloat16
FP16 = mybir.dt.float16
AF = mybir.ActivationFunctionType
ALU = mybir.AluOpType

NCORES = 8
B = 4
T = 4096
TL = 2048
D = 1024
DC = D // 128
PAIRS = [[0, 1], [2, 3], [4, 5], [6, 7]]
NEG = -1e38
F0_EXPERTS = list(range(9))
F0_DEBUG = False
F1_EXPERTS = list(range(16)) + [99]

ENGS = ("tensor", "vector", "scalar", "gpsimd", "sync")


class Sem:
    __slots__ = ("h", "n", "name")

    def __init__(self, h, name):
        self.h = h
        self.n = 0
        self.name = name

    def at(self):
        return (self, self.n)


class Prog:
    def __init__(self, nc, stack):
        self.nc = nc
        self.stack = stack
        self.ops = {e: [] for e in ENGS}
        self._waited = {}

    def sem(self, name):
        h = self.stack.enter_context(self.nc.semaphore(name))
        return Sem(h, name)

    def add(self, eng, fn, waits=(), inc=None, k=1):
        w = []
        for s, v in waits:
            if v <= 0:
                continue
            key = (eng, s.name)
            if self._waited.get(key, 0) >= v:
                continue
            self._waited[key] = v
            w.append((s.h, v))
        self.ops[eng].append((fn, w, inc.h if inc is not None else None, k))
        if inc is not None:
            inc.n += k

    def emit(self, block):
        for engname in ENGS:
            ops = self.ops[engname]

            def mk(myops):
                def f(e):
                    for fn, waits, inch, k in myops:
                        for h, v in waits:
                            e.wait_ge(h, v)
                        if fn is None:
                            if inch is not None:
                                e.nop().then_inc(inch, k)
                            continue
                        inst = fn(e)
                        if inch is not None:
                            inst.then_inc(inch, k)
                return f

            getattr(block, engname)(mk(ops))


class PsumPool:

    def __init__(self, banks):
        self.banks = banks
        self.free_at = [None] * len(banks)
        self.held = [False] * len(banks)
        self.idx = 0

    def acquire(self):
        for _ in range(len(self.banks)):
            i = self.idx % len(self.banks)
            self.idx += 1
            if not self.held[i]:
                self.held[i] = True
                w = self.free_at[i]
                self.free_at[i] = None
                return self.banks[i], ([w] if w else []), i
        raise RuntimeError("all psum banks held")

    def release(self, i, sem):
        self.held[i] = False
        self.free_at[i] = (sem, sem.n)


def ceil_div(a, b):
    return (a + b - 1) // b


class Ctx:
    def __init__(self, nc, prog, stack):
        self.nc = nc
        self.p = prog
        self.stack = stack
        self.t = {}
        self.dram = {}
        self.mm = prog.sem("mm")
        self.ev = prog.sem("ev")
        self.es = prog.sem("es")
        self.dw = prog.sem("dw")
        self.dx = prog.sem("dx")
        self.gp = prog.sem("gp")
        self.cc = prog.sem("cc")
        self.pool = None
        self.tag = ""
        self.xsrc = "x"
        self.xdst = "xo"

    def sb(self, name, shape, dtype):
        t = self.stack.enter_context(
            self.nc.sbuf_tensor(self.tag + name, shape, dtype))
        self.t[name] = t
        return t

    def ndt(self, name, shape, dtype):
        return self.nc.dram_tensor(self.tag + name, list(shape), dtype).ap()

    def nsb(self, name, shape, dtype, side=None):
        return self.nc.sbuf_tensor(self.tag + name, shape, dtype, side=side)

    def mkpool(self, n=6):
        banks = [self.stack.enter_context(
            self.nc.psum_tensor(f"pb{i}", [128, 512], F32)) for i in range(n)]
        self.pool = PsumPool(banks)
        return banks


def emit_weight_load(ctx, dst, src_ap, eng="gpsimd", waits=()):
    s = ctx.dw if eng == "gpsimd" else ctx.dx
    ctx.p.add(eng, lambda e, d=dst, a=src_ap: e.dma_start(out=d, in_=a),
              waits=waits, inc=s, k=16)
    return s.at()


def chunked(src_ap, p=128):
    rows, O = src_ap.shape
    C = rows // p
    return bass.AP(src_ap.tensor, 0, [[O, p], [p * O, C], [1, O]])


def emit_matmul_chain(ctx, psum_ap, pairs, waits=()):
    p = ctx.p
    n = len(pairs)
    for i, (lh, rh) in enumerate(pairs):
        w = waits if i == 0 else ()
        p.add("tensor",
              lambda e, o=psum_ap, a=lh, b=rh, st=(i == 0), sp=(i == n - 1):
              e.matmul(o, a, b, start=st, stop=sp),
              waits=w, inc=(ctx.mm if i == n - 1 else None))
    return ctx.mm.at()



def emit_ln(ctx, x, xln, TT, consts_w, tag, gain_col=None, bias_col=None,
            x_wait=None):
    p = ctx.p
    ones = ctx.t["onesbf"]
    sq = ctx.t["lnsq"]
    mu = ctx.t["lnmu"]
    rs = ctx.t["lnrs"]
    epsc = ctx.t["epscol"]
    waits = ([x_wait] if x_wait else []) + [consts_w]
    for k in range(DC):
        p.add("vector", lambda e, k=k: e.tensor_copy(
            out=xln[:, k * TT:(k + 1) * TT], in_=x[:, k * TT:(k + 1) * TT]),
            waits=waits if k == 0 else (), inc=ctx.ev)
    cast_done = ctx.ev.at()
    prev_mm = None
    for it in range(ceil_div(TT, 512)):
        t0 = it * 512
        tw = min(512, TT - t0)
        b1, w1, i1 = ctx.pool.acquire()
        b2, w2, i2 = ctx.pool.acquire()
        for k in range(DC):
            sqw = [cast_done]
            if k == 0 and prev_mm:
                sqw.append(prev_mm)
            p.add("scalar", lambda e, k=k, t0=t0, tw=tw: e.activation(
                out=sq[:, k * 512:k * 512 + tw],
                in_=xln[:, k * TT + t0:k * TT + t0 + tw], func=AF.Square),
                waits=sqw, inc=ctx.es)
            sq_ok = ctx.es.at()
            p.add("tensor", lambda e, k=k, t0=t0, tw=tw, b=b1, st=(k == 0), sp=(k == DC - 1):
                  e.matmul(b[:, 0:tw], ones[:], xln[:, k * TT + t0:k * TT + t0 + tw],
                           start=st, stop=sp),
                  waits=([cast_done] + w1 if k == 0 else []),
                  inc=(ctx.mm if k == DC - 1 else None))
            p.add("tensor", lambda e, k=k, tw=tw, b=b2, st=(k == 0), sp=(k == DC - 1):
                  e.matmul(b[:, 0:tw], ones[:], sq[:, k * 512:k * 512 + tw],
                           start=st, stop=sp),
                  waits=([sq_ok] + w2 if k == 0 else [sq_ok]),
                  inc=(ctx.mm if k == DC - 1 else None))
        mm_done = ctx.mm.at()
        prev_mm = mm_done
        p.add("vector", lambda e, t0=t0, tw=tw, b=b1: e.tensor_scalar(
            out=mu[:, t0:t0 + tw], in0=b[:, 0:tw], scalar1=1.0 / D, scalar2=None,
            op0=ALU.mult), waits=[mm_done], inc=ctx.ev)
        ctx.pool.release(i1, ctx.ev)
        p.add("scalar", lambda e, t0=t0, tw=tw: e.activation(
            out=rs[:, t0:t0 + tw], in_=mu[:, t0:t0 + tw], func=AF.Square),
            waits=[ctx.ev.at()], inc=ctx.es)
        p.add("vector", lambda e, t0=t0, tw=tw, b=b2: e.scalar_tensor_tensor(
            out=rs[:, t0:t0 + tw], in0=b[:, 0:tw], scalar=1.0 / D,
            in1=rs[:, t0:t0 + tw], op0=ALU.mult, op1=ALU.subtract),
            waits=[ctx.es.at()], inc=ctx.ev)
        ctx.pool.release(i2, ctx.ev)
        p.add("scalar", lambda e, t0=t0, tw=tw: e.activation(
            out=rs[:, t0:t0 + tw], in_=rs[:, t0:t0 + tw], func=AF.Ln,
            bias=epsc[:, 0:1]),
            waits=[ctx.ev.at()], inc=ctx.es)
        p.add("scalar", lambda e, t0=t0, tw=tw: e.activation(
            out=rs[:, t0:t0 + tw], in_=rs[:, t0:t0 + tw], func=AF.Exp,
            scale=-0.5), inc=ctx.es)
    stats_done = ctx.es.at()
    for k in range(DC):
        p.add("vector", lambda e, k=k: e.tensor_sub(
            out=xln[:, k * TT:(k + 1) * TT], in0=x[:, k * TT:(k + 1) * TT],
            in1=mu[:, 0:TT]),
            waits=[stats_done] if k == 0 else (), inc=ctx.ev)
        p.add("vector", lambda e, k=k: e.tensor_mul(
            out=xln[:, k * TT:(k + 1) * TT], in0=xln[:, k * TT:(k + 1) * TT],
            in1=rs[:, 0:TT]), inc=ctx.ev)
        if gain_col is not None:
            p.add("vector", lambda e, k=k: e.tensor_scalar(
                out=xln[:, k * TT:(k + 1) * TT], in0=xln[:, k * TT:(k + 1) * TT],
                scalar1=gain_col[:, k:k + 1], scalar2=bias_col[:, k:k + 1],
                op0=ALU.mult, op1=ALU.add), inc=ctx.ev)
    return ctx.ev.at()



def emit_proj_fm(ctx, w_sb, w_wait, xln, xln_wait, TT, outdim, evac,
                 outw_cols=None, nmax=512, kchunks=DC):
    oc_n = outdim // 128
    outw = outw_cols if outw_cols is not None else outdim
    for oc in range(oc_n):
        for t0 in range(0, TT, nmax):
            tw = min(nmax, TT - t0)
            bank, bw, bi = ctx.pool.acquire()
            pairs = []
            for k in range(kchunks):
                lh = w_sb[:, k * outw + oc * 128: k * outw + oc * 128 + 128]
                rh = xln[:, k * TT + t0: k * TT + t0 + tw]
                pairs.append((lh, rh))
            mmw = emit_matmul_chain(ctx, bank[:, 0:tw], pairs,
                                    waits=[w_wait, xln_wait] + bw)
            sem = evac(bank, oc, t0, tw, [mmw])
            ctx.pool.release(bi, sem)


def emit_proj_tm(ctx, w_sb, w_wait, xln, xln_wait, TT, outdim, evac,
                 outw_cols=None, nmax=512):
    outw = outw_cols if outw_cols is not None else outdim
    for tc in range(TT // 128):
        for o0 in range(0, outdim, nmax):
            ow = min(nmax, outdim - o0)
            bank, bw, bi = ctx.pool.acquire()
            pairs = []
            for k in range(DC):
                lh = xln[:, k * TT + tc * 128: k * TT + tc * 128 + 128]
                rh = w_sb[:, k * outw + o0: k * outw + o0 + ow]
                pairs.append((lh, rh))
            mmw = emit_matmul_chain(ctx, bank[:, 0:ow], pairs,
                                    waits=[w_wait, xln_wait] + bw)
            sem = evac(bank, tc, o0, ow, [mmw])
            ctx.pool.release(bi, sem)



def _T(a):
    return np.ascontiguousarray(np.asarray(a).T)


def _colmaj(v):
    v = np.asarray(v).reshape(-1)
    return np.ascontiguousarray(v.reshape(-1, 128).T)


def _rep(v, n=128):
    v = np.asarray(v).reshape(-1)
    return np.ascontiguousarray(np.broadcast_to(v[None, :], (n, v.shape[0])))


def _fm(x2d):
    tt = x2d.shape[0]
    return np.ascontiguousarray(
        x2d.T.reshape(DC, 128, tt).transpose(1, 0, 2).reshape(128, DC * tt))


def _unfm(dev):
    tt = dev.shape[1] // DC
    return np.ascontiguousarray(
        dev.reshape(128, DC, tt).transpose(1, 0, 2).reshape(D, tt).T)


def prep_weights(inp):
    w = {}
    wq = np.asarray(inp["L0_attn_a_Wq"])
    wk = np.asarray(inp["L0_attn_a_Wk"])
    wqp = np.zeros((D, 384), np.float32)
    wkp = np.zeros((D, 384), np.float32)
    bqp = np.zeros((384,), np.float32)
    bkp = np.zeros((384,), np.float32)
    for h in range(8):
        off = 128 * (h // 3) + 32 * (h % 3)
        wqp[:, off:off + 16] = wq[16 * h:16 * h + 16, :].T
        wkp[:, off:off + 16] = wk[16 * h:16 * h + 16, :].T
        bqp[off:off + 16] = np.asarray(inp["L0_attn_a_bq"])[16 * h:16 * h + 16]
        bkp[off:off + 16] = np.asarray(inp["L0_attn_a_bk"])[16 * h:16 * h + 16]
    w["a0_wq"] = wqp
    w["a0_wk"] = wkp
    w["a0_bq"] = _colmaj(bqp)
    w["a0_bk"] = _colmaj(bkp)
    w["a0_wv"] = _T(inp["L0_attn_a_Wv"])
    w["a0_bv"] = _rep(inp["L0_attn_a_bv"])
    w["a0_gamma"] = np.ascontiguousarray(
        np.broadcast_to(np.asarray(inp["L0_attn_a_gamma"]).reshape(1, 1), (128, 1)))
    w["f0_gw"] = np.asarray(inp["L0_ffn_gate_w"])
    w["f0_wg"] = np.asarray(inp["L0_ffn_w_gate"])
    w["f0_wu"] = np.asarray(inp["L0_ffn_w_up"])
    w["f0_wd"] = np.asarray(inp["L0_ffn_w_down"])
    w["f0_sg"] = np.asarray(inp["L0_ffn_ws_gate"])
    w["f0_su"] = np.asarray(inp["L0_ffn_ws_up"])
    w["f0_sd"] = np.asarray(inp["L0_ffn_ws_down"])
    w["b0_wq"] = _T(inp["L0_attn_b_wq"])
    w["b0_wk"] = _T(inp["L0_attn_b_wk"])
    w["b0_wv"] = _T(inp["L0_attn_b_wv"])
    w["b0_wo"] = _T(inp["L0_attn_b_wo"])
    w["b0_bq"] = _colmaj(inp["L0_attn_b_bq"])
    w["b0_bk"] = _colmaj(inp["L0_attn_b_bk"])
    w["b0_bv"] = _rep(inp["L0_attn_b_bv"])
    w["b0_bo"] = _colmaj(inp["L0_attn_b_bo"])
    w["w1_decay"] = _colmaj(inp["L1_attn_a_time_decay"])
    w["w1_first"] = _colmaj(inp["L1_attn_a_time_first"])
    w["w1_mk"] = _colmaj(np.asarray(inp["L1_attn_a_time_mix_k"]).reshape(-1))
    w["w1_mv"] = _colmaj(np.asarray(inp["L1_attn_a_time_mix_v"]).reshape(-1))
    w["w1_mr"] = _colmaj(np.asarray(inp["L1_attn_a_time_mix_r"]).reshape(-1))
    w["w1_bmask"] = [np.full((128, 8), float(c % 2), np.float32)
                     for c in range(NCORES)]
    w["w1_wk"] = _T(inp["L1_attn_a_W_k"])
    w["w1_wv"] = _T(inp["L1_attn_a_W_v"])
    w["w1_wr"] = _T(inp["L1_attn_a_W_r"])
    w["w1_wo"] = _T(inp["L1_attn_a_W_o"])
    w["f1_gw"] = _T(inp["L1_ffn_gate_w"])
    w["f1_eb"] = _rep(inp["L1_ffn_expert_bias"])
    w["f1_w1"] = np.ascontiguousarray(np.asarray(inp["L1_ffn_w1"]).transpose(0, 2, 1))
    w["f1_w3"] = np.ascontiguousarray(np.asarray(inp["L1_ffn_w3"]).transpose(0, 2, 1))
    w["f1_w2"] = np.ascontiguousarray(np.asarray(inp["L1_ffn_w2"]).transpose(0, 2, 1))
    w["f1_b1"] = np.ascontiguousarray(
        np.asarray(inp["L1_ffn_b1"]).reshape(16, 4, 128).transpose(0, 2, 1))
    w["f1_b3"] = np.ascontiguousarray(
        np.asarray(inp["L1_ffn_b3"]).reshape(16, 4, 128).transpose(0, 2, 1))
    w["f1_b2"] = np.asarray(inp["L1_ffn_b2"])
    w["f1_sw1"] = _T(inp["L1_ffn_sw1"])
    w["f1_sw3"] = _T(inp["L1_ffn_sw3"])
    w["f1_sw2"] = _T(inp["L1_ffn_sw2"])
    w["f1_sb1"] = _colmaj(inp["L1_ffn_sb1"])
    w["f1_sb3"] = _colmaj(inp["L1_ffn_sb3"])
    w["f1_sb2"] = _colmaj(inp["L1_ffn_sb2"])
    w["b1_nw"] = _colmaj(inp["L1_attn_b_norm_w"])
    w["b1_nb"] = _colmaj(inp["L1_attn_b_norm_b"])
    w["b1_qkv"] = _T(inp["L1_attn_b_qkv_w"])
    qb = np.asarray(inp["L1_attn_b_qkv_b"])
    w["b1_bq"] = _rep(qb[0:D])
    w["b1_bk"] = _rep(qb[D:2 * D])
    w["b1_bv"] = _colmaj(qb[2 * D:3 * D])
    w["b1_proj"] = _T(inp["L1_attn_b_proj_w"])
    w["b1_pb"] = _colmaj(inp["L1_attn_b_proj_b"])
    return w


WEIGHT_SHAPES = {
    "a0_wq": (D, 384), "a0_bq": (128, 3), "a0_wk": (D, 384), "a0_bk": (128, 3),
    "a0_wv": (D, D), "a0_bv": (128, D), "a0_gamma": (128, 1),
    "f0_gw": (D, 8), "f0_wg": (8, D, 2048), "f0_wu": (8, D, 2048),
    "f0_wd": (8, 2048, D), "f0_sg": (D, 2048), "f0_su": (D, 2048),
    "f0_sd": (2048, D),
    "b0_wq": (D, D), "b0_wk": (D, D), "b0_wv": (D, D), "b0_wo": (D, D),
    "b0_bq": (128, 8), "b0_bk": (128, 8), "b0_bv": (128, D), "b0_bo": (128, 8),
    "w1_decay": (128, 8), "w1_first": (128, 8), "w1_mk": (128, 8),
    "w1_bmask": (128, 8),
    "w1_mv": (128, 8), "w1_mr": (128, 8),
    "w1_wk": (D, D), "w1_wv": (D, D), "w1_wr": (D, D), "w1_wo": (D, D),
    "f1_gw": (D, 16), "f1_eb": (128, 16),
    "f1_w1": (16, D, 512), "f1_w3": (16, D, 512), "f1_w2": (16, 512, D),
    "f1_b1": (16, 128, 4), "f1_b3": (16, 128, 4), "f1_b2": (16, D),
    "f1_sw1": (D, D), "f1_sw3": (D, D), "f1_sw2": (D, D),
    "f1_sb1": (128, 8), "f1_sb3": (128, 8), "f1_sb2": (128, 8),
    "b1_nw": (128, 8), "b1_nb": (128, 8), "b1_qkv": (D, 3 * D),
    "b1_bq": (128, D), "b1_bk": (128, D), "b1_bv": (128, 8),
    "b1_proj": (D, D), "b1_pb": (128, 8),
}


def setup_common(nc, stack, wnames):
    prog = Prog(nc, stack)
    ctx = Ctx(nc, prog, stack)
    ctx.dram["x"] = nc.dram_tensor("x", [128, DC * TL], F32, kind="ExternalInput").ap()
    ctx.dram["xo"] = nc.dram_tensor("xo", [128, DC * TL], F32, kind="ExternalOutput").ap()
    for name in wnames:
        ctx.dram[name] = nc.dram_tensor(
            name, list(WEIGHT_SHAPES[name]), F32, kind="ExternalInput").ap()
    if F0_DEBUG:
        ctx.dram["dbg"] = nc.dram_tensor("dbg", [128, 16 * 1024], BF16,
                                         kind="ExternalOutput").ap()
        ctx.dram["dbgy"] = nc.dram_tensor("dbgy", [128, DC * 1024], F32,
                                          kind="ExternalOutput").ap()
        ctx.dram["dbgg"] = nc.dram_tensor("dbgg", [128, (TL // 128) * 8], BF16,
                                          kind="ExternalOutput").ap()
        ctx.dram["dbgl"] = nc.dram_tensor("dbgl", [128, 32], F32,
                                          kind="ExternalOutput").ap()
    return ctx


def emit_load_x(ctx):
    XSRC = ctx.dram[ctx.xsrc]
    XDST = ctx.dram[ctx.xdst]
    x = ctx.sb("x_sb", [128, DC * TL], F32)
    ctx.p.add("sync", lambda e: e.dma_start(out=x[:], in_=XSRC),
              inc=ctx.dx, k=16)
    return x, ctx.dx.at()


def emit_store_x(ctx, x, waits):
    XSRC = ctx.dram[ctx.xsrc]
    XDST = ctx.dram[ctx.xdst]
    ctx.p.add("sync", lambda e: e.dma_start(out=XDST, in_=x[:]),
              waits=waits, inc=ctx.dx, k=16)
    ctx.p.add("sync", None, waits=[ctx.dx.at()])


def emit_consts(ctx, TT=TL, sdt=BF16):
    ones = ctx.sb("onesbf", [128, 128], sdt)
    ctx.p.add("vector", lambda e: e.memset(ones[:], 1.0), inc=ctx.ev)
    eps = ctx.sb("epscol", [128, 1], F32)
    ctx.p.add("vector", lambda e: e.memset(eps[:], 1e-5), inc=ctx.ev)
    ctx.sb("lnsq", [128, DC * 512], sdt)
    ctx.sb("lnmu", [128, TT], F32)
    ctx.sb("lnrs", [128, TT], F32)
    return ones, ctx.ev.at()



def emit_l0_attn_a(ctx, x, xw):
    XSRC = ctx.dram[ctx.xsrc]
    XDST = ctx.dram[ctx.xdst]
    p = ctx.p
    nc = ctx.nc
    ones, cw = emit_consts(ctx)
    xln = ctx.sb("xln", [128, DC * TL], BF16)
    ln_w = emit_ln(ctx, x, xln, TL, cw, "ln", x_wait=xw)

    k_in = ctx.ndt("k_in", [384, TL], BF16)
    v_in = ctx.ndt("v_in", [TL, D], BF16)
    k_out = ctx.ndt("k_out", [768, TL], BF16)
    v_out = ctx.ndt("v_out", [2 * TL, D], BF16)

    wq_sb = ctx.sb("wq_sb", [128, DC * 384], BF16)
    wk_sb = ctx.sb("wk_sb", [128, DC * 384], BF16)
    wv_sb = ctx.sb("wv_sb", [128, DC * D], BF16)
    bq_sb = ctx.sb("bq_sb", [128, 3], F32)
    bk_sb = ctx.sb("bk_sb", [128, 3], F32)
    bv_sb = ctx.sb("bv_sb", [128, D], BF16)
    gam_sb = ctx.sb("gam_sb", [128, 1], F32)

    w_q = emit_weight_load(ctx, wq_sb[:], chunked(ctx.dram["a0_wq"]))
    w_k = emit_weight_load(ctx, wk_sb[:], chunked(ctx.dram["a0_wk"]))
    w_v = emit_weight_load(ctx, wv_sb[:], chunked(ctx.dram["a0_wv"]))
    w_bq = emit_weight_load(ctx, bq_sb[:], ctx.dram["a0_bq"])
    w_bk = emit_weight_load(ctx, bk_sb[:], ctx.dram["a0_bk"])
    w_bv = emit_weight_load(ctx, bv_sb[:], ctx.dram["a0_bv"])
    w_g = emit_weight_load(ctx, gam_sb[:], ctx.dram["a0_gamma"])

    q_sb = ctx.sb("q_sb", [128, 3 * TL], BF16)
    k_sb = ctx.sb("k_sb", [128, 3 * TL], BF16)
    bq4 = ctx.sb("bq4", [128, 3], F32)
    p.add("vector", lambda e: e.tensor_scalar(
        out=bq4[:], in0=bq_sb[:], scalar1=0.25, scalar2=None, op0=ALU.mult),
        waits=[w_bq], inc=ctx.ev)
    bq4_w = ctx.ev.at()

    def evac_q(bank, oc, t0, tw, waits):
        p.add("scalar", lambda e, bk=bank, oc=oc, t0=t0, tw=tw: e.activation(
            out=q_sb[:, oc * TL + t0: oc * TL + t0 + tw], in_=bk[:, 0:tw],
            func=AF.Identity, bias=bq4[:, oc:oc + 1], scale=0.25),
            waits=waits + [bq4_w], inc=ctx.es)
        return ctx.es

    def evac_k(bank, oc, t0, tw, waits):
        p.add("scalar", lambda e, bk=bank, oc=oc, t0=t0, tw=tw: e.activation(
            out=k_sb[:, oc * TL + t0: oc * TL + t0 + tw], in_=bk[:, 0:tw],
            func=AF.Identity, bias=bk_sb[:, oc:oc + 1]),
            waits=waits + [w_bk], inc=ctx.es)
        return ctx.es

    emit_proj_fm(ctx, wq_sb, w_q, xln, ln_w, TL, 384, evac_q, outw_cols=384)
    emit_proj_fm(ctx, wk_sb, w_k, xln, ln_w, TL, 384, evac_k, outw_cols=384)

    vstage = ctx.sb("vstage", [128, 4 * 512], BF16)
    vslot_free = [None] * 4
    vslot_i = [0]

    def evac_v(bank, tc, o0, ow, waits):
        sl = vslot_i[0] % 4
        vslot_i[0] += 1
        w2 = waits + [w_bv] + ([vslot_free[sl]] if vslot_free[sl] else [])
        p.add("vector", lambda e, bk=bank, sl=sl, ow=ow: e.tensor_add(
            out=vstage[:, sl * 512: sl * 512 + ow], in0=bk[:, 0:ow],
            in1=bv_sb[:, o0:o0 + ow]),
            waits=w2, inc=ctx.ev)
        evw = ctx.ev.at()
        p.add("sync", lambda e, sl=sl, tc=tc, o0=o0, ow=ow: e.dma_start(
            out=v_in[tc * 128:(tc + 1) * 128, o0:o0 + ow],
            in_=vstage[:, sl * 512: sl * 512 + ow]),
            waits=[evw], inc=ctx.dx, k=16)
        vslot_free[sl] = ctx.dx.at()
        return ctx.ev

    emit_proj_tm(ctx, wv_sb, w_v, xln, ln_w, TL, D, evac_v)
    v_shipped = ctx.dx.at()
    kv_done_s = ctx.es.at()

    p.add("sync", lambda e: e.dma_start(
        out=chunked(k_in), in_=k_sb[:]),
        waits=[kv_done_s], inc=ctx.dx, k=16)
    p.add("gpsimd", lambda e: e.collective_compute(
        "AllGather", ALU.bypass, replica_groups=PAIRS,
        ins=[k_in.opt()], outs=[k_out.opt()]),
        waits=[ctx.dx.at()], inc=ctx.cc)
    p.add("gpsimd", lambda e: e.collective_compute(
        "AllGather", ALU.bypass, replica_groups=PAIRS,
        ins=[v_in.opt()], outs=[v_out.opt()]),
        waits=[v_shipped], inc=ctx.cc)
    cc_w = ctx.cc.at()

    for k in range(DC):
        p.add("vector", lambda e, k=k: e.tensor_add(
            out=x[:, k * TL:(k + 1) * TL], in0=x[:, k * TL:(k + 1) * TL],
            in1=xln[:, k * TL:(k + 1) * TL]),
            waits=[ln_w], inc=ctx.ev)
    xb_w = ctx.ev.at()

    kf2 = ctx.sb("kf2", [128, T], BF16)

    def kf_slice(kc, lo, hi):
        if kc < 2:
            return wv_sb[:, kc * T + lo: kc * T + hi]
        return kf2[:, lo:hi]

    for r in range(2):
        for c2 in range(3):
            dst = kf_slice(c2, r * TL, r * TL + TL)
            p.add("sync", lambda e, r=r, c2=c2, d=dst: e.dma_start(
                out=d, in_=k_out[r * 384 + c2 * 128: r * 384 + (c2 + 1) * 128, :]),
                waits=[cc_w], inc=ctx.dx, k=16)
    kv_full_w = ctx.dx.at()

    P_buf = xln
    sexp = ctx.sb("sexp", [128, 4 * 512], BF16)
    vh_sb = ctx.sb("vh_sb", [128, 32 * 128], BF16)
    dinv = ctx.t["lnrs"]
    tmp5 = ctx.t["lnmu"]
    slot_free = [None] * 4
    vh_free = [None]
    ones_w = cw

    for h in range(8):
        kc, kr = h // 3, 32 * (h % 3)
        hs = 0
        vh_ap = bass.AP(v_out.tensor, h * 128, [[D, 128], [128 * D, 32], [1, 128]])
        wv_h = [cc_w] + ([vh_free[hs]] if vh_free[hs] else [])
        p.add("sync", lambda e, hs=hs, a=vh_ap: e.dma_start(
            out=vh_sb[:, hs * 4096:(hs + 1) * 4096], in_=a),
            waits=wv_h, inc=ctx.dx, k=16)
        vh_w = ctx.dx.at()
        for tq0 in range(0, TL, 512):
            for c in range(32):
                bank, bw, bi = ctx.pool.acquire()
                lh = kf_slice(kc, c * 128, (c + 1) * 128)[kr:kr + 32, :]
                rh = q_sb[kr:kr + 32, kc * TL + tq0: kc * TL + tq0 + 512]
                p.add("tensor", lambda e, o=bank, a=lh, b=rh: e.matmul(
                    o[:, 0:512], a, b, start=True, stop=True),
                    waits=[kv_full_w, kv_done_s] + bw, inc=ctx.mm)
                mmw = ctx.mm.at()
                sl = c % 4
                p.add("scalar", lambda e, bk=bank, c=c: e.activation(
                    out=P_buf[:, c * 512:(c + 1) * 512], in_=bk[:, 0:512],
                    func=AF.Exp), waits=[mmw, xb_w], inc=ctx.es)
                expw = [ctx.es.at()] + ([slot_free[sl]] if slot_free[sl] else [])
                p.add("scalar", lambda e, bk=bank, sl=sl: e.activation(
                    out=sexp[:, sl * 512:(sl + 1) * 512], in_=bk[:, 0:512],
                    func=AF.Exp, scale=0.2), waits=expw, inc=ctx.es)
                esw = ctx.es.at()
                p.add("vector", lambda e, c=c, sl=sl: e.tensor_max(
                    out=P_buf[:, c * 512:(c + 1) * 512],
                    in0=P_buf[:, c * 512:(c + 1) * 512],
                    in1=sexp[:, sl * 512:(sl + 1) * 512]),
                    waits=[esw], inc=ctx.ev)
                slot_free[sl] = ctx.ev.at()
                ctx.pool.release(bi, ctx.ev)
            P_w = ctx.ev.at()
            den_bank, dbw, dbi = ctx.pool.acquire()
            dpairs = [(ones[:], P_buf[:, c * 512:(c + 1) * 512]) for c in range(32)]
            dmw = emit_matmul_chain(ctx, den_bank[:, 0:512], dpairs,
                                    waits=[P_w, ones_w] + dbw)
            p.add("vector", lambda e, b=den_bank: e.reciprocal(
                out=dinv[:, 0:512], in_=b[:, 0:512]),
                waits=[dmw], inc=ctx.ev)
            ctx.pool.release(dbi, ctx.ev)
            dinv_w = ctx.ev.at()
            bank, bw, bi = ctx.pool.acquire()
            avpairs = [(vh_sb[:, hs * 4096 + c * 128: hs * 4096 + (c + 1) * 128],
                        P_buf[:, c * 512:(c + 1) * 512]) for c in range(32)]
            avw = emit_matmul_chain(ctx, bank[:, 0:512], avpairs,
                                    waits=[P_w, vh_w] + bw)
            vh_free[hs] = avw
            p.add("vector", lambda e, bk=bank: e.tensor_mul(
                out=tmp5[:, 0:512], in0=bk[:, 0:512], in1=dinv[:, 0:512]),
                waits=[avw, dinv_w], inc=ctx.ev)
            ctx.pool.release(bi, ctx.ev)
            p.add("vector", lambda e, h=h, tq0=tq0: e.scalar_tensor_tensor(
                out=x[:, h * TL + tq0: h * TL + tq0 + 512],
                in0=tmp5[:, 0:512], scalar=gam_sb[:, 0:1],
                in1=x[:, h * TL + tq0: h * TL + tq0 + 512],
                op0=ALU.mult, op1=ALU.add),
                waits=[xb_w, w_g], inc=ctx.ev)
    return ctx.ev.at()



def ap3(t, off, dims):
    return bass.AP(t.tensor if hasattr(t, "tensor") else t, off, dims)


def emit_l0_ffn(ctx):
    XSRC = ctx.dram[ctx.xsrc]
    XDST = ctx.dram[ctx.xdst]
    p = ctx.p
    nc = ctx.nc
    import contextlib
    xg = ctx.nsb("x_sb", [128, DC * TL], F32, side="right")
    x = xg.__enter__()
    p.add("sync", lambda e: e.dma_start(out=x[:], in_=XSRC),
          inc=ctx.dx, k=16)
    xw = ctx.dx.at()
    ones, cw = emit_consts(ctx)
    xln = ctx.sb("xln", [128, DC * TL], BF16)
    ln_w = emit_ln(ctx, x, xln, TL, cw, "ln", x_wait=xw)
    xg.__exit__(None, None, None)

    E = 8
    gw_sb = ctx.sb("gw_sb", [128, DC * E], BF16)
    w_gw = emit_weight_load(ctx, gw_sb[:], chunked(ctx.dram["f0_gw"]))
    gtok = ctx.sb("gtok", [128, (TL // 128) * E], BF16)
    gA = ctx.sb("gA", [128, E], F32)
    gL = ctx.sb("gL", [128, E], F32)
    if F0_DEBUG:
        ctx.sb("dbgl_sb", [128, 4 * E], F32)
    gB = ctx.sb("gB", [128, E], F32)
    gcol = ctx.sb("gcol", [128, 4], F32)
    idbf = ctx.sb("idbf", [128, 128], BF16)
    lnmu_t = ctx.t["lnmu"]
    p.add("gpsimd", lambda e: e.iota(
        lnmu_t[:, 0:128], [[1, 128]], base=0, channel_multiplier=-1,
        allow_small_or_imprecise_dtypes=True), waits=[ln_w], inc=ctx.gp)
    p.add("vector", lambda e: e.tensor_scalar(
        out=idbf[:], in0=lnmu_t[:, 0:128], scalar1=0.0, scalar2=None,
        op0=ALU.is_equal), waits=[ctx.gp.at()], inc=ctx.ev)
    id_w = ctx.ev.at()

    for tc in range(TL // 128):
        bank, bw, bi = ctx.pool.acquire()
        pairs = []
        for k in range(DC):
            pairs.append((xln[:, k * TL + tc * 128: k * TL + (tc + 1) * 128],
                          gw_sb[:, k * E:(k + 1) * E]))
        mmw = emit_matmul_chain(ctx, bank[:, 0:E], pairs, waits=[w_gw, ln_w] + bw)
        def gser(eng, fn):
            p.add(eng, fn, waits=[ctx.ev.at(), ctx.es.at(), ctx.mm.at()],
                  inc=(ctx.ev if eng == "vector" else ctx.es))
        gser("vector", lambda e, b=bank: e.tensor_copy(out=gL[:], in_=b[:, 0:E]))
        if F0_DEBUG and tc == 0:
            gser("vector", lambda e: e.tensor_copy(out=ctx.t["dbgl_sb"][:, 0:E], in_=gL[:]))
        gser("vector", lambda e: e.reduce_max(out=gcol[:, 0:1], in_=gL[:], axis=mybir.AxisListType.X))
        gser("vector", lambda e: e.tensor_scalar(out=gA[:], in0=gL[:], scalar1=gcol[:, 0:1], scalar2=None, op0=ALU.is_ge))
        if F0_DEBUG and tc == 0:
            gser("vector", lambda e: e.tensor_copy(out=ctx.t["dbgl_sb"][:, E:2 * E], in_=gA[:]))
        gser("vector", lambda e: e.scalar_tensor_tensor(out=gB[:], in0=gA[:], scalar=-1e30, in1=gL[:], op0=ALU.mult, op1=ALU.add))
        gser("vector", lambda e: e.reduce_max(out=gcol[:, 1:2], in_=gB[:], axis=mybir.AxisListType.X))
        gser("vector", lambda e: e.tensor_scalar(out=gB[:], in0=gB[:], scalar1=gcol[:, 1:2], scalar2=None, op0=ALU.is_ge))
        gser("vector", lambda e: e.tensor_sub(out=gcol[:, 2:3], in0=gcol[:, 1:2], in1=gcol[:, 0:1]))
        gser("scalar", lambda e: e.activation(out=gcol[:, 2:3], in_=gcol[:, 2:3], func=AF.Exp))
        gser("vector", lambda e: e.tensor_scalar(out=gcol[:, 3:4], in0=gcol[:, 2:3], scalar1=1.0, scalar2=None, op0=ALU.add))
        gser("vector", lambda e: e.reciprocal(out=gcol[:, 3:4], in_=gcol[:, 3:4]))
        gser("vector", lambda e: e.tensor_scalar(out=gB[:], in0=gB[:], scalar1=gcol[:, 2:3], scalar2=None, op0=ALU.mult))
        gser("vector", lambda e: e.tensor_add(out=gB[:], in0=gB[:], in1=gA[:]))
        gser("vector", lambda e, tc=tc: e.tensor_scalar(out=gtok[:, tc * E:(tc + 1) * E], in0=gB[:], scalar1=gcol[:, 3:4], scalar2=None, op0=ALU.mult))
        if F0_DEBUG and tc == 0:
            gser("vector", lambda e: e.tensor_copy(out=ctx.t["dbgl_sb"][:, 2 * E:3 * E], in_=gB[:]))
            gser("vector", lambda e: e.tensor_copy(out=ctx.t["dbgl_sb"][:, 3 * E:3 * E + 4], in_=gcol[:, 0:4]))
            p.add("sync", lambda e: e.dma_start(out=ctx.dram["dbgl"], in_=ctx.t["dbgl_sb"][:]),
                  waits=[ctx.ev.at()], inc=ctx.dx, k=16)
        ctx.pool.release(bi, ctx.ev)
    gtok_w = ctx.ev.at()
    if F0_DEBUG:
        p.add("sync", lambda e: e.dma_start(out=ctx.dram["dbgg"], in_=gtok[:]),
              waits=[gtok_w], inc=ctx.dx, k=16)

    ptr_bf = ctx.stack.enter_context(nc.psum_tensor(ctx.tag + "ptr_bf", [128, 1024], BF16))
    gfm = ctx.sb("gfm", [128, TL], BF16)
    grow = ctx.sb("grow", [1, 1024], BF16)
    grow_user = [None]
    for tc in range(TL // 128):
        p.add("tensor", lambda e, tc=tc: e.transpose(
            ptr_bf[0:E, 0:128], gtok[:, tc * E:(tc + 1) * E], idbf[:]),
            waits=[gtok_w, id_w] if tc == 0 else [ctx.ev.at()], inc=ctx.mm)
        p.add("vector", lambda e, tc=tc: e.tensor_copy(
            out=gfm[0:E, tc * 128:(tc + 1) * 128], in_=ptr_bf[0:E, 0:128]),
            waits=[ctx.mm.at()], inc=ctx.ev)
    gfm_w = ctx.ev.at()

    hid = ctx.sb("hid", [128, 16 * 1024], BF16)
    w13 = ctx.sb("w13", [128, 2 * DC * 1024], BF16)
    w2b_ = ctx.sb("w2b_", [128, 2 * DC * 1024], BF16)
    grep = ctx.sb("grep", [128, 1024], F32)
    stile = ctx.sb("stile", [128, 2 * 512], BF16)
    utile = ctx.sb("utile", [128, 2 * 512], BF16)
    stage = ctx.sb("stage", [128, 4 * 512], F32)

    p.add("gpsimd", lambda e, XS=XSRC, XD=XDST: e.dma_start(out=XD, in_=XS),
          inc=ctx.dw, k=16)
    xo_init = ctx.dw.at()

    HD = 1024 * 2048
    qslot_user = [None, None]
    w2_user = [None]
    hid_user = [None]
    qctr = [0]
    oslot = [0]
    ofree = [None] * 4

    def run_expert(e_idx, th, shared, first=False):
        sfree = [None, None]
        if not shared:
            gw2 = [gfm_w] + ([grow_user[0]] if grow_user[0] else [])
            p.add("sync", lambda e: e.dma_start(
                out=grow[0:1, 0:1024],
                in_=gfm[e_idx:e_idx + 1, th * 1024:(th + 1) * 1024]),
                waits=gw2, inc=ctx.dx, k=16)
            growd = ctx.dx.at()
            for i in range(2):
                bank, bw, bi = ctx.pool.acquire()
                p.add("tensor", lambda e, b=bank, i=i: e.matmul(
                    b[:, 0:512], ones[0:1, 0:128],
                    grow[0:1, i * 512:(i + 1) * 512],
                    start=True, stop=True),
                    waits=[growd, cw] + bw, inc=ctx.mm)
                p.add("vector", lambda e, b=bank, i=i: e.tensor_copy(
                    out=grep[:, i * 512:(i + 1) * 512], in_=b[:, 0:512]),
                    waits=[ctx.mm.at()], inc=ctx.ev)
                ctx.pool.release(bi, ctx.ev)
            grow_user[0] = ctx.mm.at()
        grep_w = ctx.ev.at()
        if shared:
            aw2a = ap3(ctx.dram["f0_sd"], 0, [[1024, 128], [128 * 1024, DC], [1, 1024]])
            aw2b = ap3(ctx.dram["f0_sd"], 1024 * 1024, [[1024, 128], [128 * 1024, DC], [1, 1024]])
        else:
            off = e_idx * 2048 * 1024
            aw2a = ap3(ctx.dram["f0_wd"], off, [[1024, 128], [128 * 1024, DC], [1, 1024]])
            aw2b = ap3(ctx.dram["f0_wd"], off + 1024 * 1024, [[1024, 128], [128 * 1024, DC], [1, 1024]])
        lw = [w2_user[0]] if w2_user[0] else []
        wl2a = emit_weight_load(ctx, w2b_[:, 0:DC * 1024], aw2a, waits=lw)
        wl2b = emit_weight_load(ctx, w2b_[:, DC * 1024:], aw2b, waits=lw)
        for dq in range(4):
            sl = qctr[0] % 2
            qctr[0] += 1
            base = sl * DC * 1024
            w1q = w13[:, base: base + DC * 512]
            w3q = w13[:, base + DC * 512: base + DC * 1024]
            if shared:
                a1 = ap3(ctx.dram["f0_sg"], dq * 512, [[2048, 128], [128 * 2048, DC], [1, 512]])
                a3 = ap3(ctx.dram["f0_su"], dq * 512, [[2048, 128], [128 * 2048, DC], [1, 512]])
            else:
                a1 = ap3(ctx.dram["f0_wg"], e_idx * HD + dq * 512,
                         [[2048, 128], [128 * 2048, DC], [1, 512]])
                a3 = ap3(ctx.dram["f0_wu"], e_idx * HD + dq * 512,
                         [[2048, 128], [128 * 2048, DC], [1, 512]])
            lw = [qslot_user[sl]] if qslot_user[sl] else []
            wl1 = emit_weight_load(ctx, w1q, a1, waits=lw)
            wl3 = emit_weight_load(ctx, w3q, a3, waits=lw)
            for dc in range(4):
                for t5 in range(2):
                    tog = (dc * 2 + t5) % 2
                    hcol = (dq * 4 + dc) * 1024 + t5 * 512
                    xcol = th * 1024 + t5 * 512
                    b1, bw1, bi1 = ctx.pool.acquire()
                    pairs = [(w1q[:, k * 512 + dc * 128: k * 512 + (dc + 1) * 128],
                              xln[:, k * TL + xcol: k * TL + xcol + 512])
                             for k in range(DC)]
                    m1w = emit_matmul_chain(ctx, b1[:, 0:512], pairs,
                                            waits=[wl1, ln_w] + bw1)
                    sw = [m1w] + ([sfree[tog]] if sfree[tog] else [])
                    p.add("scalar", lambda e, b=b1, tog=tog: e.activation(
                        out=stile[:, tog * 512:(tog + 1) * 512], in_=b[:, 0:512],
                        func=AF.Silu), waits=sw, inc=ctx.es)
                    ctx.pool.release(bi1, ctx.es)
                    b3, bw3, bi3 = ctx.pool.acquire()
                    pairs = [(w3q[:, k * 512 + dc * 128: k * 512 + (dc + 1) * 128],
                              xln[:, k * TL + xcol: k * TL + xcol + 512])
                             for k in range(DC)]
                    m3w = emit_matmul_chain(ctx, b3[:, 0:512], pairs,
                                            waits=[wl3, ln_w] + bw3)
                    if shared:
                        p.add("vector", lambda e, b=b3, tog=tog: e.tensor_copy(
                            out=utile[:, tog * 512:(tog + 1) * 512], in_=b[:, 0:512]),
                            waits=[m3w], inc=ctx.ev)
                    else:
                        p.add("vector", lambda e, b=b3, tog=tog, t5=t5: e.tensor_mul(
                            out=utile[:, tog * 512:(tog + 1) * 512], in0=b[:, 0:512],
                            in1=grep[:, t5 * 512:(t5 + 1) * 512]),
                            waits=[m3w, grep_w], inc=ctx.ev)
                    ctx.pool.release(bi3, ctx.ev)
                    hw = [ctx.es.at(), ctx.ev.at()]
                    if hid_user[0] is not None:
                        hw.append(hid_user[0])
                    p.add("vector", lambda e, hcol=hcol, tog=tog: e.tensor_mul(
                        out=hid[:, hcol:hcol + 512],
                        in0=stile[:, tog * 512:(tog + 1) * 512],
                        in1=utile[:, tog * 512:(tog + 1) * 512]),
                        waits=hw, inc=ctx.ev)
                    sfree[tog] = ctx.ev.at()
            qslot_user[sl] = ctx.mm.at()
        hid_w = ctx.ev.at()
        for oc in range(DC):
            for t5 in range(2):
                bank, bw, bi = ctx.pool.acquire()
                pairs = []
                for half in range(2):
                    for kc in range(DC):
                        pairs.append((w2b_[:, half * DC * 1024 + kc * 1024 + oc * 128:
                                           half * DC * 1024 + kc * 1024 + (oc + 1) * 128],
                                      hid[:, (half * 8 + kc) * 1024 + t5 * 512:
                                          (half * 8 + kc) * 1024 + t5 * 512 + 512]))
                mmw = emit_matmul_chain(ctx, bank[:, 0:512], pairs,
                                        waits=[wl2a, wl2b, hid_w] + bw)
                sl2 = oslot[0] % 4
                oslot[0] += 1
                sw = [mmw] + ([ofree[sl2]] if ofree[sl2] else [])
                p.add("scalar", lambda e, b=bank, sl2=sl2: e.activation(
                    out=stage[:, sl2 * 512:(sl2 + 1) * 512], in_=b[:, 0:512],
                    func=AF.Identity), waits=sw, inc=ctx.es)
                ctx.pool.release(bi, ctx.es)
                tcol = th * 1024 + t5 * 512
                p.add("gpsimd", lambda e, sl2=sl2, oc=oc, tcol=tcol, XD=XDST: e.dma_start(
                    out=XD[:, oc * TL + tcol: oc * TL + tcol + 512],
                    in_=stage[:, sl2 * 512:(sl2 + 1) * 512], accum_op=ALU.add),
                    waits=[ctx.es.at(), xo_init], inc=ctx.dw, k=16)
                ofree[sl2] = ctx.dw.at()
        hid_user[0] = ctx.mm.at()
        w2_user[0] = ctx.mm.at()

    for th in range(2):
        for pos, e_idx in enumerate(F0_EXPERTS):
            run_expert(e_idx if e_idx < 8 else 0, th, shared=(e_idx == 8))
    p.add("sync", None, waits=[ctx.dw.at(), ctx.dx.at()])


def _emit_f0(ctx):
    emit_l0_ffn(ctx)



def emit_l0_attn_b(ctx):
    XSRC = ctx.dram[ctx.xsrc]
    XDST = ctx.dram[ctx.xdst]
    p = ctx.p
    nc = ctx.nc
    p.add("gpsimd", lambda e: e.dma_start(out=XDST, in_=XSRC),
          inc=ctx.dw, k=16)
    xo_init = ctx.dw.at()

    xg = ctx.nsb("x_sb", [128, DC * TL], F32, side="right")
    x = xg.__enter__()
    p.add("sync", lambda e: e.dma_start(out=x[:], in_=XSRC),
          inc=ctx.dx, k=16)
    xw = ctx.dx.at()
    ones, cw = emit_consts(ctx, sdt=FP16)
    xln = ctx.sb("xln", [128, DC * TL], FP16)
    ln_w = emit_ln(ctx, x, xln, TL, cw, "ln", x_wait=xw)
    xg.__exit__(None, None, None)

    k_in = ctx.ndt("k_in", [D, TL], FP16)
    v_in = ctx.ndt("v_in", [TL, D], BF16)
    k_out = ctx.ndt("k_out", [2 * D, TL], FP16)
    v_out = ctx.ndt("v_out", [2 * TL, D], BF16)
    att_d = ctx.ndt("att_d", [D, TL], BF16)

    q_sb = ctx.sb("q_sb", [128, DC * TL], FP16)
    wbuf = ctx.sb("wbuf", [128, DC * D], FP16)
    bqc = ctx.sb("bqc", [128, 8], F32)
    bkc = ctx.sb("bkc", [128, 8], F32)
    boc = ctx.sb("boc", [128, 8], F32)
    bvr = ctx.sb("bvr", [128, D], BF16)
    stage = ctx.sb("stage", [128, 4 * 512], F32)
    stbf = ctx.sb("stbf", [128, 4 * 512], BF16)
    w_bq = emit_weight_load(ctx, bqc[:], ctx.dram["b0_bq"])
    w_bk = emit_weight_load(ctx, bkc[:], ctx.dram["b0_bk"])
    w_bo = emit_weight_load(ctx, boc[:], ctx.dram["b0_bo"])
    w_bv = emit_weight_load(ctx, bvr[:], ctx.dram["b0_bv"])

    kg = ctx.nsb("k_sb", [128, DC * TL], FP16, side="right")
    k_sb = kg.__enter__()
    w_k = emit_weight_load(ctx, wbuf[:], chunked(ctx.dram["b0_wk"]))

    def evac_k(bank, oc, t0, tw, waits):
        p.add("vector", lambda e, bk=bank, oc=oc, t0=t0, tw=tw: e.tensor_scalar(
            out=k_sb[:, oc * TL + t0: oc * TL + t0 + tw], in0=bk[:, 0:tw],
            scalar1=bkc[:, oc:oc + 1], scalar2=-2.0, op0=ALU.add, op1=ALU.mult),
            waits=waits + [w_bk], inc=ctx.ev)
        return ctx.ev

    emit_proj_fm(ctx, wbuf, w_k, xln, ln_w, TL, D, evac_k)
    p.add("sync", lambda e: e.dma_start(out=chunked(k_in), in_=k_sb[:]),
          waits=[ctx.ev.at()], inc=ctx.dx, k=16)
    p.add("gpsimd", lambda e: e.collective_compute(
        "AllGather", ALU.bypass, replica_groups=PAIRS,
        ins=[k_in.opt()], outs=[k_out.opt()]),
        waits=[ctx.dx.at()], inc=ctx.cc)
    kg.__exit__(None, None, None)

    w_q = emit_weight_load(ctx, wbuf[:], chunked(ctx.dram["b0_wq"]),
                           waits=[ctx.mm.at()])

    def evac_q(bank, oc, t0, tw, waits):
        p.add("scalar", lambda e, bk=bank, oc=oc, t0=t0, tw=tw: e.activation(
            out=q_sb[:, oc * TL + t0: oc * TL + t0 + tw], in_=bk[:, 0:tw],
            func=AF.Identity, bias=bqc[:, oc:oc + 1]),
            waits=waits + [w_bq], inc=ctx.es)
        return ctx.es

    emit_proj_fm(ctx, wbuf, w_q, xln, ln_w, TL, D, evac_q)

    w_v = emit_weight_load(ctx, wbuf[:], chunked(ctx.dram["b0_wv"]),
                           waits=[ctx.mm.at()])
    vslot_free = [None] * 4
    vslot_i = [0]

    def evac_v(bank, tc, o0, ow, waits):
        sl = vslot_i[0] % 4
        vslot_i[0] += 1
        w2 = waits + [w_bv] + ([vslot_free[sl]] if vslot_free[sl] else [])
        p.add("vector", lambda e, bk=bank, sl=sl, o0=o0, ow=ow: e.tensor_add(
            out=stbf[:, sl * 512: sl * 512 + ow],
            in0=bk[:, 0:ow], in1=bvr[:, o0:o0 + ow]),
            waits=w2, inc=ctx.ev)
        evw = ctx.ev.at()
        p.add("sync", lambda e, sl=sl, tc=tc, o0=o0, ow=ow: e.dma_start(
            out=v_in[tc * 128:(tc + 1) * 128, o0:o0 + ow],
            in_=stbf[:, sl * 512: sl * 512 + ow]),
            waits=[evw], inc=ctx.dx, k=16)
        vslot_free[sl] = ctx.dx.at()
        return ctx.ev

    emit_proj_tm(ctx, wbuf, w_v, xln, ln_w, TL, D, evac_v)
    p.add("gpsimd", lambda e: e.collective_compute(
        "AllGather", ALU.bypass, replica_groups=PAIRS,
        ins=[v_in.opt()], outs=[v_out.opt()]),
        waits=[ctx.dx.at()], inc=ctx.cc)
    cc_w = ctx.cc.at()

    P_buf = xln
    augq = ctx.sb("augq", [128, TL], FP16)
    augk = ctx.sb("augk", [128, T], FP16)
    vha = ctx.sb("vha", [128, 32 * 65], FP16)
    rowq = ctx.sb("rowq", [1, TL], FP16)
    rowk = ctx.sb("rowk", [1, T], FP16)
    drow = ctx.sb("drow", [1, 512], F32)
    drbf = ctx.sb("drbf", [1, 512], FP16)
    epsln = ctx.sb("epsln", [128, 1], F32)
    p.add("vector", lambda e: e.memset(epsln[:], 1e-6), inc=ctx.ev)
    sq = ctx.t["lnsq"]
    slots = ctx.t["lnmu"]
    dinvrep = ctx.t["lnrs"]
    augq_user = [None]
    augk_user = [None]
    vha_user = [None]
    prev_P = [None]

    for h in range(16):
        qc, qr = h // 2, 64 * (h % 2)
        aw = [ctx.es.at()] + ([augq_user[0]] if augq_user[0] else [])
        p.add("vector", lambda e: e.memset(augq[64:66, :], 1.0),
              waits=([augq_user[0]] if augq_user[0] else []), inc=ctx.ev)
        p.add("vector", lambda e, qc=qc, qr=qr: e.tensor_copy(
            out=augq[0:64, :], in_=q_sb[qr:qr + 64, qc * TL:(qc + 1) * TL]),
            waits=aw, inc=ctx.ev)
        p.add("scalar", lambda e: e.activation(
            out=sq[0:64, 0:TL], in_=augq[0:64, :], func=AF.Square),
            waits=[ctx.ev.at()], inc=ctx.es)
        sq_w = ctx.es.at()
        for i in range(TL // 512):
            bank, bw, bi = ctx.pool.acquire()
            p.add("tensor", lambda e, b=bank, i=i: e.matmul(
                b[0:1, 0:512], ones[0:64, 0:1], sq[0:64, i * 512:(i + 1) * 512],
                start=True, stop=True),
                waits=[sq_w, cw] + bw, inc=ctx.mm)
            p.add("scalar", lambda e, b=bank, i=i: e.activation(
                out=rowq[0:1, i * 512:(i + 1) * 512], in_=b[0:1, 0:512],
                func=AF.Identity), waits=[ctx.mm.at()], inc=ctx.es)
            ctx.pool.release(bi, ctx.es)
        p.add("sync", lambda e: e.dma_start(out=augq[65:66, :], in_=rowq[0:1, :]),
              waits=[ctx.es.at(), ctx.ev.at()], inc=ctx.dx, k=16)
        augq_w = ctx.dx.at()
        p.add("vector", lambda e: e.memset(augk[64:66, :], 1.0),
              waits=([augk_user[0]] if augk_user[0] else []), inc=ctx.ev)
        mko = ctx.ev.at()
        for r in range(2):
            p.add("sync", lambda e, r=r, h=h: e.dma_start(
                out=augk[0:64, r * TL:(r + 1) * TL],
                in_=k_out[r * D + 64 * h: r * D + 64 * h + 64, :]),
                waits=[cc_w] + ([augk_user[0]] if augk_user[0] else []),
                inc=ctx.dx, k=16)
        p.add("scalar", lambda e: e.activation(
            out=sq[0:64, 0:T // 2], in_=augk[0:64, 0:T // 2], func=AF.Square,
            scale=0.5), waits=[ctx.dx.at()], inc=ctx.es)
        p.add("scalar", lambda e: e.activation(
            out=sq[0:64, T // 2:T], in_=augk[0:64, T // 2:T], func=AF.Square,
            scale=0.5), inc=ctx.es)
        sqk_w = ctx.es.at()
        for i in range(T // 512):
            bank, bw, bi = ctx.pool.acquire()
            p.add("tensor", lambda e, b=bank, i=i: e.matmul(
                b[0:1, 0:512], ones[0:64, 0:1], sq[0:64, i * 512:(i + 1) * 512],
                start=True, stop=True),
                waits=[sqk_w, cw] + bw, inc=ctx.mm)
            p.add("scalar", lambda e, b=bank, i=i: e.activation(
                out=rowk[0:1, i * 512:(i + 1) * 512], in_=b[0:1, 0:512],
                func=AF.Identity), waits=[ctx.mm.at()], inc=ctx.es)
            ctx.pool.release(bi, ctx.es)
        p.add("sync", lambda e: e.dma_start(out=augk[64:65, :], in_=rowk[0:1, :]),
              waits=[ctx.es.at(), mko], inc=ctx.dx, k=16)
        augk_w = ctx.dx.at()
        vw = [cc_w] + ([vha_user[0]] if vha_user[0] else [])
        p.add("vector", lambda e: e.memset(vha[:], 1.0),
              waits=([vha_user[0]] if vha_user[0] else []), inc=ctx.ev)
        mvo = ctx.ev.at()
        vsrc = bass.AP(v_out.tensor, 64 * h, [[D, 128], [128 * D, 32], [1, 64]])
        vdst = bass.AP(vha, 0, [[32 * 65, 128], [65, 32], [1, 64]])
        p.add("gpsimd", lambda e, a=vsrc, d=vdst: e.dma_start(out=d, in_=a),
              waits=vw + [mvo], inc=ctx.dw, k=16)
        vha_w = ctx.dw.at()

        for tq0 in range(0, TL, 512):
            for c in range(32):
                bank, bw, bi = ctx.pool.acquire()
                p.add("tensor", lambda e, b=bank, c=c, tq0=tq0: e.matmul(
                    b[:, 0:512], augk[0:66, c * 128:(c + 1) * 128],
                    augq[0:66, tq0:tq0 + 512], start=True, stop=True),
                    waits=[augq_w, augk_w] + bw +
                          ([prev_P[0]] if (prev_P[0] and c == 0 and tq0 == 0) else []),
                    inc=ctx.mm)
                mmw = ctx.mm.at()
                sl = (c % 2) * 512
                p.add("scalar", lambda e, b=bank, sl=sl: e.activation(
                    out=slots[:, sl:sl + 512], in_=b[:, 0:512], func=AF.Ln,
                    bias=epsln[:, 0:1]), waits=[mmw], inc=ctx.es)
                p.add("scalar", lambda e, sl=sl: e.activation(
                    out=slots[:, sl:sl + 512], in_=slots[:, sl:sl + 512],
                    func=AF.Exp, scale=0.5), inc=ctx.es)
                p.add("scalar", lambda e, c=c, sl=sl: e.activation(
                    out=P_buf[:, c * 512:(c + 1) * 512], in_=slots[:, sl:sl + 512],
                    func=AF.Exp, scale=-1.0), inc=ctx.es)
                ctx.pool.release(bi, ctx.es)
            P_w = ctx.es.at()
            bank, bw, bi = ctx.pool.acquire()
            avpairs = [(vha[:, c * 65:(c + 1) * 65],
                        P_buf[:, c * 512:(c + 1) * 512]) for c in range(32)]
            avw = emit_matmul_chain(ctx, bank[0:65, 0:512], avpairs,
                                    waits=[P_w, vha_w] + bw)
            vha_user[0] = avw
            prev_P[0] = avw
            p.add("vector", lambda e, b=bank: e.reciprocal(
                out=drow[0:1, 0:512], in_=b[64:65, 0:512]),
                waits=[avw], inc=ctx.ev)
            p.add("vector", lambda e: e.tensor_copy(
                out=drbf[0:1, 0:512], in_=drow[0:1, 0:512]), inc=ctx.ev)
            b2, bw2, bi2 = ctx.pool.acquire()
            p.add("tensor", lambda e, b=b2: e.matmul(
                b[0:64, 0:512], ones[0:1, 0:64], drbf[0:1, 0:512],
                start=True, stop=True),
                waits=[ctx.ev.at(), cw] + bw2, inc=ctx.mm)
            p.add("vector", lambda e, b=b2: e.tensor_copy(
                out=dinvrep[0:64, 0:512], in_=b[0:64, 0:512]),
                waits=[ctx.mm.at()], inc=ctx.ev)
            ctx.pool.release(bi2, ctx.ev)
            sl = (tq0 // 512) % 4
            p.add("vector", lambda e, b=bank, sl=sl: e.tensor_mul(
                out=stbf[0:64, sl * 512: sl * 512 + 512],
                in0=b[0:64, 0:512], in1=dinvrep[0:64, 0:512]),
                waits=[ctx.ev.at()], inc=ctx.ev)
            ctx.pool.release(bi, ctx.ev)
            p.add("sync", lambda e, h=h, tq0=tq0, sl=sl: e.dma_start(
                out=att_d[64 * h:64 * h + 64, tq0:tq0 + 512],
                in_=stbf[0:64, sl * 512: sl * 512 + 512]),
                waits=[ctx.ev.at()], inc=ctx.dx, k=16)
        augq_user[0] = ctx.mm.at()
        augk_user[0] = ctx.mm.at()
    att_ship = ctx.dx.at()

    w_o = emit_weight_load(ctx, wbuf[:], chunked(ctx.dram["b0_wo"]),
                           waits=[ctx.mm.at()])
    attk = ctx.sb("attk", [128, DC * 512], FP16)
    oslot_free = [None] * 4
    osl = [0]
    attk_user = [None]
    for t4 in range(4):
        akw = [att_ship] + ([attk_user[0]] if attk_user[0] else [])
        asrc = ap3(att_d, t4 * 512, [[TL, 128], [128 * TL, DC], [1, 512]])
        p.add("gpsimd", lambda e, a=asrc: e.dma_start(out=attk[:], in_=a),
              waits=akw, inc=ctx.dw, k=16)
        ak_w = ctx.dw.at()
        for oc in range(DC):
            bank, bw, bi = ctx.pool.acquire()
            pairs = [(wbuf[:, k * D + oc * 128: k * D + (oc + 1) * 128],
                      attk[:, k * 512:(k + 1) * 512]) for k in range(DC)]
            mmw = emit_matmul_chain(ctx, bank[:, 0:512], pairs,
                                    waits=[w_o, ak_w] + bw)
            sl = osl[0] % 4
            osl[0] += 1
            sw = [mmw, w_bo] + ([oslot_free[sl]] if oslot_free[sl] else [])
            p.add("vector", lambda e, b=bank, sl=sl, oc=oc: e.tensor_scalar(
                out=stage[:, sl * 512:(sl + 1) * 512], in0=b[:, 0:512],
                scalar1=boc[:, oc:oc + 1], scalar2=None, op0=ALU.add),
                waits=sw, inc=ctx.ev)
            ctx.pool.release(bi, ctx.ev)
            evw = ctx.ev.at()
            p.add("gpsimd", lambda e, sl=sl, oc=oc, t4=t4: e.dma_start(
                out=XDST[:, oc * TL + t4 * 512: oc * TL + (t4 + 1) * 512],
                in_=stage[:, sl * 512:(sl + 1) * 512], accum_op=ALU.add),
                waits=[evw, xo_init], inc=ctx.dw, k=16)
            oslot_free[sl] = ctx.dw.at()
        attk_user[0] = ctx.mm.at()
    p.add("sync", None, waits=[ctx.dw.at()])


def _emit_b0(ctx):
    emit_l0_attn_b(ctx)



def emit_l1_wkv(ctx):
    XSRC = ctx.dram[ctx.xsrc]
    XDST = ctx.dram[ctx.xdst]
    p = ctx.p
    nc = ctx.nc
    p.add("gpsimd", lambda e: e.dma_start(out=XDST, in_=XSRC),
          inc=ctx.dw, k=16)
    xo_init = ctx.dw.at()

    xg = ctx.nsb("x_sb", [128, DC * TL], F32, side="right")
    x = xg.__enter__()
    p.add("sync", lambda e: e.dma_start(out=x[:], in_=XSRC),
          inc=ctx.dx, k=16)
    xw = ctx.dx.at()
    ones, cw = emit_consts(ctx)

    wbuf = ctx.sb("wbuf", [128, DC * D], BF16)
    stage = ctx.sb("stage", [128, 4 * 512], F32)
    stbf = ctx.sb("stbf", [128, 4 * 512], BF16)
    dcol = ctx.sb("dcol", [128, 8], F32)
    ucol = ctx.sb("ucol", [128, 8], F32)
    mkc = ctx.sb("mkc", [128, 8], F32)
    mvc = ctx.sb("mvc", [128, 8], F32)
    mrc = ctx.sb("mrc", [128, 8], F32)
    bmask = ctx.sb("bmask", [128, 8], F32)
    bprev = ctx.sb("bprev", [128, 8], F32)
    carry = ctx.sb("carry", [128, 4], F32)
    w_d = emit_weight_load(ctx, dcol[:], ctx.dram["w1_decay"])
    w_u = emit_weight_load(ctx, ucol[:], ctx.dram["w1_first"])
    w_mk = emit_weight_load(ctx, mkc[:], ctx.dram["w1_mk"])
    w_mv = emit_weight_load(ctx, mvc[:], ctx.dram["w1_mv"])
    w_mr = emit_weight_load(ctx, mrc[:], ctx.dram["w1_mr"])
    w_bm = emit_weight_load(ctx, bmask[:], ctx.dram["w1_bmask"])

    guards = []

    def salloc(name, shape, dtype):
        g = ctx.nsb(name, shape, dtype)
        t = g.__enter__()
        guards.append(g)
        return t

    xln = salloc("xln", [128, DC * TL], BF16)
    ctx.t["xln"] = xln

    ln_w = emit_ln(ctx, x, xln, TL, cw, "ln", x_wait=xw)
    xg.__exit__(None, None, None)

    bcs = salloc("bcs", [128, 8], F32)
    mixk = salloc("mixk", [128, DC * 512], BF16)
    mixv = salloc("mixv", [128, DC * 512], BF16)
    mixr = salloc("mixr", [128, DC * 512], BF16)
    dbuf = salloc("dbuf", [128, DC * 512], BF16)
    wv_sb = salloc("wv2_sb", [128, DC * D], BF16)
    wr_sb = salloc("wr_sb", [128, DC * D], BF16)

    bc_in = ctx.ndt("bc_in", [128, 8], F32)
    bc_out = ctx.ndt("bc_out", [256, 8], F32)
    kv_in = ctx.ndt("kv_in", [2 * D, TL], BF16)
    kv_out = ctx.ndt("kv_out", [4 * D, TL], BF16)
    r_d = ctx.ndt("r_d", [D, TL], BF16)
    wkv_d = ctx.ndt("wkv_d", [D, T], BF16)

    for k in range(DC):
        p.add("vector", lambda e, k=k: e.tensor_copy(
            out=bcs[:, k:k + 1], in_=xln[:, k * TL + TL - 1: k * TL + TL]),
            waits=[ln_w] if k == 0 else (), inc=ctx.ev)
    p.add("sync", lambda e: e.dma_start(out=bc_in, in_=bcs[:]),
          waits=[ctx.ev.at()], inc=ctx.dx, k=16)
    p.add("gpsimd", lambda e: e.collective_compute(
        "AllGather", ALU.bypass, replica_groups=PAIRS,
        ins=[bc_in.opt()], outs=[bc_out.opt()]),
        waits=[ctx.dx.at()], inc=ctx.cc)
    p.add("sync", lambda e: e.dma_start(out=bprev[:], in_=bc_out[0:128, :]),
          waits=[ctx.cc.at()], inc=ctx.dx, k=16)
    p.add("vector", lambda e: e.tensor_mul(out=bprev[:], in0=bprev[:], in1=bmask[:]),
          waits=[ctx.dx.at(), w_bm], inc=ctx.ev)
    bprev_w = ctx.ev.at()

    w_wk = emit_weight_load(ctx, wbuf[:], chunked(ctx.dram["w1_wk"]))
    w_wv = emit_weight_load(ctx, wv_sb[:], chunked(ctx.dram["w1_wv"]))
    w_wr = emit_weight_load(ctx, wr_sb[:], chunked(ctx.dram["w1_wr"]))

    slot_i = [0]
    slot_free = [None] * 4
    mix_user = [None]

    def ship(tgt, r0, t0, bank, waits, act=None):
        sl = slot_i[0] % 4
        slot_i[0] += 1
        w2 = list(waits) + ([slot_free[sl]] if slot_free[sl] else [])
        if act is None:
            p.add("vector", lambda e, b=bank, sl=sl: e.tensor_copy(
                out=stbf[:, sl * 512: sl * 512 + 512], in_=b[:, 0:512]),
                waits=w2, inc=ctx.ev)
            dwn = ctx.ev.at()
            sem = ctx.ev
        else:
            p.add("scalar", lambda e, b=bank, sl=sl: e.activation(
                out=stbf[:, sl * 512: sl * 512 + 512], in_=b[:, 0:512], func=act),
                waits=w2, inc=ctx.es)
            dwn = ctx.es.at()
            sem = ctx.es
        p.add("sync", lambda e, sl=sl, tgt=tgt, r0=r0, t0=t0: e.dma_start(
            out=tgt[r0:r0 + 128, t0:t0 + 512],
            in_=stbf[:, sl * 512: sl * 512 + 512]),
            waits=[dwn], inc=ctx.dx, k=16)
        slot_free[sl] = ctx.dx.at()
        return sem

    for ti in [1, 2, 3, 0]:
        t0 = ti * 512
        mu = ([mix_user[0]] if mix_user[0] else [])
        for k in range(DC):
            lo = t0 - 1 if ti > 0 else 0
            span = 512 if ti > 0 else 511
            od = 0 if ti > 0 else 1
            p.add("vector", lambda e, k=k, t0=t0, lo=lo, span=span, od=od: e.tensor_sub(
                out=dbuf[:, k * 512 + od: k * 512 + od + span],
                in0=xln[:, k * TL + t0 + od: k * TL + t0 + 512],
                in1=xln[:, k * TL + lo: k * TL + lo + span]),
                waits=(mu + [ln_w]) if k == 0 else (), inc=ctx.ev)
            for mx, mc in ((mixk, mkc), (mixv, mvc), (mixr, mrc)):
                p.add("vector", lambda e, k=k, t0=t0, lo=lo, span=span, od=od, mx=mx, mc=mc:
                      e.scalar_tensor_tensor(
                          out=mx[:, k * 512 + od: k * 512 + od + span],
                          in0=dbuf[:, k * 512 + od: k * 512 + od + span],
                          scalar=mc[:, k:k + 1],
                          in1=xln[:, k * TL + lo: k * TL + lo + span],
                          op0=ALU.mult, op1=ALU.add),
                      waits=[w_mk, w_mv, w_mr] if k == 0 else (), inc=ctx.ev)
            if ti == 0:
                p.add("vector", lambda e, k=k: e.tensor_sub(
                    out=dbuf[:, k * 512: k * 512 + 1],
                    in0=xln[:, k * TL: k * TL + 1], in1=bprev[:, k:k + 1]),
                    waits=[bprev_w], inc=ctx.ev)
                for mx, mc in ((mixk, mkc), (mixv, mvc), (mixr, mrc)):
                    p.add("vector", lambda e, k=k, mx=mx, mc=mc: e.scalar_tensor_tensor(
                        out=mx[:, k * 512: k * 512 + 1],
                        in0=dbuf[:, k * 512: k * 512 + 1],
                        scalar=mc[:, k:k + 1], in1=bprev[:, k:k + 1],
                        op0=ALU.mult, op1=ALU.add), inc=ctx.ev)
        mix_w = ctx.ev.at()
        for (wsb, wld, mx, tgt, r00, act) in (
                (wbuf, w_wk, mixk, kv_in, 0, None),
                (wv_sb, w_wv, mixv, kv_in, D, None),
                (wr_sb, w_wr, mixr, r_d, 0, AF.Sigmoid)):
            for oc in range(DC):
                bank, bw, bi = ctx.pool.acquire()
                pairs = [(wsb[:, k * D + oc * 128: k * D + (oc + 1) * 128],
                          mx[:, k * 512:(k + 1) * 512]) for k in range(DC)]
                mmw = emit_matmul_chain(ctx, bank[:, 0:512], pairs,
                                       waits=[wld, mix_w] + bw)
                sem = ship(tgt, r00 + oc * 128, t0, bank, [mmw], act)
                ctx.pool.release(bi, sem)
        mix_user[0] = ctx.mm.at()
    p.add("gpsimd", lambda e: e.collective_compute(
        "AllGather", ALU.bypass, replica_groups=PAIRS,
        ins=[kv_in.opt()], outs=[kv_out.opt()]),
        waits=[ctx.dx.at()], inc=ctx.cc)
    kv_cc = ctx.cc.at()

    for g in reversed(guards):
        g.__exit__(None, None, None)
    TH = TL
    guards2 = []

    def salloc2(name, shape, dtype):
        g = ctx.nsb(name, shape, dtype)
        t = g.__enter__()
        guards2.append(g)
        return t

    S = [salloc2(f"scan{i}", [128, TH + 2], F32) for i in range(8)]
    ktile = salloc2("ktile", [128, TH], BF16)
    vtile = salloc2("vtile", [128, TH], BF16)
    wkvbf = salloc2("wkvbf", [128, TH], BF16)

    for cc2 in range(8):
        for th2 in range(2):
            tb = th2 * TH
            bwv = [ctx.ev.at(), ctx.es.at(), ctx.gp.at(), ctx.dx.at(), ctx.mm.at()]
            if th2 == 0:
                p.add("vector", lambda e: e.memset(carry[:, 0:1], NEG), waits=bwv, inc=ctx.ev)
                p.add("vector", lambda e: e.memset(carry[:, 1:3], 0.0), inc=ctx.ev)
            p.add("gpsimd", lambda e, tb=tb: e.iota(
                S[0][:, 0:TH], [[1, TH]], base=tb, channel_multiplier=0,
                allow_small_or_imprecise_dtypes=True), waits=bwv, inc=ctx.gp)
            p.add("vector", lambda e, cc2=cc2: e.tensor_scalar(
                out=S[0][:, 0:TH], in0=S[0][:, 0:TH], scalar1=dcol[:, cc2:cc2 + 1],
                scalar2=None, op0=ALU.mult), waits=bwv + [ctx.gp.at(), w_d], inc=ctx.ev)
            p.add("sync", lambda e, th2=th2, cc2=cc2: e.dma_start(
                out=ktile[:],
                in_=kv_out[th2 * 2 * D + cc2 * 128: th2 * 2 * D + (cc2 + 1) * 128, :]),
                waits=bwv + [kv_cc], inc=ctx.dx, k=16)
            p.add("sync", lambda e, th2=th2, cc2=cc2: e.dma_start(
                out=vtile[:],
                in_=kv_out[th2 * 2 * D + D + cc2 * 128: th2 * 2 * D + D + (cc2 + 1) * 128, :]),
                waits=bwv + [kv_cc], inc=ctx.dx, k=16)
            kvw = ctx.dx.at()
            p.add("vector", lambda e, cc2=cc2: e.scalar_tensor_tensor(
                out=S[1][:, 0:TH], in0=ktile[:], scalar=dcol[:, cc2:cc2 + 1],
                in1=S[0][:, 0:TH], op0=ALU.subtract, op1=ALU.subtract),
                waits=[kvw], inc=ctx.ev)
            p.add("vector", lambda e, cc2=cc2: e.tensor_scalar(
                out=S[4][:, 0:TH], in0=ktile[:], scalar1=ucol[:, cc2:cc2 + 1],
                scalar2=None, op0=ALU.add), waits=[kvw, w_u], inc=ctx.ev)
            p.add("vector", lambda e: e.memset(S[3][:, 0:TH], 0.0), inc=ctx.ev)
            p.add("vector", lambda e: e.tensor_copy(
                out=S[2][:, 0:1], in_=carry[:, 0:1]), inc=ctx.ev)
            p.add("vector", lambda e: e.tensor_tensor_scan(
                out=S[2][:, 1:TH + 1], data0=S[1][:, 0:TH], data1=S[3][:, 0:TH],
                initial=carry[:, 0:1], op0=ALU.max, op1=ALU.add),
                waits=[ctx.ev.at()], inc=ctx.ev)
            p.add("vector", lambda e: e.tensor_add(
                out=S[0][:, 0:TH], in0=S[0][:, 0:TH], in1=S[2][:, 0:TH]), inc=ctx.ev)
            p.add("vector", lambda e: e.tensor_sub(
                out=S[3][:, 0:TH], in0=S[2][:, 0:TH], in1=S[2][:, 1:TH + 1]), inc=ctx.ev)
            p.add("scalar", lambda e: e.activation(
                out=S[3][:, 0:TH], in_=S[3][:, 0:TH], func=AF.Exp),
                waits=[ctx.ev.at()], inc=ctx.es)
            p.add("vector", lambda e: e.tensor_sub(
                out=S[1][:, 0:TH], in0=S[1][:, 0:TH], in1=S[2][:, 1:TH + 1]), inc=ctx.ev)
            p.add("scalar", lambda e: e.activation(
                out=S[1][:, 0:TH], in_=S[1][:, 0:TH], func=AF.Exp),
                waits=[ctx.ev.at()], inc=ctx.es)
            ed_w = ctx.es.at()
            p.add("vector", lambda e: e.tensor_mul(
                out=S[5][:, 0:TH], in0=vtile[:], in1=S[1][:, 0:TH]),
                waits=[ed_w], inc=ctx.ev)
            p.add("vector", lambda e: e.tensor_copy(
                out=S[6][:, 0:1], in_=carry[:, 1:2]), inc=ctx.ev)
            p.add("vector", lambda e: e.tensor_tensor_scan(
                out=S[6][:, 1:TH + 1], data0=S[3][:, 0:TH], data1=S[5][:, 0:TH],
                initial=carry[:, 1:2], op0=ALU.mult, op1=ALU.add),
                waits=[ctx.ev.at()], inc=ctx.ev)
            p.add("vector", lambda e: e.tensor_copy(
                out=S[7][:, 0:1], in_=carry[:, 2:3]), inc=ctx.ev)
            p.add("vector", lambda e: e.tensor_tensor_scan(
                out=S[7][:, 1:TH + 1], data0=S[3][:, 0:TH], data1=S[1][:, 0:TH],
                initial=carry[:, 2:3], op0=ALU.mult, op1=ALU.add),
                waits=[ctx.ev.at()], inc=ctx.ev)
            if th2 == 0:
                p.add("vector", lambda e: e.tensor_copy(
                    out=carry[:, 0:1], in_=S[2][:, TH:TH + 1]), inc=ctx.ev)
                p.add("vector", lambda e: e.tensor_copy(
                    out=carry[:, 1:2], in_=S[6][:, TH:TH + 1]), inc=ctx.ev)
                p.add("vector", lambda e: e.tensor_copy(
                    out=carry[:, 2:3], in_=S[7][:, TH:TH + 1]), inc=ctx.ev)
            p.add("vector", lambda e: e.tensor_sub(
                out=S[2][:, 0:TH], in0=S[0][:, 0:TH], in1=S[4][:, 0:TH]), inc=ctx.ev)
            p.add("vector", lambda e: e.tensor_scalar(
                out=S[0][:, 0:TH], in0=S[2][:, 0:TH], scalar1=0.0, scalar2=None,
                op0=ALU.min), inc=ctx.ev)
            p.add("vector", lambda e: e.tensor_scalar(
                out=S[4][:, 0:TH], in0=S[2][:, 0:TH], scalar1=-1.0, scalar2=0.0,
                op0=ALU.mult, op1=ALU.min), inc=ctx.ev)
            p.add("scalar", lambda e: e.activation(
                out=S[0][:, 0:TH], in_=S[0][:, 0:TH], func=AF.Exp),
                waits=[ctx.ev.at()], inc=ctx.es)
            p.add("scalar", lambda e: e.activation(
                out=S[4][:, 0:TH], in_=S[4][:, 0:TH], func=AF.Exp), inc=ctx.es)
            e12_w = ctx.es.at()
            p.add("vector", lambda e: e.tensor_mul(
                out=S[2][:, 0:TH], in0=S[6][:, 0:TH], in1=S[0][:, 0:TH]),
                waits=[e12_w], inc=ctx.ev)
            p.add("vector", lambda e: e.tensor_mul(
                out=S[3][:, 0:TH], in0=vtile[:], in1=S[4][:, 0:TH]), inc=ctx.ev)
            p.add("vector", lambda e: e.tensor_add(
                out=S[2][:, 0:TH], in0=S[2][:, 0:TH], in1=S[3][:, 0:TH]), inc=ctx.ev)
            p.add("vector", lambda e: e.tensor_mul(
                out=S[3][:, 0:TH], in0=S[7][:, 0:TH], in1=S[0][:, 0:TH]), inc=ctx.ev)
            p.add("vector", lambda e: e.scalar_tensor_tensor(
                out=S[3][:, 0:TH], in0=S[3][:, 0:TH], scalar=1e-9, in1=S[4][:, 0:TH],
                op0=ALU.add, op1=ALU.add), inc=ctx.ev)
            p.add("vector", lambda e: e.reciprocal(
                out=S[3][:, 0:TH], in_=S[3][:, 0:TH]), inc=ctx.ev)
            p.add("vector", lambda e: e.tensor_mul(
                out=wkvbf[:], in0=S[2][:, 0:TH], in1=S[3][:, 0:TH]), inc=ctx.ev)
            p.add("sync", lambda e, cc2=cc2, tb=tb: e.dma_start(
                out=wkv_d[cc2 * 128:(cc2 + 1) * 128, tb:tb + TH], in_=wkvbf[:]),
                waits=[ctx.ev.at()], inc=ctx.dx, k=16)
    wkv_ship = ctx.dx.at()

    for g in reversed(guards2):
        g.__exit__(None, None, None)
    wkv_loc = ctx.stack.enter_context(
        ctx.nsb("wkv_loc", [128, DC * TL], BF16))
    h0 = ctx.stack.enter_context(
        ctx.nsb("wl0", [128, DC * TL], BF16))
    r_loc = ctx.stack.enter_context(
        ctx.nsb("r_loc", [128, DC * TL], BF16))
    a0_ = ap3(wkv_d, 0, [[T, 128], [128 * T, DC], [1, TL]])
    a1_ = ap3(wkv_d, TL, [[T, 128], [128 * T, DC], [1, TL]])
    p.add("sync", lambda e, a=a0_: e.dma_start(out=h0[:], in_=a),
          waits=[wkv_ship], inc=ctx.dx, k=16)
    p.add("sync", lambda e, a=a1_: e.dma_start(out=wkv_loc[:], in_=a),
          waits=[wkv_ship], inc=ctx.dx, k=16)
    p.add("sync", lambda e: e.dma_start(out=r_loc[:], in_=chunked(r_d)),
          inc=ctx.dx, k=16)
    wl_w = ctx.dx.at()
    for k in range(DC):
        p.add("vector", lambda e, k=k: e.tensor_sub(
            out=wkv_loc[:, k * TL:(k + 1) * TL], in0=wkv_loc[:, k * TL:(k + 1) * TL],
            in1=h0[:, k * TL:(k + 1) * TL]),
            waits=[wl_w] if k == 0 else (), inc=ctx.ev)
        p.add("vector", lambda e, k=k: e.scalar_tensor_tensor(
            out=wkv_loc[:, k * TL:(k + 1) * TL],
            in0=wkv_loc[:, k * TL:(k + 1) * TL], scalar=bmask[:, 0:1],
            in1=h0[:, k * TL:(k + 1) * TL], op0=ALU.mult, op1=ALU.add), inc=ctx.ev)
        p.add("vector", lambda e, k=k: e.tensor_mul(
            out=wkv_loc[:, k * TL:(k + 1) * TL],
            in0=wkv_loc[:, k * TL:(k + 1) * TL],
            in1=r_loc[:, k * TL:(k + 1) * TL]), inc=ctx.ev)
    rwkv_w = ctx.ev.at()
    w_wo = emit_weight_load(ctx, wbuf[:], chunked(ctx.dram["w1_wo"]),
                            waits=[ctx.mm.at()])
    oslot = [0]
    ofree = [None] * 4
    for oc in range(DC):
        for t0 in range(0, TL, 512):
            bank, bw, bi = ctx.pool.acquire()
            pairs = [(wbuf[:, k * D + oc * 128: k * D + (oc + 1) * 128],
                      wkv_loc[:, k * TL + t0: k * TL + t0 + 512]) for k in range(DC)]
            mmw = emit_matmul_chain(ctx, bank[:, 0:512], pairs,
                                    waits=[w_wo, rwkv_w] + bw)
            sl = oslot[0] % 4
            oslot[0] += 1
            sw = [mmw] + ([ofree[sl]] if ofree[sl] else [])
            p.add("scalar", lambda e, b=bank, sl=sl: e.activation(
                out=stage[:, sl * 512:(sl + 1) * 512], in_=b[:, 0:512],
                func=AF.Identity), waits=sw, inc=ctx.es)
            ctx.pool.release(bi, ctx.es)
            evw = ctx.es.at()
            p.add("gpsimd", lambda e, sl=sl, oc=oc, t0=t0: e.dma_start(
                out=XDST[:, oc * TL + t0: oc * TL + t0 + 512],
                in_=stage[:, sl * 512:(sl + 1) * 512], accum_op=ALU.add),
                waits=[evw, xo_init], inc=ctx.dw, k=16)
            ofree[sl] = ctx.dw.at()
    p.add("sync", None, waits=[ctx.dw.at()])


def _emit_w1(ctx):
    emit_l1_wkv(ctx)



def emit_l1_ffn(ctx):
    XSRC = ctx.dram[ctx.xsrc]
    XDST = ctx.dram[ctx.xdst]
    p = ctx.p
    nc = ctx.nc
    E = 16
    p.add("gpsimd", lambda e: e.dma_start(out=XDST, in_=XSRC),
          inc=ctx.dw, k=16)
    xo_init = ctx.dw.at()

    xg = ctx.nsb("x_sb", [128, DC * TL], F32, side="right")
    x = xg.__enter__()
    p.add("sync", lambda e: e.dma_start(out=x[:], in_=XSRC),
          inc=ctx.dx, k=16)
    xw = ctx.dx.at()
    ones, cw = emit_consts(ctx)
    xln = ctx.sb("xln", [128, DC * TL], BF16)
    ln_w = emit_ln(ctx, x, xln, TL, cw, "ln", x_wait=xw)
    xg.__exit__(None, None, None)

    gw_sb = ctx.sb("gw_sb", [128, DC * E], BF16)
    w_gw = emit_weight_load(ctx, gw_sb[:], chunked(ctx.dram["f1_gw"]))
    ebr = ctx.sb("ebr", [128, E], F32)
    w_eb = emit_weight_load(ctx, ebr[:], ctx.dram["f1_eb"])
    gtok = ctx.sb("gtok", [128, (TL // 128) * E], BF16)
    gA = ctx.sb("gA", [128, E], F32)
    gB = ctx.sb("gB", [128, E], F32)
    gS = ctx.sb("gS", [128, E], F32)
    gcol = ctx.sb("gcol", [128, 4], F32)
    idbf = ctx.sb("idbf", [128, 128], BF16)
    lnmu_t = ctx.t["lnmu"]
    p.add("gpsimd", lambda e: e.iota(
        lnmu_t[:, 0:128], [[1, 128]], base=0, channel_multiplier=-1,
        allow_small_or_imprecise_dtypes=True), waits=[ln_w], inc=ctx.gp)
    p.add("vector", lambda e: e.tensor_scalar(
        out=idbf[:], in0=lnmu_t[:, 0:128], scalar1=0.0, scalar2=None,
        op0=ALU.is_equal), waits=[ctx.gp.at()], inc=ctx.ev)

    for tc in range(TL // 128):
        bank, bw, bi = ctx.pool.acquire()
        pairs = [(xln[:, k * TL + tc * 128: k * TL + (tc + 1) * 128],
                  gw_sb[:, k * E:(k + 1) * E]) for k in range(DC)]
        mmw = emit_matmul_chain(ctx, bank[:, 0:E], pairs, waits=[w_gw, ln_w] + bw)

        def gser(eng, fn):
            p.add(eng, fn, waits=[ctx.ev.at(), ctx.es.at(), ctx.mm.at(), w_eb],
                  inc=(ctx.ev if eng == "vector" else ctx.es))
        gser("scalar", lambda e, b=bank: e.activation(
            out=gS[:], in_=b[:, 0:E], func=AF.Sigmoid))
        gser("vector", lambda e, b=bank: e.tensor_add(
            out=gB[:], in0=b[:, 0:E], in1=ebr[:]))
        gser("vector", lambda e: e.reduce_max(
            out=gcol[:, 0:1], in_=gB[:], axis=mybir.AxisListType.X))
        gser("vector", lambda e: e.tensor_scalar(
            out=gA[:], in0=gB[:], scalar1=gcol[:, 0:1], scalar2=None, op0=ALU.is_ge))
        gser("vector", lambda e: e.scalar_tensor_tensor(
            out=gB[:], in0=gA[:], scalar=-1e30, in1=gB[:], op0=ALU.mult, op1=ALU.add))
        gser("vector", lambda e: e.reduce_max(
            out=gcol[:, 1:2], in_=gB[:], axis=mybir.AxisListType.X))
        gser("vector", lambda e: e.tensor_scalar(
            out=gB[:], in0=gB[:], scalar1=gcol[:, 1:2], scalar2=None, op0=ALU.is_ge))
        gser("vector", lambda e: e.tensor_add(out=gA[:], in0=gA[:], in1=gB[:]))
        gser("vector", lambda e: e.tensor_mul(out=gA[:], in0=gA[:], in1=gS[:]))
        gser("vector", lambda e: e.reduce_sum(
            out=gcol[:, 2:3], in_=gA[:], axis=mybir.AxisListType.X))
        gser("vector", lambda e: e.tensor_scalar(
            out=gcol[:, 2:3], in0=gcol[:, 2:3], scalar1=1e-10, scalar2=None,
            op0=ALU.add))
        gser("vector", lambda e: e.reciprocal(out=gcol[:, 2:3], in_=gcol[:, 2:3]))
        gser("vector", lambda e, tc=tc: e.tensor_scalar(
            out=gtok[:, tc * E:(tc + 1) * E], in0=gA[:], scalar1=gcol[:, 2:3],
            scalar2=None, op0=ALU.mult))
        ctx.pool.release(bi, ctx.ev)
    gtok_w = ctx.ev.at()

    ptr_bf = ctx.stack.enter_context(nc.psum_tensor(ctx.tag + "ptr_bf", [128, 1024], BF16))
    gfm = ctx.sb("gfm", [128, TL], BF16)
    grow = ctx.sb("grow", [1, 1024], BF16)
    grow_user = [None]
    for tc in range(TL // 128):
        p.add("tensor", lambda e, tc=tc: e.transpose(
            ptr_bf[0:E, 0:128], gtok[:, tc * E:(tc + 1) * E], idbf[:]),
            waits=[gtok_w] if tc == 0 else [ctx.ev.at()], inc=ctx.mm)
        p.add("vector", lambda e, tc=tc: e.tensor_copy(
            out=gfm[0:E, tc * 128:(tc + 1) * 128], in_=ptr_bf[0:E, 0:128]),
            waits=[ctx.mm.at()], inc=ctx.ev)
    gfm_w = ctx.ev.at()

    b1c = ctx.sb("b1c", [128, E * 4], F32)
    b3c = ctx.sb("b3c", [128, E * 4], F32)
    b2sb = ctx.sb("b2sb", [128, D], BF16)
    sb1c = ctx.sb("sb1c", [128, 8], F32)
    sb3c = ctx.sb("sb3c", [128, 8], F32)
    sb2c = ctx.sb("sb2c", [128, 8], F32)
    w_b1 = emit_weight_load(ctx, b1c[:], ap3(ctx.dram["f1_b1"], 0,
                            [[4, 128], [512, E], [1, 4]]))
    w_b3 = emit_weight_load(ctx, b3c[:], ap3(ctx.dram["f1_b3"], 0,
                            [[4, 128], [512, E], [1, 4]]))
    w_b2 = emit_weight_load(ctx, b2sb[0:16, :], ctx.dram["f1_b2"])
    w_s1 = emit_weight_load(ctx, sb1c[:], ctx.dram["f1_sb1"])
    w_s3 = emit_weight_load(ctx, sb3c[:], ctx.dram["f1_sb3"])
    w_s2 = emit_weight_load(ctx, sb2c[:], ctx.dram["f1_sb2"])

    hid = ctx.sb("hid", [128, 8 * 1024], BF16)
    w1b = ctx.sb("w1b", [128, 2 * DC * 512], BF16)
    w3b = ctx.sb("w3b", [128, 2 * DC * 512], BF16)
    w2bu = ctx.sb("w2bu", [128, 2 * 4 * D], BF16)
    wS = ctx.sb("wS", [128, DC * D], BF16)
    grep = ctx.sb("grep", [128, 1024], F32)
    stile = ctx.sb("stile", [128, 2 * 512], BF16)
    stage = ctx.sb("stage", [128, 4 * 512], F32)
    oslot = [0]
    ofree = [None] * 4
    slot_user = [None, None]
    wS_user = [None]
    hid_user = [None]

    def run_f1_expert(e_idx, th, shared, add_b2=False):
        NDF = 8 if shared else 4
        grep_w = ctx.ev.at()
        if not shared:
            gw2 = [gfm_w] + ([grow_user[0]] if grow_user[0] else [])
            p.add("sync", lambda e, e_idx=e_idx, th=th: e.dma_start(
                out=grow[0:1, 0:1024],
                in_=gfm[e_idx:e_idx + 1, th * 1024:(th + 1) * 1024]),
                waits=gw2, inc=ctx.dx, k=16)
            growd = ctx.dx.at()
            for i in range(2):
                bank, bw, bi = ctx.pool.acquire()
                p.add("tensor", lambda e, b=bank, i=i: e.matmul(
                    b[:, 0:512], ones[0:1, 0:128], grow[0:1, i * 512:(i + 1) * 512],
                    start=True, stop=True), waits=[growd, cw] + bw, inc=ctx.mm)
                p.add("vector", lambda e, b=bank, i=i: e.tensor_copy(
                    out=grep[:, i * 512:(i + 1) * 512], in_=b[:, 0:512]),
                    waits=[ctx.mm.at()], inc=ctx.ev)
                ctx.pool.release(bi, ctx.ev)
            grow_user[0] = ctx.mm.at()
            grep_w = ctx.ev.at()
        if shared:
            lw = [wS_user[0]] if wS_user[0] else []
            wl1 = emit_weight_load(
                ctx, wS[:], ap3(ctx.dram["f1_sw1"], 0,
                                [[D, 128], [128 * D, DC], [1, D]]), waits=lw)
            w1q = wS[:]
            w3q = None
            w2q = None
        else:
            sl = e_idx % 2
            lw = [slot_user[sl]] if slot_user[sl] else []
            wl1 = emit_weight_load(
                ctx, w1b[:, sl * DC * 512:(sl + 1) * DC * 512],
                ap3(ctx.dram["f1_w1"], e_idx * D * 512,
                    [[512, 128], [128 * 512, DC], [1, 512]]), waits=lw)
            wl3 = emit_weight_load(
                ctx, w3b[:, sl * DC * 512:(sl + 1) * DC * 512],
                ap3(ctx.dram["f1_w3"], e_idx * D * 512,
                    [[512, 128], [128 * 512, DC], [1, 512]]), waits=lw)
            wl2 = emit_weight_load(
                ctx, w2bu[:, sl * 4 * D:(sl + 1) * 4 * D],
                ap3(ctx.dram["f1_w2"], e_idx * 512 * D,
                    [[D, 128], [128 * D, 4], [1, D]]), waits=lw)
            w1q = w1b[:, sl * DC * 512:(sl + 1) * DC * 512]
            w3q = w3b[:, sl * DC * 512:(sl + 1) * DC * 512]
            w2q = w2bu[:, sl * 4 * D:(sl + 1) * 4 * D]
        sfree = [None, None]
        for dc in range(NDF):
            for t5 in range(2):
                tog = t5
                xcol = th * 1024 + t5 * 512
                bank, bw, bi = ctx.pool.acquire()
                pairs = [(w1q[:, k * NDF * 128 + dc * 128: k * NDF * 128 + (dc + 1) * 128],
                          xln[:, k * TL + xcol: k * TL + xcol + 512])
                         for k in range(DC)]
                mmw = emit_matmul_chain(ctx, bank[:, 0:512], pairs,
                                        waits=[wl1, ln_w] + bw)
                bcol = (sb1c[:, dc:dc + 1] if shared
                        else b1c[:, e_idx * 4 + dc: e_idx * 4 + dc + 1])
                sw = [mmw, w_b1, w_s1] + ([sfree[tog]] if sfree[tog] else [])
                p.add("scalar", lambda e, b=bank, tog=tog, bc=bcol: e.activation(
                    out=stile[:, tog * 512:(tog + 1) * 512], in_=b[:, 0:512],
                    func=AF.Silu, bias=bc), waits=sw, inc=ctx.es)
                ctx.pool.release(bi, ctx.es)
                hcol = dc * 1024 + t5 * 512
                hw = [ctx.es.at()] + ([hid_user[0]] if hid_user[0] else [])
                if shared:
                    p.add("vector", lambda e, hcol=hcol, tog=tog: e.tensor_copy(
                        out=hid[:, hcol:hcol + 512],
                        in_=stile[:, tog * 512:(tog + 1) * 512]),
                        waits=hw, inc=ctx.ev)
                else:
                    p.add("vector", lambda e, hcol=hcol, tog=tog, t5=t5: e.tensor_mul(
                        out=hid[:, hcol:hcol + 512],
                        in0=stile[:, tog * 512:(tog + 1) * 512],
                        in1=grep[:, t5 * 512:(t5 + 1) * 512]),
                        waits=hw + [grep_w], inc=ctx.ev)
                sfree[tog] = ctx.ev.at()
        if shared:
            ph1 = ctx.mm.at()
            wl3 = emit_weight_load(
                ctx, wS[:], ap3(ctx.dram["f1_sw3"], 0,
                                [[D, 128], [128 * D, DC], [1, D]]), waits=[ph1])
            w3q = wS[:]
        for dc in range(NDF):
            for t5 in range(2):
                xcol = th * 1024 + t5 * 512
                hcol = dc * 1024 + t5 * 512
                bank, bw, bi = ctx.pool.acquire()
                pairs = [(w3q[:, k * NDF * 128 + dc * 128: k * NDF * 128 + (dc + 1) * 128],
                          xln[:, k * TL + xcol: k * TL + xcol + 512])
                         for k in range(DC)]
                mmw = emit_matmul_chain(ctx, bank[:, 0:512], pairs,
                                        waits=[wl3, ln_w] + bw)
                bcol = (sb3c[:, dc:dc + 1] if shared
                        else b3c[:, e_idx * 4 + dc: e_idx * 4 + dc + 1])
                p.add("vector", lambda e, b=bank, hcol=hcol, bc=bcol: e.scalar_tensor_tensor(
                    out=hid[:, hcol:hcol + 512], in0=b[:, 0:512], scalar=bc,
                    in1=hid[:, hcol:hcol + 512], op0=ALU.add, op1=ALU.mult),
                    waits=[mmw, w_b3, w_s3], inc=ctx.ev)
                ctx.pool.release(bi, ctx.ev)
        hid_w = ctx.ev.at()
        if shared:
            ph2 = ctx.mm.at()
            wl2 = emit_weight_load(
                ctx, wS[:], ap3(ctx.dram["f1_sw2"], 0,
                                [[D, 128], [128 * D, DC], [1, D]]), waits=[ph2])
            w2q = wS[:]
        for oc in range(DC):
            for t5 in range(2):
                tcol = th * 1024 + t5 * 512
                bank, bw, bi = ctx.pool.acquire()
                nk = 8 if shared else 4
                pairs = [(w2q[:, kc * D + oc * 128: kc * D + (oc + 1) * 128],
                          hid[:, kc * 1024 + t5 * 512: kc * 1024 + t5 * 512 + 512])
                         for kc in range(nk)]
                if not add_b2:
                    mmw = emit_matmul_chain(ctx, bank[:, 0:512], pairs,
                                            waits=[wl2, hid_w] + bw)
                else:
                    for i, (lh, rh) in enumerate(pairs):
                        w0 = [wl2, hid_w] + bw if i == 0 else ()
                        p.add("tensor",
                              lambda e, o=bank[:, 0:512], a=lh, b=rh, st=(i == 0): e.matmul(
                                  o, a, b, start=st, stop=False),
                              waits=w0)
                    p.add("tensor", lambda e, b=bank, oc=oc, tcol=tcol: e.matmul(
                        b[:, 0:512], b2sb[0:16, oc * 128:(oc + 1) * 128],
                        gfm[0:16, tcol:tcol + 512], start=False, stop=True),
                        waits=[w_b2, gfm_w], inc=ctx.mm)
                    mmw = ctx.mm.at()
                sl2 = oslot[0] % 4
                oslot[0] += 1
                sw = [mmw] + ([ofree[sl2]] if ofree[sl2] else [])
                if shared:
                    p.add("vector", lambda e, b=bank, sl2=sl2, oc=oc: e.tensor_scalar(
                        out=stage[:, sl2 * 512:(sl2 + 1) * 512], in0=b[:, 0:512],
                        scalar1=sb2c[:, oc:oc + 1], scalar2=None, op0=ALU.add),
                        waits=sw + [w_s2], inc=ctx.ev)
                else:
                    p.add("scalar", lambda e, b=bank, sl2=sl2: e.activation(
                        out=stage[:, sl2 * 512:(sl2 + 1) * 512], in_=b[:, 0:512],
                        func=AF.Identity), waits=sw, inc=ctx.es)
                ctx.pool.release(bi, ctx.es)
                p.add("gpsimd", lambda e, sl2=sl2, oc=oc, tcol=tcol, XD=XDST: e.dma_start(
                    out=XD[:, oc * TL + tcol: oc * TL + tcol + 512],
                    in_=stage[:, sl2 * 512:(sl2 + 1) * 512], accum_op=ALU.add),
                    waits=[ctx.es.at(), ctx.ev.at(), xo_init], inc=ctx.dw, k=16)
                ofree[sl2] = ctx.dw.at()
        if shared:
            wS_user[0] = ctx.mm.at()
        else:
            slot_user[e_idx % 2] = ctx.mm.at()
        hid_user[0] = ctx.mm.at()

    for th in range(2):
        nsh = [e for e in F1_EXPERTS if e != 99]
        for e_idx in F1_EXPERTS:
            if e_idx == 99:
                run_f1_expert(0, th, True)
            else:
                run_f1_expert(e_idx, th, False,
                              add_b2=(nsh and e_idx == nsh[-1]))
    p.add("sync", None, waits=[ctx.dw.at(), ctx.dx.at()])


def _emit_f1(ctx):
    emit_l1_ffn(ctx)



def emit_l1_attn_b(ctx):
    XSRC = ctx.dram[ctx.xsrc]
    XDST = ctx.dram[ctx.xdst]
    p = ctx.p
    nc = ctx.nc
    p.add("gpsimd", lambda e: e.dma_start(out=XDST, in_=XSRC),
          inc=ctx.dw, k=16)
    xo_init = ctx.dw.at()

    xg = ctx.nsb("x_sb", [128, DC * TL], F32, side="right")
    x = xg.__enter__()
    p.add("sync", lambda e: e.dma_start(out=x[:], in_=XSRC),
          inc=ctx.dx, k=16)
    xw = ctx.dx.at()
    ones, cw = emit_consts(ctx)
    xln = ctx.sb("xln", [128, DC * TL], BF16)
    ln_w = emit_ln(ctx, x, xln, TL, cw, "ln", x_wait=xw)
    xg.__exit__(None, None, None)

    nwc = ctx.sb("nwc", [128, 8], F32)
    nbc = ctx.sb("nbc", [128, 8], F32)
    bqr = ctx.sb("bqr", [128, D], BF16)
    bkr = ctx.sb("bkr", [128, D], F32)
    bvc = ctx.sb("bvc", [128, 8], F32)
    pbc = ctx.sb("pbc", [128, 8], F32)
    w_nw = emit_weight_load(ctx, nwc[:], ctx.dram["b1_nw"])
    w_nb = emit_weight_load(ctx, nbc[:], ctx.dram["b1_nb"])
    w_bq = emit_weight_load(ctx, bqr[:], ctx.dram["b1_bq"])
    w_bk = emit_weight_load(ctx, bkr[:], ctx.dram["b1_bk"])
    w_bv = emit_weight_load(ctx, bvc[:], ctx.dram["b1_bv"])
    w_pb = emit_weight_load(ctx, pbc[:], ctx.dram["b1_pb"])
    bkr16 = ctx.sb("bkr16", [128, D], BF16)
    p.add("vector", lambda e: e.tensor_scalar(
        out=bkr16[:], in0=bkr[:], scalar1=1.0 / 64.0, scalar2=None, op0=ALU.mult),
        waits=[w_bk], inc=ctx.ev)
    bk_w = ctx.ev.at()

    wbuf = ctx.sb("wbuf", [128, DC * D], BF16)
    stage = ctx.sb("stage", [128, 4 * 512], F32)
    stbf = ctx.sb("stbf", [128, 4 * 512], BF16)

    cc_in = ctx.ndt("cc_in", [D, D], F32)
    cc_out = ctx.ndt("cc_out", [D, D], F32)
    v_d = ctx.ndt("v_d", [D, TL], BF16)

    guards = []

    def salloc(name, shape, dtype):
        g = ctx.nsb(name, shape, dtype)
        t = g.__enter__()
        guards.append(g)
        return t

    h = ctx.sb("h_sb", [128, DC * TL], BF16)
    h_w = emit_ln(ctx, xln, h, TL, cw, "ln2", gain_col=nwc, bias_col=nbc,
                  x_wait=ln_w)

    q_tm = salloc("q_tm", [128, (TL // 128) * D], BF16)
    k_tm = salloc("k_tm", [128, (TL // 128) * D], BF16)

    def evac_q(bank, tc, o0, ow, waits):
        p.add("vector", lambda e, bk=bank, tc=tc, o0=o0, ow=ow: e.tensor_add(
            out=q_tm[:, tc * D + o0: tc * D + o0 + ow], in0=bk[:, 0:ow],
            in1=bqr[:, o0:o0 + ow]), waits=waits + [w_bq], inc=ctx.ev)
        return ctx.ev

    def evac_k(bank, tc, o0, ow, waits):
        p.add("vector", lambda e, bk=bank, tc=tc, o0=o0, ow=ow: e.scalar_tensor_tensor(
            out=k_tm[:, tc * D + o0: tc * D + o0 + ow], in0=bk[:, 0:ow],
            scalar=1.0 / 64.0, in1=bkr16[:, o0:o0 + ow],
            op0=ALU.mult, op1=ALU.add), waits=waits + [bk_w], inc=ctx.ev)
        return ctx.ev

    w_q = emit_weight_load(ctx, wbuf[:], ap3(ctx.dram["b1_qkv"], 0,
                           [[3 * D, 128], [128 * 3 * D, DC], [1, D]]))
    emit_proj_tm(ctx, wbuf, w_q, h, h_w, TL, D, evac_q)
    w_k = emit_weight_load(ctx, wbuf[:], ap3(ctx.dram["b1_qkv"], D,
                           [[3 * D, 128], [128 * 3 * D, DC], [1, D]]),
                           waits=[ctx.mm.at()])
    emit_proj_tm(ctx, wbuf, w_k, h, h_w, TL, D, evac_k)
    w_v = emit_weight_load(ctx, wbuf[:], ap3(ctx.dram["b1_qkv"], 2 * D,
                           [[3 * D, 128], [128 * 3 * D, DC], [1, D]]),
                           waits=[ctx.mm.at()])
    vslot = [0]
    vfree = [None] * 4

    def evac_v(bank, oc, t0, tw, waits):
        sl = vslot[0] % 4
        vslot[0] += 1
        sw = waits + [w_bv] + ([vfree[sl]] if vfree[sl] else [])
        p.add("vector", lambda e, bk=bank, sl=sl, oc=oc, tw=tw: e.tensor_scalar(
            out=stbf[:, sl * 512: sl * 512 + tw], in0=bk[:, 0:tw],
            scalar1=bvc[:, oc:oc + 1], scalar2=None, op0=ALU.add),
            waits=sw, inc=ctx.ev)
        evw = ctx.ev.at()
        p.add("sync", lambda e, sl=sl, oc=oc, t0=t0, tw=tw: e.dma_start(
            out=v_d[oc * 128:(oc + 1) * 128, t0:t0 + tw],
            in_=stbf[:, sl * 512: sl * 512 + tw]), waits=[evw], inc=ctx.dx, k=16)
        vfree[sl] = ctx.dx.at()
        return ctx.ev

    emit_proj_fm(ctx, wbuf, w_v, h, h_w, TL, D, evac_v)
    v_ship = ctx.dx.at()


    qk_w = ctx.ev.at()
    sslot = [0]
    sfree = [None] * 4
    for kdc in range(DC):
        for qt in range(2):
            bank, bw, bi = ctx.pool.acquire()
            pairs = [(k_tm[:, c * D + kdc * 128: c * D + (kdc + 1) * 128],
                      q_tm[:, c * D + qt * 512: c * D + qt * 512 + 512])
                     for c in range(TL // 128)]
            mmw = emit_matmul_chain(ctx, bank[:, 0:512], pairs, waits=[qk_w] + bw)
            sl = sslot[0] % 4
            sslot[0] += 1
            sw = [mmw] + ([sfree[sl]] if sfree[sl] else [])
            p.add("vector", lambda e, b=bank, sl=sl: e.tensor_copy(
                out=stage[:, sl * 512:(sl + 1) * 512], in_=b[:, 0:512]),
                waits=sw, inc=ctx.ev)
            ctx.pool.release(bi, ctx.ev)
            evw = ctx.ev.at()
            p.add("sync", lambda e, sl=sl, kdc=kdc, qt=qt: e.dma_start(
                out=cc_in[kdc * 128:(kdc + 1) * 128, qt * 512:(qt + 1) * 512],
                in_=stage[:, sl * 512:(sl + 1) * 512]), waits=[evw], inc=ctx.dx, k=16)
            sfree[sl] = ctx.dx.at()
    p.add("gpsimd", lambda e: e.collective_compute(
        "AllReduce", ALU.add, replica_groups=PAIRS,
        ins=[cc_in.opt()], outs=[cc_out.opt()]),
        waits=[ctx.dx.at()], inc=ctx.cc)
    s_cc = ctx.cc.at()

    guards[1].__exit__(None, None, None)
    guards[0].__exit__(None, None, None)

    PT = ctx.stack.enter_context(ctx.nsb("PT", [128, DC * 1024], BF16))
    a_fm = ctx.stack.enter_context(ctx.nsb("a_fm", [128, DC * TL], BF16))
    Dc = ctx.stack.enter_context(ctx.nsb("Dc", [128, 8], F32))
    vt = ctx.stack.enter_context(ctx.nsb("vt", [128, DC * 512], BF16))
    slots = ctx.t["lnmu"]
    for kdc in range(DC):
        sl = (kdc % 2) * 1024
        p.add("sync", lambda e, kdc=kdc, sl=sl: e.dma_start(
            out=slots[:, sl:sl + 1024], in_=cc_out[kdc * 128:(kdc + 1) * 128, :]),
            waits=[s_cc] + ([ctx.es.at()] if kdc >= 2 else []), inc=ctx.dx, k=16)
        p.add("scalar", lambda e, kdc=kdc, sl=sl: e.activation(
            out=PT[:, kdc * 1024:(kdc + 1) * 1024], in_=slots[:, sl:sl + 1024],
            func=AF.Exp), waits=[ctx.dx.at()], inc=ctx.es)
    PT_w = ctx.es.at()
    for qdc in range(DC):
        bank, bw, bi = ctx.pool.acquire()
        pairs = [(PT[:, kdc * 1024 + qdc * 128: kdc * 1024 + (qdc + 1) * 128],
                  ones[:, 0:1]) for kdc in range(DC)]
        mmw = emit_matmul_chain(ctx, bank[:, 0:1], pairs, waits=[PT_w, cw] + bw)
        p.add("vector", lambda e, b=bank, qdc=qdc: e.tensor_copy(
            out=Dc[:, qdc:qdc + 1], in_=b[:, 0:1]), waits=[mmw], inc=ctx.ev)
        ctx.pool.release(bi, ctx.ev)
    p.add("vector", lambda e: e.reciprocal(out=Dc[:], in_=Dc[:]),
          waits=[ctx.ev.at()], inc=ctx.ev)
    Dinv_w = ctx.ev.at()
    vt_user = [None]
    for ct in range(4):
        vw = [v_ship] + ([vt_user[0]] if vt_user[0] else [])
        p.add("sync", lambda e, ct=ct: e.dma_start(
            out=vt[:], in_=ap3(v_d, ct * 512, [[TL, 128], [128 * TL, DC], [1, 512]])),
            waits=vw, inc=ctx.dx, k=16)
        vt_w = ctx.dx.at()
        for qic in range(DC):
            bank, bw, bi = ctx.pool.acquire()
            pairs = [(PT[:, kdc * 1024 + qic * 128: kdc * 1024 + (qic + 1) * 128],
                      vt[:, kdc * 512:(kdc + 1) * 512]) for kdc in range(DC)]
            mmw = emit_matmul_chain(ctx, bank[:, 0:512], pairs,
                                    waits=[PT_w, vt_w] + bw)
            p.add("vector", lambda e, b=bank, qic=qic, ct=ct: e.tensor_scalar(
                out=a_fm[:, qic * TL + ct * 512: qic * TL + (ct + 1) * 512],
                in0=b[:, 0:512], scalar1=Dc[:, qic:qic + 1], scalar2=None,
                op0=ALU.mult), waits=[mmw, Dinv_w], inc=ctx.ev)
            ctx.pool.release(bi, ctx.ev)
        vt_user[0] = ctx.mm.at()
    a_w = ctx.ev.at()
    w_p = emit_weight_load(ctx, wbuf[:], chunked(ctx.dram["b1_proj"]),
                           waits=[ctx.mm.at()])
    oslot = [0]
    ofree = [None] * 4
    for oc in range(DC):
        for t0 in range(0, TL, 512):
            bank, bw, bi = ctx.pool.acquire()
            pairs = [(wbuf[:, k * D + oc * 128: k * D + (oc + 1) * 128],
                      a_fm[:, k * TL + t0: k * TL + t0 + 512]) for k in range(DC)]
            mmw = emit_matmul_chain(ctx, bank[:, 0:512], pairs,
                                    waits=[w_p, a_w] + bw)
            sl = oslot[0] % 4
            oslot[0] += 1
            sw = [mmw, w_pb, ln_w] + ([ofree[sl]] if ofree[sl] else [])
            p.add("vector", lambda e, b=bank, sl=sl, oc=oc, t0=t0: e.scalar_tensor_tensor(
                out=stage[:, sl * 512:(sl + 1) * 512], in0=b[:, 0:512],
                scalar=pbc[:, oc:oc + 1], in1=xln[:, oc * TL + t0: oc * TL + t0 + 512],
                op0=ALU.add, op1=ALU.add), waits=sw, inc=ctx.ev)
            ctx.pool.release(bi, ctx.ev)
            evw = ctx.ev.at()
            p.add("gpsimd", lambda e, sl=sl, oc=oc, t0=t0: e.dma_start(
                out=XDST[:, oc * TL + t0: oc * TL + t0 + 512],
                in_=stage[:, sl * 512:(sl + 1) * 512], accum_op=ALU.add),
                waits=[evw, xo_init], inc=ctx.dw, k=16)
            ofree[sl] = ctx.dw.at()
    p.add("sync", None, waits=[ctx.dw.at()])


def _emit_b1(ctx):
    emit_l1_attn_b(ctx)


SUBLAYER_WEIGHTS = {
    "a0": ["a0_wq", "a0_bq", "a0_wk", "a0_bk", "a0_wv", "a0_bv", "a0_gamma"],
    "f0": ["f0_gw", "f0_wg", "f0_wu", "f0_wd", "f0_sg", "f0_su", "f0_sd"],
    "b0": ["b0_wq", "b0_wk", "b0_wv", "b0_wo", "b0_bq", "b0_bk", "b0_bv", "b0_bo"],
    "w1": ["w1_decay", "w1_first", "w1_mk", "w1_mv", "w1_mr", "w1_bmask",
           "w1_wk", "w1_wv", "w1_wr", "w1_wo"],
    "f1": ["f1_gw", "f1_eb", "f1_w1", "f1_w3", "f1_w2", "f1_b1", "f1_b3", "f1_b2",
           "f1_sw1", "f1_sw3", "f1_sw2", "f1_sb1", "f1_sb3", "f1_sb2"],
    "b1": ["b1_nw", "b1_nb", "b1_qkv", "b1_bq", "b1_bk", "b1_bv", "b1_proj", "b1_pb"],
}

SUBLAYER_EMIT = {}


def _emit_a0(ctx):
    x, xw = emit_load_x(ctx)
    done = emit_l0_attn_a(ctx, x, xw)
    emit_store_x(ctx, x, [done])


SUBLAYER_EMIT["a0"] = _emit_a0
SUBLAYER_EMIT["f0"] = _emit_f0
SUBLAYER_EMIT["b0"] = _emit_b0
SUBLAYER_EMIT["w1"] = _emit_w1
SUBLAYER_EMIT["f1"] = _emit_f1
SUBLAYER_EMIT["b1"] = _emit_b1


def build_sublayer(name):
    nc = bass.Bass("TRN2", target_bir_lowering=False)
    stack = ExitStack()
    with stack:
        ctx = setup_common(nc, stack, SUBLAYER_WEIGHTS[name])
        ctx.mkpool(6)
        SUBLAYER_EMIT[name](ctx)
        with nc.Block() as block:
            ctx.p.emit(block)
    return nc


def run_sublayer(name, X, w):
    nc = build_sublayer(name)
    in_maps = []
    for c in range(NCORES):
        m = {"x": X[c]}
        for wn in SUBLAYER_WEIGHTS[name]:
            val = w[wn]
            m[wn] = val[c] if isinstance(val, list) else val
        in_maps.append(m)
    res = run_bass_kernel_spmd(nc, in_maps, core_ids=list(range(NCORES)))
    if F0_DEBUG:
        global DBG_OUT
        DBG_OUT = [np.asarray(res.results[c].get("dbg")) for c in range(NCORES)]
        global DBG_Y
        DBG_Y = [np.asarray(res.results[c].get("dbgy")) for c in range(NCORES)]
        global DBG_G
        DBG_G = [np.asarray(res.results[c].get("dbgg")) for c in range(NCORES)]
        global DBG_L
        DBG_L = [np.asarray(res.results[c].get("dbgl")) for c in range(NCORES)]
    return [np.asarray(res.results[c]["xo"]) for c in range(NCORES)]


SUBLAYER_ORDER = ["a0", "f0", "b0", "f0", "w1", "f1", "b1", "f1"]
FUSED = True
FUSED_K = len(SUBLAYER_ORDER)
FUSED_TAP = None
ALL_WEIGHTS = sorted({w for ws in SUBLAYER_WEIGHTS.values() for w in ws})


def emit_sub_barrier(ctx, i):
    prog = ctx.p
    s = prog.sem(f"bar{i}")
    for e in ENGS:
        prog.add(e, lambda en: en.nop(), inc=s)
    tgt = s.n
    for e in ENGS:
        prog.add(e, None, waits=[(s, tgt)])


def build_fused():
    nc = bass.Bass("TRN2", target_bir_lowering=False)
    stack = ExitStack()
    with stack:
        prog = Prog(nc, stack)
        ctx = Ctx(nc, prog, stack)
        n = FUSED_K
        ctx.dram["x0"] = nc.dram_tensor("x", [128, DC * TL], F32,
                                        kind="ExternalInput").ap()
        ctx.dram[f"x{n}"] = nc.dram_tensor("xo", [128, DC * TL], F32,
                                           kind="ExternalOutput").ap()
        for i in range(1, n):
            ctx.dram[f"x{i}"] = nc.dram_tensor(f"xb{i}", [128, DC * TL], F32).ap()
        fw = sorted({w for nm in SUBLAYER_ORDER[:n] for w in SUBLAYER_WEIGHTS[nm]})
        for name in fw:
            ctx.dram[name] = nc.dram_tensor(
                name, list(WEIGHT_SHAPES[name]), F32, kind="ExternalInput").ap()
        ctx.mkpool(6)
        if FUSED_TAP is not None:
            ctx.dram["tap"] = nc.dram_tensor("tap", [128, DC * TL], F32,
                                             kind="ExternalOutput").ap()
        outer = stack
        for i, nm in enumerate(SUBLAYER_ORDER[:n]):
            ctx.tag = f"s{i}_"
            ctx.xsrc = f"x{i}"
            ctx.xdst = f"x{i + 1}"
            ctx.t = {}
            sub = ExitStack()
            ctx.stack = sub
            with sub:
                SUBLAYER_EMIT[nm](ctx)
            ctx.stack = outer
            emit_sub_barrier(ctx, i)
            if FUSED_TAP is not None and i == FUSED_TAP:
                prog.add("gpsimd", lambda e, i=i: e.dma_start(
                    out=ctx.dram["tap"], in_=ctx.dram[f"x{i + 1}"]),
                    inc=ctx.dw, k=16)
                emit_sub_barrier(ctx, 100 + i)
        with nc.Block() as block:
            prog.emit(block)
    return nc


def kernel_fused(X, w):
    nc = build_fused()
    fw = sorted({wn for nm in SUBLAYER_ORDER[:FUSED_K] for wn in SUBLAYER_WEIGHTS[nm]})
    in_maps = []
    for c in range(NCORES):
        m = {"x": X[c]}
        for wn in fw:
            val = w[wn]
            m[wn] = val[c] if isinstance(val, list) else val
        in_maps.append(m)
    res = run_bass_kernel_spmd(nc, in_maps, core_ids=list(range(NCORES)))
    return [np.asarray(res.results[c]["xo"]) for c in range(NCORES)]


def kernel(**inputs):
    w = prep_weights(inputs)
    x0 = np.asarray(inputs["L0_attn_a_x"])
    X = []
    for c in range(NCORES):
        b, s = c // 2, c % 2
        X.append(_fm(x0[b, s * TL:(s + 1) * TL, :]))
    if FUSED:
        X = kernel_fused(X, w)
        for name in SUBLAYER_ORDER[FUSED_K:]:
            X = run_sublayer(name, X, w)
    else:
        for name in SUBLAYER_ORDER:
            X = run_sublayer(name, X, w)
    out = np.zeros((B, T, D), np.float32)
    for c in range(NCORES):
        b, s = c // 2, c % 2
        out[b, s * TL:(s + 1) * TL, :] = _unfm(X[c])
    return out
